# version 06
# baseline (speedup 1.0000x reference)
import numpy as np
from contextlib import ExitStack

import concourse.bass as bass
import concourse.mybir as mybir
from concourse.bass_utils import run_bass_kernel_spmd

F32 = mybir.dt.float32
BF16 = mybir.dt.bfloat16
I32 = mybir.dt.int32
AF = mybir.ActivationFunctionType
ALU = mybir.AluOpType
AX = mybir.AxisListType.X

T_OWN = 2048
NT = T_OWN // 128
D = 1024
DC = D // 128
HALO = 63


class Prog:
    def __init__(self):
        self.nc = bass.Bass("TRN2", target_bir_lowering=False)
        self.es = ExitStack()
        self._sems = {}
        self.block = None

    def sem(self, name):
        if name not in self._sems:
            h = self.es.enter_context(self.nc.semaphore(name))
            self._sems[name] = [h, 0]
        return self._sems[name]

    def n(self, name):
        return self.sem(name)[1]

    def dinc(self, ins, name):
        s = self.sem(name)
        ins.then_inc(s[0], 16)
        s[1] += 16
        return s[1]

    def cinc(self, ins, name, k=1):
        s = self.sem(name)
        ins.then_inc(s[0], k)
        s[1] += k
        return s[1]

    def wait(self, eng, name, target=None):
        s = self.sem(name)
        t = s[1] if target is None else target
        if t > 0:
            eng.wait_ge(s[0], t)

    def ssync(self, eng, ins, name):
        s = self.sem(name)
        ins.then_inc(s[0], 1)
        s[1] += 1
        eng.wait_ge(s[0], s[1])

    def sbuf(self, name, shape, dtype):
        st = self._scopes[-1] if getattr(self, "_scopes", None) else self.es
        import os
        if os.environ.get("SBUF_DEBUG"):
            per = int(np.prod(shape[1:])) * mybir.dt.size(dtype)
            print(f"SBUF {name}: {per}B/prt, remaining {self.nc.sbuf_bytes_remaining}")
        return st.enter_context(self.nc.sbuf_tensor(name, shape, dtype))

    def push_scope(self):
        if not hasattr(self, "_scopes"):
            self._scopes = []
        self._scopes.append(ExitStack())

    def pop_scope(self):
        self._scopes.pop().close()

    def psum(self, name, shape, dtype=F32):
        return self.es.enter_context(self.nc.psum_tensor(name, shape, dtype))


def ts(i, k=128):
    return slice(i * k, (i + 1) * k)




def emit_consts(p):
    nc = p.nc
    p.ident = p.sbuf("ident", [128, 128], BF16)
    p.ones_r = p.sbuf("ones_r", [1, 512], BF16)
    p.ones_f = p.sbuf("ones_f", [1, 512], F32)
    p.eps = p.sbuf("eps", [128, 8], F32)
    p.iotai = p.sbuf("iotai", [128, 128], I32)

    @p.block.gpsimd
    def _(g):
        g.iota(p.iotai[:], pattern=[[1, 128]], base=0, channel_multiplier=-1,
               allow_small_or_imprecise_dtypes=True)
        g.memset(p.eps[:, 0:1], 1e-8)
        g.memset(p.eps[:, 1:2], 1e-5)
        g.memset(p.eps[:, 2:3], 1e-6)
        g.memset(p.eps[:, 3:4], 1e-9)
        g.memset(p.eps[:, 4:5], 1e-12)
        p.cinc(g.memset(p.eps[:, 5:6], 0.0), "cst")

    @p.block.vector
    def _(v):
        p.wait(v, "cst")
        v.tensor_single_scalar(p.ident[:], p.iotai[:], 0, ALU.is_equal)
        v.memset(p.ones_f[:], 1.0)
        p.cinc(v.memset(p.ones_r[:], 1.0), "cst")


def emit_load_x(p, x_ap, xhalo_ap=None):
    p.xsb = p.sbuf("xsb", [128, NT * D], F32)
    if xhalo_ap is not None:
        p.xhalo = p.sbuf("xhalo", [63, D], F32)

    @p.block.sync
    def _(s):
        for i in range(NT):
            p.dinc(s.dma_start(out=p.xsb[:, ts(i, D)], in_=x_ap[ts(i), :]), "xld")
        if xhalo_ap is not None:
            p.dinc(s.dma_start(out=p.xhalo[:], in_=xhalo_ap), "xld")


def emit_store_x(p, out_ap, wait_sems):
    @p.block.sync
    def _(s):
        for name, tgt in wait_sems:
            p.wait(s, name, tgt)
        for i in range(NT):
            p.dinc(s.dma_start(out=out_ap[ts(i), :], in_=p.xsb[:, ts(i, D)]), "xst")
        p.wait(s, "xst")


def alloc_ln_scratch(p):
    p.scol = p.sbuf("scol", [128, 1], F32)
    p.qcol = p.sbuf("qcol", [128, 1], F32)
    p.m2col = p.sbuf("m2col", [128, 1], F32)
    p.mcol2 = p.sbuf("mcol2", [128, 2], F32)
    p.vcol = p.sbuf("vcol", [128, 2], F32)
    p.rscol = p.sbuf("rscol", [128, 2], F32)
    p.rrcol = p.sbuf("rrcol", [128, 1], F32)
    p.sqf = p.sbuf("sqf", [128, 2 * D], BF16)
    p.ln2 = p.sbuf("ln2", [128, 2 * D], BF16)
    p.tpsum8 = p.psum("tps8", [128, 8 * 128], F32)


def emit_ln_transpose(p, dst_T, n_tiles, src_tile_fn, kind, tag, dst_col0=0,
                      gamma_bc=None, pre_wait=None, evac_wait_extra=None,
                      dst_col_fn=None, cb_every=None, cb_tensor=None,
                      cb_vector=None, cb_scalar=None, base=0):
    nc = p.nc
    eps = p.eps[:, 1:2] if kind == "ln" else p.eps[:, 2:3]
    if dst_col_fn is None:
        dst_col_fn = lambda i: dst_col0 + i * 128

    @p.block.vector
    def _(v):
        for it in range(n_tiles + 2):
            k = it - 2
            if k >= 0:
                p.wait(v, tag + "q", base + k + 1)
                p.ssync(v, v.reciprocal(p.rrcol[:], p.rscol[:, k % 2:k % 2 + 1]),
                        "vslf")
                if k >= 2:
                    p.wait(v, tag + "t", DC * (base + k - 1))
                dst = p.ln2[:, ts(k % 2, D)]
                src = src_tile_fn(k)
                if kind == "ln":
                    ins = v.tensor_scalar(dst, src, p.mcol2[:, k % 2:k % 2 + 1],
                                          p.rrcol[:], op0=ALU.subtract,
                                          op1=ALU.mult)
                elif gamma_bc is None:
                    ins = v.tensor_scalar(dst, src, p.rrcol[:], None,
                                          op0=ALU.mult)
                else:
                    ins = v.scalar_tensor_tensor(dst, src, p.rrcol[:], gamma_bc,
                                                 op0=ALU.mult, op1=ALU.mult)
                p.cinc(ins, tag + "n")
                if cb_vector is not None and (k + 1) % cb_every == 0:
                    cb_vector(v, (k + 1) // cb_every - 1)
            if it < n_tiles:
                if pre_wait is not None:
                    pre_wait(v, it)
                src = src_tile_fn(it)
                if kind == "ln":
                    p.ssync(v, v.reduce_sum(p.scol[:], src, AX), "vslf")
                    v.tensor_scalar_mul(p.mcol2[:, it % 2:it % 2 + 1],
                                        p.scol[:], 1.0 / D)
            j = it - 1
            if 0 <= j < n_tiles:
                p.wait(v, tag + "s", base + j + 1)
                if kind == "ln":
                    v.tensor_mul(p.m2col[:], p.mcol2[:, j % 2:j % 2 + 1],
                                 p.mcol2[:, j % 2:j % 2 + 1])
                p.ssync(v, v.reduce_sum(p.qcol[:], p.sqf[:, ts(j % 2, D)], AX),
                        "vslf")
                if kind == "ln":
                    ins = v.scalar_tensor_tensor(p.vcol[:, j % 2:j % 2 + 1],
                                                 p.qcol[:], 1.0 / D,
                                                 p.m2col[:], op0=ALU.mult,
                                                 op1=ALU.subtract)
                else:
                    ins = v.tensor_scalar_mul(p.vcol[:, j % 2:j % 2 + 1],
                                              p.qcol[:], 1.0 / D)
                p.cinc(ins, tag + "v")

    def emit_evac(s, i):
        p.wait(s, tag + "t", DC * (base + i + 1))
        if evac_wait_extra is not None:
            evac_wait_extra(s, i)
        c0 = dst_col_fn(i)
        for dc in range(DC):
            p.cinc(s.copy(dst_T[dc][:, c0:c0 + 128],
                          p.tpsum8[:, ts(dc)]), tag + "e")

    @p.block.scalar
    def _(s):
        for it in range(n_tiles + 2):
            k = it - 2
            if 0 <= k < n_tiles:
                emit_evac(s, k)
                if cb_scalar is not None and (k + 1) % cb_every == 0:
                    cb_scalar(s, (k + 1) // cb_every - 1)
            if it < n_tiles:
                if pre_wait is not None:
                    pre_wait(s, it)
                if it >= 2:
                    p.wait(s, tag + "v", base + it - 1)
                p.cinc(s.activation(p.sqf[:, ts(it % 2, D)], src_tile_fn(it),
                                    AF.Square), tag + "s")
            j = it - 1
            if 0 <= j < n_tiles:
                p.wait(s, tag + "v", base + j + 1)
                p.cinc(s.activation(p.rscol[:, j % 2:j % 2 + 1],
                                    p.vcol[:, j % 2:j % 2 + 1], AF.Sqrt,
                                    bias=eps), tag + "q")

    @p.block.tensor
    def _(t):
        for i in range(n_tiles):
            p.wait(t, tag + "n", base + i + 1)
            if i >= 1:
                p.wait(t, tag + "e", DC * (base + i))
            for dc in range(DC):
                p.cinc(t.matmul(p.tpsum8[:, ts(dc)],
                                p.ln2[:, (i % 2) * D + dc * 128:
                                      (i % 2) * D + dc * 128 + 128],
                                p.ident[:], start=True, stop=True), tag + "t")
            if cb_tensor is not None and (i + 1) % cb_every == 0:
                cb_tensor(t, (i + 1) // cb_every - 1)




def emit_l0f_outer_ln(p):
    oxt = p.oxt

    @p.block.vector
    def _(v):
        for nm, tgt in p.x_ready:
            p.wait(v, nm, tgt)
        p.wait(v, "cst")
        for i in range(NT):
            src = p.xsb[:, ts(i, D)]
            tmp = oxt[:, 0:D]
            v.tensor_mul(tmp, src, src)
            p.ssync(v, v.reduce_sum(p.qcol[:], tmp, AX), "vslf")
            p.ssync(v, v.reduce_sum(p.scol[:], src, AX), "vslf")
            p.ssync(v, v.tensor_scalar_mul(p.mcol2[:, 0:1], p.scol[:], 1.0 / D),
                    "vslf")
            p.ssync(v, v.tensor_mul(p.m2col[:], p.mcol2[:, 0:1],
                                    p.mcol2[:, 0:1]), "vslf")
            p.cinc(v.scalar_tensor_tensor(p.vcol[:, 0:1], p.qcol[:], 1.0 / D,
                                          p.m2col[:], op0=ALU.mult,
                                          op1=ALU.subtract), "oLv")
            p.wait(v, "oLq", i + 1)
            p.ssync(v, v.reciprocal(p.rrcol[:], p.rscol[:, 0:1]), "vslf")
            v.tensor_scalar(tmp, src, p.mcol2[:, 0:1], p.rrcol[:],
                            op0=ALU.subtract, op1=ALU.mult)
            p.cinc(v.tensor_copy(src, tmp), "oLc")

    @p.block.scalar
    def _(s):
        p.wait(s, "cst")
        for i in range(NT):
            p.wait(s, "oLv", i + 1)
            p.cinc(s.activation(p.rscol[:, 0:1], p.vcol[:, 0:1], AF.Sqrt,
                                bias=p.eps[:, 1:2]), "oLq")
    p.x_ready = [("oLc", NT)]


def emit_l0f_outer_add(p, x_ap):
    oxt = p.oxt

    @p.block.gpsimd
    def _(g):
        for nm, tgt in p.x_ready:
            p.wait(g, nm, tgt)
        for i in range(NT):
            if i >= 2:
                g.wait_ge(p.sem("oRa")[0], i - 1)
            p.dinc(g.dma_start(out=oxt[:, ts(i % 2, D)], in_=x_ap[ts(i), :]),
                   "oRl")

    @p.block.vector
    def _(v):
        for i in range(NT):
            p.wait(v, "oRl", 16 * (i + 1))
            dst = p.xsb[:, ts(i, D)]
            p.cinc(v.tensor_tensor(dst, oxt[:, ts(i % 2, D)], dst, ALU.add),
                   "oRa")
    p.x_ready = [("oRa", NT)]


def emit_l0_ffn(p, nw_ap, rw_ap, w1_ap, w3_ap, w2_ap, x_ap=None):
    nc = p.nc
    E, FC, TCN = 8, 16, 2
    znT = [p.sbuf(f"l0znT{dc}", [128, T_OWN], BF16) for dc in range(DC)]
    normbc = p.sbuf("l0nbc", [128, D], BF16)
    nwrow = p.sbuf("l0nw", [1, D], BF16)
    rw_sb = p.sbuf("l0rw", [128, DC * 8], BF16)
    sel8 = p.sbuf("l0sel8", [8, 8 * 128], BF16)
    ioe = p.sbuf("l0ioe", [8, 8 * 128], I32)
    cw_all = p.sbuf("l0cw", [128, 128], BF16)
    cwT = p.sbuf("l0cwT", [8, T_OWN], BF16)
    esc = p.sbuf("l0esc", [128, 128], F32)
    esc2 = p.sbuf("l0esc2", [128, 128], F32)
    ssb = p.sbuf("l0ssb", [128, 128], F32)
    msk1 = p.sbuf("l0m1", [128, 128], F32)
    mskd = p.sbuf("l0md", [128, 128], F32)
    msel = p.sbuf("l0msel", [128, 128], F32)
    gcol = p.sbuf("l0gA", [128, 16], F32)
    gcol2 = p.sbuf("l0gB", [128, 16], F32)
    gcol3 = p.sbuf("l0gC", [128, 16], F32)
    scol2 = p.sbuf("l0sc2", [128, 16], F32)
    cwbc = p.sbuf("l0cwbc", [128, 1024], BF16)
    w13 = [p.sbuf(f"l0w13_{j}", [128, 2 * D], BF16) for j in range(2)]
    w2h = p.sbuf("l0w2h", [128, FC * 512], BF16)
    h_all = p.sbuf("l0h", [128, FC * 1024], BF16)
    s_sb = p.sbuf("l0s", [128, 512], BF16)
    t1_sb = p.sbuf("l0t1", [128, 512], BF16)
    p.dbg_znT0 = znT[0][:]
    p.dbg_cwsb = cw_all[:]
    p.dbg_ssb = ssb[:]
    p.dbg_msk1 = msk1[:]
    p.dbg_msel = msel[:]
    p.dbg_esc = esc[:]
    p.dbg_esc2 = esc2[:]
    p.dbg_cwT = cwT[:]
    p.dbg_hall = h_all[:]
    p.dbg_cwbc = cwbc[:]
    p.dbg_normbc = normbc[:]

    @p.block.gpsimd
    def _(g):
        g.iota(ioe[:], pattern=[[1, 8], [0, 128]], base=0, channel_multiplier=-1,
               allow_small_or_imprecise_dtypes=True)
        p.dinc(g.dma_start(out=nwrow[:], in_=nw_ap.unsqueeze(0)), "l0w0")
        for dc in range(DC):
            p.dinc(g.dma_start(out=rw_sb[:, dc * 8:(dc + 1) * 8],
                               in_=rw_ap[ts(dc), :]), "l0w0")
        p.cinc(g.memset(cw_all[:], 0.0), "l0c1")

    @p.block.vector
    def _(v):
        p.wait(v, "l0c1")
        p.cinc(v.tensor_single_scalar(sel8[:], ioe[:], 0, ALU.is_equal), "l0c2")

    @p.block.tensor
    def _(t):
        p.wait(t, "l0c2")
        p.wait(t, "cst")
        p.wait(t, "l0w0")
        for half in range(2):
            ins = t.matmul(p.aux1[:, ts(half, 512)], p.ones_r[:, 0:128],
                           nwrow[:, ts(half, 512)], start=True, stop=True)
        p.cinc(ins, "l0c3")

    @p.block.scalar
    def _(s):
        p.wait(s, "l0c3")
        for half in range(2):
            p.cinc(s.copy(normbc[:, ts(half, 512)], p.aux1[:, ts(half, 512)]),
                   "l0c4")

    if x_ap is not None:
        p.oxt = p.sbuf("l0oxt", [128, 2 * D], F32)
        emit_l0f_outer_ln(p)

    import os
    nloops = int(os.environ.get("L0F_LOOPS", "2"))
    for l in range(nloops):
        _emit_moe_loop(p, l, w1_ap, w3_ap, w2_ap, znT, normbc, rw_sb, sel8,
                       cw_all, cwT, esc, esc2, ssb, msk1, mskd, msel,
                       gcol, gcol2, gcol3, scol2, cwbc, w13, w2h, h_all,
                       s_sb, t1_sb)
    p.x_ready = [(f"m{nloops-1}ad", p.n(f"m{nloops-1}ad"))]
    if x_ap is not None:
        emit_l0f_outer_add(p, x_ap)


def _emit_moe_loop(p, l, w1_ap, w3_ap, w2_ap, znT, normbc, rw_sb, sel8,
                   cw_all, cwT, esc, esc2, ssb, msk1, mskd, msel,
                   gcol, gcol2, gcol3, scol2, cwbc, w13, w2h, h_all,
                   s_sb, t1_sb):
    nc = p.nc
    E, FC, TCN = 8, 16, 2
    tg = f"m{l}"

    def pre(eng, i):
        if i == 0:
            p.wait(eng, "l0c4")
            if l == 0:
                for nm, tgt in p.x_ready:
                    p.wait(eng, nm, tgt)
            if l == 1:
                p.wait(eng, "m0ad")

    emit_ln_transpose(p, znT, NT, lambda i: p.xsb[:, ts(i, D)], "rms", tg,
                      gamma_bc=normbc[:], pre_wait=pre)

    @p.block.tensor
    def _(t):
        for i in range(NT):
            p.wait(t, tg + "e", DC * (i + 1))
            for dc in range(DC):
                ins = t.matmul(p.aux1[:, i * 8:(i + 1) * 8], znT[dc][:, ts(i)],
                               rw_sb[:, dc * 8:(dc + 1) * 8],
                               start=(dc == 0), stop=(dc == DC - 1))
            p.cinc(ins, tg + "lg")
        p.wait(t, tg + "cw", 1)
        for i in range(NT):
            if i >= 1:
                p.wait(t, tg + "ce", i)
            p.cinc(t.matmul(p.aux1[0:8, 512:640], cw_all[:, i * 8:(i + 1) * 8],
                            p.ident[:], start=True, stop=True), tg + "ct")

    @p.block.vector
    def _(v):
        p.wait(v, tg + "lg", NT)
        lg3 = p.aux1[:, 0:128].rearrange("q (g e) -> q g e", e=8)
        def b16(col):
            return col.unsqueeze(2).to_broadcast([128, NT, 8])
        S = p.ssync
        g3 = lambda t2d: t2d.rearrange("q (g e) -> q g e", e=8)
        S(v, v.reduce_max(gcol[:], lg3, AX), "vslf")
        S(v, v.tensor_tensor(g3(esc[:]), lg3, b16(gcol[:]), ALU.subtract), "vslf")
        p.cinc(v.tensor_copy(scol2[:], gcol[:]), tg + "x1")
        p.wait(v, tg + "x2", 1)
        S(v, v.reduce_sum(gcol[:], g3(esc2[:]), AX), "vslf")
        S(v, v.reciprocal(gcol2[:], gcol[:]), "vslf")
        S(v, v.tensor_tensor(g3(ssb[:]), g3(esc2[:]), b16(gcol2[:]), ALU.mult),
          "vslf")
        S(v, v.reduce_max(gcol[:], g3(ssb[:]), AX), "vslf")
        S(v, v.tensor_tensor(g3(msk1[:]), g3(ssb[:]), b16(gcol[:]), ALU.is_ge),
          "vslf")
        S(v, v.scalar_tensor_tensor(g3(mskd[:]), g3(msk1[:]), -1e30, g3(ssb[:]),
                                    op0=ALU.mult, op1=ALU.add), "vslf")
        S(v, v.reduce_max(gcol2[:], g3(mskd[:]), AX), "vslf")
        S(v, v.tensor_tensor(g3(msel[:]), g3(mskd[:]), b16(gcol2[:]), ALU.is_ge),
          "vslf")
        S(v, v.tensor_add(g3(mskd[:]), g3(msk1[:]), g3(msel[:])), "vslf")
        S(v, v.tensor_add(gcol3[:], gcol[:], gcol2[:]), "vslf")
        S(v, v.tensor_scalar(gcol[:], gcol3[:], 1e-9, None, op0=ALU.add), "vslf")
        S(v, v.reciprocal(gcol2[:], gcol[:]), "vslf")
        S(v, v.tensor_mul(g3(msk1[:]), g3(ssb[:]), g3(mskd[:])), "vslf")
        p.cinc(v.tensor_tensor(g3(cw_all[:]), g3(msk1[:]), b16(gcol2[:]),
                               ALU.mult), tg + "cw")

    @p.block.scalar
    def _(s):
        p.wait(s, tg + "x1", 1)
        p.cinc(s.activation(esc2[:], esc[:], AF.Exp), tg + "x2")
        for i in range(NT):
            p.wait(s, tg + "ct", i + 1)
            p.cinc(s.copy(cwT[:, ts(i)], p.aux1[0:8, 512:640]), tg + "ce")

    def up_mark(gi, tq):
        return gi * 2 + tq + 1

    def dn_mark(hi, sub):
        return hi * 8 + sub + 1

    @p.block.gpsimd
    def _(g):
        for e in range(E):
            for tc in range(TCN):
                et = e * TCN + tc
                for fc in range(FC):
                    gi = et * FC + fc
                    buf = w13[gi % 2]
                    if gi >= 2:
                        g.wait_ge(p.sem(tg + "mm")[0], up_mark(gi - 2, 1))
                    src1 = bass.AP(w1_ap.tensor, e * D * 2048 + fc * 128,
                                   [[2048, 128], [128 * 2048, 8], [1, 128]])
                    src3 = bass.AP(w3_ap.tensor, e * D * 2048 + fc * 128,
                                   [[2048, 128], [128 * 2048, 8], [1, 128]])
                    p.dinc(g.dma_start(out=buf[:, 0:D].rearrange(
                        "q (c m) -> q c m", m=128), in_=src1), tg + "wa")
                    p.dinc(g.dma_start(out=buf[:, D:2 * D].rearrange(
                        "q (c m) -> q c m", m=128), in_=src3), tg + "wa")
                for half in range(2):
                    hi = et * 2 + half
                    if hi >= 1:
                        g.wait_ge(p.sem(tg + "dm")[0], dn_mark(hi - 1, 7))
                    for fc2 in range(FC):
                        src2 = bass.AP(w2_ap.tensor,
                                       e * 2048 * D + fc2 * 128 * D + half * 512,
                                       [[D, 128], [1, 512]])
                        p.dinc(g.dma_start(out=w2h[:, fc2 * 512:fc2 * 512 + 512],
                                           in_=src2), tg + "wb")

    @p.block.tensor
    def _(t):
        for e in range(E):
            for tc in range(TCN):
                et = e * TCN + tc
                if et == 0:
                    p.wait(t, tg + "ce", NT)
                else:
                    p.wait(t, tg + "cbe", 2 * et)
                for tq in range(2):
                    p.cinc(t.matmul(p.aux1[:, ts(tq, 512)], sel8[:, ts(e)],
                                    cwT[:, tc * 1024 + tq * 512:
                                        tc * 1024 + tq * 512 + 512],
                                    start=True, stop=True), tg + "cb")
                for fc in range(FC):
                    gi = et * FC + fc
                    buf = w13[gi % 2]
                    t.wait_ge(p.sem(tg + "wa")[0], 32 * (gi + 1))
                    for tq in range(2):
                        idx = gi * 2 + tq
                        if idx >= 1:
                            t.wait_ge(p.sem(tg + "h")[0], 2 * idx)
                        cols = slice(tc * 1024 + tq * 512,
                                     tc * 1024 + tq * 512 + 512)
                        for dc in range(DC):
                            t.matmul(p.ps_a[:], buf[:, ts(dc)], znT[dc][:, cols],
                                     start=(dc == 0), stop=(dc == DC - 1))
                        for dc in range(DC):
                            ins = t.matmul(p.ps_b[:],
                                           buf[:, D + dc * 128:D + dc * 128 + 128],
                                           znT[dc][:, cols],
                                           start=(dc == 0), stop=(dc == DC - 1))
                        p.cinc(ins, tg + "mm")
                t.wait_ge(p.sem(tg + "h")[0], 2 * (et + 1) * FC * 2)
                for half in range(2):
                    hi = et * 2 + half
                    t.wait_ge(p.sem(tg + "wb")[0], 16 * 16 * (hi + 1))
                    for sub in range(8):
                        dmark = dn_mark(hi, sub)
                        if dmark > 2:
                            t.wait_ge(p.sem(tg + "ad")[0], dmark - 2)
                        pd = p.ps_c[sub % 2]
                        for fc in range(FC):
                            ins = t.matmul(pd[:],
                                           h_all[:, fc * 1024 + sub * 128:
                                                 fc * 1024 + sub * 128 + 128],
                                           w2h[:, fc * 512:fc * 512 + 512],
                                           start=(fc == 0), stop=(fc == FC - 1))
                        p.cinc(ins, tg + "dm")

    @p.block.scalar
    def _(s):
        for e in range(E):
            for tc in range(TCN):
                et = e * TCN + tc
                p.wait(s, tg + "cb", 2 * (et + 1))
                if et >= 1:
                    s.wait_ge(p.sem(tg + "h")[0], 2 * et * FC * 2)
                for tq in range(2):
                    p.cinc(s.copy(cwbc[:, ts(tq, 512)], p.aux1[:, ts(tq, 512)]),
                           tg + "cbe")
                for fc in range(FC):
                    for tq in range(2):
                        idx = (et * FC + fc) * 2 + tq
                        s.wait_ge(p.sem(tg + "mm")[0], idx + 1)
                        if idx >= 1:
                            s.wait_ge(p.sem(tg + "h")[0], 2 * idx - 1)
                        p.cinc(s.activation(s_sb[:], p.ps_a[:], AF.Silu),
                               tg + "h")

    @p.block.vector
    def _(v):
        for e in range(E):
            for tc in range(TCN):
                et = e * TCN + tc
                if et >= 1:
                    v.wait_ge(p.sem(tg + "dm")[0], 16 * et)
                v.wait_ge(p.sem(tg + "cbe")[0], 2 * (et + 1))
                for fc in range(FC):
                    for tq in range(2):
                        idx = (et * FC + fc) * 2 + tq
                        v.wait_ge(p.sem(tg + "h")[0], 2 * idx + 1)
                        v.scalar_tensor_tensor(t1_sb[:], p.ps_b[:], 1.0, s_sb[:],
                                               op0=ALU.mult, op1=ALU.mult)
                        p.cinc(v.tensor_mul(
                            h_all[:, fc * 1024 + tq * 512:
                                  fc * 1024 + tq * 512 + 512],
                            t1_sb[:], cwbc[:, ts(tq, 512)]), tg + "h")
                for half in range(2):
                    hi = et * 2 + half
                    for sub in range(8):
                        v.wait_ge(p.sem(tg + "dm")[0], dn_mark(hi, sub))
                        ti = tc * 8 + sub
                        dst = p.xsb[:, ti * D + half * 512:
                                    ti * D + half * 512 + 512]
                        p.cinc(v.scalar_tensor_tensor(dst, p.ps_c[sub % 2][:],
                                                      1.0, dst, op0=ALU.mult,
                                                      op1=ALU.add), tg + "ad")






QW = 2112


def emit_l0_attn(p, ins, dr):
    nc = p.nc
    qT = p.sbuf("sqT", [128, 8 * QW], BF16)
    vsb = p.sbuf("sV", [128, 8 * 17 * 128], BF16)
    rq = p.sbuf("srq", [16, QW], BF16)
    coefb = p.sbuf("scoef", [16, QW], BF16)
    m_sb = p.sbuf("sm", [128, 8], F32)
    negm2 = p.sbuf("snegm2", [128, 8], F32)
    mo2 = p.sbuf("smo2", [128, 2], BF16)
    Hm = p.sbuf("sHm", [128, 8 * 128], BF16)
    selR = p.sbuf("sselR", [16, 8 * 192], BF16)
    selC = p.sbuf("sselC", [16, 8 * 128], BF16)
    basem = p.sbuf("sbasem", [128, 8], F32)
    clipc = p.sbuf("sclipc", [128, 64], F32)
    clipn = p.sbuf("sclipn", [128, 64], F32)
    clip2 = p.sbuf("sclip2", [128, 64], F32)
    clip2n = p.sbuf("sclip2n", [128, 64], F32)
    sigp = p.sbuf("ssigp", [128, 8], F32)
    sigb = p.sbuf("ssigb", [128, 8], BF16)
    spp = p.sbuf("sspp", [128, 8], F32)
    import os
    vsdbg = None
    if "hvchk" in os.environ.get("DBG", ""):
        vsdbg = p.sbuf("svsdbg", [64, 256], BF16)
        p.dbg_vstgchk = vsdbg[:]

    _l0a_consts(p, ins, dr, locals())
    _l0a_proj(p, ins, dr, locals())
    _l0a_select(p, ins, dr, locals())
    import os
    if "hvchk" in os.environ.get("DBG", ""):
        hvdbg = p.sbuf("shvdbg", [63, D], BF16)
        p.dbg_hvchk = hvdbg[:]

        @p.block.sync
        def _(s):
            p.wait(s, "sdx", 32)
            p.dinc(s.dma_start(out=hvdbg[:], in_=dr["hv"][0:63, :]), "shvc")
            p.wait(s, "shvc")


def _l0a_consts(p, ins, dr, E):
    nc = p.nc
    selR, selC, Hm = E["selR"], E["selC"], E["Hm"]
    basem, clipc, clipn = E["basem"], E["clipc"], E["clipn"]
    clip2, clip2n, mo2 = E["clip2"], E["clip2n"], E["mo2"]
    p.push_scope()
    ioc = p.sbuf("sioc", [16, 192], I32)
    hebu = p.sbuf("shebu", [32, 1024], BF16)
    hebv = p.sbuf("shebv", [32, 1024], BF16)
    zb16 = p.sbuf("szb16", [128, 256], BF16)

    @p.block.gpsimd
    def _(g):
        for h in range(8):
            if h >= 1:
                p.wait(g, "ssel", 2 * h)
            p.cinc(g.iota(ioc[:], pattern=[[0, 192]], base=2 * h,
                          channel_multiplier=-1,
                          allow_small_or_imprecise_dtypes=True), "sio")
        p.dinc(g.dma_start(out=basem[:], in_=ins["basem"]), "scl")
        p.dinc(g.dma_start(out=clipc[:], in_=ins["clipc"]), "scl")
        p.dinc(g.dma_start(out=clipn[:], in_=ins["clipn"]), "scl")
        p.dinc(g.dma_start(out=clip2[:], in_=ins["clip2"]), "scl")
        p.dinc(g.dma_start(out=clip2n[:], in_=ins["clip2n"]), "scl")
        p.dinc(g.dma_start(out=hebu[:], in_=ins["hebUT"]), "scl")
        p.dinc(g.dma_start(out=hebv[:], in_=ins["hebV"]), "scl")
        p.cinc(g.memset(E["vsb"][:], 0.0), "svz")
        p.cinc(g.memset(zb16[:], 0.0), "szf")

    @p.block.vector
    def _(v):
        for h in range(8):
            p.wait(v, "sio", h + 1)
            p.cinc(v.tensor_single_scalar(selR[:, h * 192:h * 192 + 192],
                                          ioc[:], 0, ALU.is_equal), "ssel")
            p.cinc(v.tensor_single_scalar(selC[:, ts(h)], ioc[:, 0:128],
                                          -1, ALU.is_equal), "ssel")
        v.memset(mo2[:, 1:2], 1.0)
        v.memset(E["spp"][:], 0.0)
        p.cinc(v.memset(mo2[:, 0:1], 0.0), "ssel")

    @p.block.sync
    def _(s):
        p.wait(s, "szf")
        for j in range(8):
            p.dinc(s.dma_start(out=bass.AP(dr["wband"].tensor, j * 128 * 256,
                                           [[256, 128], [1, 256]]),
                               in_=zb16[:]), "szf2")

    @p.block.tensor
    def _(t):
        p.wait(t, "scl")
        p.wait(t, "cst")
        for h in range(8):
            if h >= 1:
                p.wait(t, "shme", h)
            p.cinc(t.matmul(p.aux1[:, 0:128], hebu[:, ts(h)], hebv[:, ts(h)],
                            start=True, stop=True), "shm")

    @p.block.scalar
    def _(s):
        for h in range(8):
            p.wait(s, "shm", h + 1)
            p.cinc(s.copy(Hm[:, ts(h)], p.aux1[:, 0:128]), "shme")
    p.pop_scope()


def _l0a_proj(p, ins, dr, E):
    nc = p.nc
    qT, vsb, rq, coefb = E["qT"], E["vsb"], E["rq"], E["coefb"]
    m_sb, negm2, mo2 = E["m_sb"], E["negm2"], E["mo2"]
    basem, sigp, sigb = E["basem"], E["sigp"], E["sigb"]
    p.push_scope()
    lnTr = [p.sbuf(f"slnT{dc}", [128, 2 * 512], BF16) for dc in range(DC)]
    hT = [p.sbuf(f"shT{dc}", [128, 63], BF16) for dc in range(DC)]
    enc_sb = p.sbuf("senc", [128, 8 * 1024], BF16)
    hcol = p.sbuf("shcol", [63, 4], F32)
    vstg = p.sbuf("svstg", [64, D], BF16)
    p.push_scope()
    hln = p.sbuf("shln", [63, D], BF16)
    p.push_scope()
    hx = p.sbuf("shx", [63, D], BF16)
    hsq = p.sbuf("shsq", [63, D], BF16)

    @p.block.gpsimd
    def _(g):
        p.dinc(g.dma_start(out=hx[:], in_=ins["x_halo"]), "shxl")

    @p.block.scalar
    def _(s):
        p.wait(s, "shxl")
        p.cinc(s.activation(hsq[:], hx[:], AF.Square), "shs")
        p.wait(s, "shv", 1)
        p.cinc(s.activation(hcol[:, 0:1], hcol[:, 3:4], AF.Sqrt,
                            bias=p.eps[0:63, 1:2]), "shq")

    @p.block.vector
    def _(v):
        p.wait(v, "shxl")
        p.wait(v, "cst")
        p.ssync(v, v.reduce_sum(hcol[:, 0:1], hx[:], AX), "vslf")
        p.ssync(v, v.tensor_scalar_mul(hcol[:, 1:2], hcol[:, 0:1], 1.0 / D),
                "vslf")
        v.tensor_mul(hcol[:, 2:3], hcol[:, 1:2], hcol[:, 1:2])
        p.wait(v, "shs", 1)
        p.ssync(v, v.reduce_sum(hcol[:, 0:1], hsq[:], AX), "vslf")
        p.ssync(v, v.scalar_tensor_tensor(hcol[:, 3:4], hcol[:, 0:1], 1.0 / D,
                                          hcol[:, 2:3], op0=ALU.mult,
                                          op1=ALU.subtract), "vslf")
        p.cinc(v.tensor_copy(p.scol[0:1, 0:1], hcol[0:1, 0:1]), "shv")
        p.wait(v, "shq", 1)
        p.ssync(v, v.reciprocal(hcol[:, 2:3], hcol[:, 0:1]), "vslf")
        p.cinc(v.tensor_scalar(hln[:], hx[:], hcol[:, 1:2], hcol[:, 2:3],
                               op0=ALU.subtract, op1=ALU.mult), "shn")
    p.pop_scope()

    @p.block.tensor
    def _(t):
        p.wait(t, "shn", 1)
        for dc in range(DC):
            if dc >= 1:
                p.wait(t, "shte", dc)
            p.cinc(t.matmul(p.ps_c[0][:, 0:63], hln[:, ts(dc)],
                            p.ident[0:63, 0:63], start=True, stop=True),
                   "sht")

    @p.block.scalar
    def _(s):
        for dc in range(DC):
            p.wait(s, "sht", dc + 1)
            p.cinc(s.copy(hT[dc][:], p.ps_c[0][:, 0:63]), "shte")
    p.pop_scope()

    for qk in range(2):
        wsrc = ins["encq"] if qk == 0 else ins["encv"]
        tgl = "sA" if qk == 0 else "sB"

        @p.block.gpsimd
        def _(g, qk=qk):
            if qk == 1:
                g.wait_ge(p.sem("spj")[0], 32)
                g.wait_ge(p.sem("sph")[0], 8)
                g.wait_ge(p.sem("spj")[0], 64)
            for dc in range(DC):
                p.dinc(g.dma_start(out=enc_sb[:, ts(dc, 1024)],
                                   in_=wsrc[ts(dc), :]), "senl")

        def cbt(t, ch, qk=qk):
            t.wait_ge(p.sem("senl")[0], 16 * DC * (qk + 1))
            t.wait_ge(p.sem(tgl + "e")[0], DC * 4 * (ch + 1))
            if qk == 0:
                t.wait_ge(p.sem("shte")[0], 8)
                for h in range(8):
                    idx = ch * 8 + h
                    if idx >= 2:
                        t.wait_ge(p.sem("spje")[0], idx - 1)
                    ps = p.ps_a if idx % 2 == 0 else p.ps_b
                    for dc in range(DC):
                        ie = t.matmul(ps[:], enc_sb[:, dc * 1024 + h * 128:
                                                    dc * 1024 + h * 128 + 128],
                                      lnTr[dc][:, ts(ch % 2, 512)],
                                      start=(dc == 0), stop=(dc == DC - 1))
                    p.cinc(ie, "spj")
                if ch == 3:
                    for h in range(8):
                        if h >= 1:
                            t.wait_ge(p.sem("sphe")[0], h)
                        for dc in range(DC):
                            ie = t.matmul(p.ps_c[1][:, 0:63],
                                          enc_sb[:, dc * 1024 + h * 128:
                                                  dc * 1024 + h * 128 + 128],
                                          hT[dc][:],
                                          start=(dc == 0), stop=(dc == DC - 1))
                        p.cinc(ie, "sph")
            else:
                for sub in range(4):
                    for half in range(2):
                        vidx = (ch * 4 + sub) * 2 + half
                        if vidx >= 2:
                            t.wait_ge(p.sem("spve")[0], vidx - 1)
                        ps = p.ps_a if vidx % 2 == 0 else p.ps_b
                        for dc in range(DC):
                            ie = t.matmul(ps[:],
                                          lnTr[dc][:, (ch % 2) * 512 + sub * 128:
                                               (ch % 2) * 512 + sub * 128 + 128],
                                          enc_sb[:, dc * 1024 + half * 512:
                                                  dc * 1024 + half * 512 + 512],
                                          start=(dc == 0), stop=(dc == DC - 1))
                        p.cinc(ie, "spv")
                if ch == 3:
                    for half in range(2):
                        if half == 1:
                            t.wait_ge(p.sem("sphve")[0], 1)
                        for dc in range(DC):
                            ie = t.matmul(p.ps_c[1][0:63, :], hT[dc][:],
                                          enc_sb[:, dc * 1024 + half * 512:
                                                  dc * 1024 + half * 512 + 512],
                                          start=(dc == 0), stop=(dc == DC - 1))
                        p.cinc(ie, "sphv")

        def cbv(v, ch, qk=qk):
            if qk == 0:
                return
            if ch == 0:
                p.wait(v, "svz", 1)
            for sub in range(4):
                for half in range(2):
                    vidx = (ch * 4 + sub) * 2 + half
                    v.wait_ge(p.sem("spv")[0], vidx + 1)
                    ps = p.ps_a if vidx % 2 == 0 else p.ps_b
                    tile = ch * 4 + sub + 1
                    dst = bass.AP(vsb[:].tensor,
                                  tile * 128 + half * 4 * 17 * 128,
                                  [[8 * 17 * 128, 128], [17 * 128, 4], [1, 128]])
                    p.cinc(v.tensor_relu(
                        dst, ps[:].rearrange("q (h e) -> q h e", e=128)),
                        "spve")

        def pre(eng, i, qk=qk):
            if i == 0:
                p.wait(eng, "cst")
                if qk == 0:
                    for nm, tgt in p.x_ready:
                        p.wait(eng, nm, tgt)

        def cbs(s, ch, qk=qk):
            if qk != 0:
                return
            for h in range(8):
                idx = ch * 8 + h
                s.wait_ge(p.sem("spj")[0], idx + 1)
                ps = p.ps_a if idx % 2 == 0 else p.ps_b
                p.cinc(s.activation(
                    qT[:, h * QW + 63 + ch * 512:
                       h * QW + 63 + ch * 512 + 512],
                    ps[:], AF.Relu), "spje")
            if ch == 3:
                for h in range(8):
                    s.wait_ge(p.sem("sph")[0], h + 1)
                    p.cinc(s.activation(qT[:, h * QW:h * QW + 63],
                                        p.ps_c[1][:, 0:63], AF.Relu), "sphe")

        emit_ln_transpose(p, lnTr, NT, lambda i: p.xsb[:, ts(i, D)], "ln", tgl,
                          pre_wait=pre,
                          dst_col_fn=lambda i: ((i // 4) % 2) * 512 + (i % 4) * 128,
                          cb_every=4, cb_tensor=cbt, cb_vector=cbv,
                          cb_scalar=cbs)

        if qk == 0:
            xpst = p.sbuf("sxpst", [128, 2 * D], BF16)
            qstg = p.sbuf("sqstg", [128, 2 * 512], BF16)
            spp = E["spp"]

            @p.block.gpsimd
            def _(g):
                p.wait(g, "sht", 8)
                for i in range(NT):
                    if i >= 2:
                        p.wait(g, "sAn", 16 + i - 1)
                    p.dinc(g.dma_start(out=xpst[:, ts(i % 2, D)],
                                       in_=ins["x_pair"][ts(i), :]), "sxpl")

            def cpre(eng, i):
                eng.wait_ge(p.sem("sxpl")[0], 16 * (i + 1))
                if i == 0:
                    p.wait(eng, "cst")

            def ccbt(t, ch):
                t.wait_ge(p.sem("sAe")[0], DC * (16 + 4 * (ch + 1)))
                for h in range(8):
                    idx = ch * 8 + h
                    t.wait_ge(p.sem("spje")[0], 31 + idx)
                    ps = p.ps_a if idx % 2 == 0 else p.ps_b
                    for dc in range(DC):
                        ie = t.matmul(ps[:], enc_sb[:, dc * 1024 + h * 128:
                                                    dc * 1024 + h * 128 + 128],
                                      lnTr[dc][:, ts(ch % 2, 512)],
                                      start=(dc == 0), stop=(dc == DC - 1))
                    p.cinc(ie, "spj")

            def ccbs(s, ch):
                for h in range(8):
                    idx = ch * 8 + h
                    s.wait_ge(p.sem("spj")[0], 33 + idx)
                    if idx >= 2:
                        s.wait_ge(p.sem("ssgp")[0], idx - 1)
                    ps = p.ps_a if idx % 2 == 0 else p.ps_b
                    p.cinc(s.activation(qstg[:, ts(idx % 2, 512)], ps[:],
                                        AF.Relu), "spje")

            def ccbv(v, ch):
                for h in range(8):
                    idx = ch * 8 + h
                    v.wait_ge(p.sem("spje")[0], 33 + idx)
                    S = p.ssync
                    S(v, v.reduce_sum(p.scol[:], qstg[:, ts(idx % 2, 512)],
                                      AX), "vslf")
                    S(v, v.reduce_sum(p.m2col[:], bass.AP(
                        qstg[:].tensor, (idx % 2) * 512,
                        [[2 * 512, 128], [8, 64]]), AX), "vslf")
                    S(v, v.scalar_tensor_tensor(
                        spp[:, h:h + 1], p.scol[:], 1.0 / 4096.0,
                        spp[:, h:h + 1], op0=ALU.mult, op1=ALU.add), "vslf")
                    p.cinc(v.scalar_tensor_tensor(
                        spp[:, h:h + 1], p.m2col[:], 0.5 / 512.0,
                        spp[:, h:h + 1], op0=ALU.mult, op1=ALU.add), "ssgp")

            emit_ln_transpose(p, lnTr, NT, lambda i: xpst[:, ts(i % 2, D)],
                              "ln", "sA", base=16, pre_wait=cpre,
                              dst_col_fn=lambda i: ((i // 4) % 2) * 512 +
                              (i % 4) * 128,
                              cb_every=4, cb_tensor=ccbt, cb_vector=ccbv,
                              cb_scalar=ccbs)

        if qk == 1:
            @p.block.scalar
            def _(s):
                for half in range(2):
                    s.wait_ge(p.sem("sphv")[0], half + 1)
                    p.cinc(s.activation(vstg[0:63, ts(half, 512)],
                                        p.ps_c[1][0:63, :], AF.Relu), "sphve")

            if E.get("vsdbg") is not None:
                @p.block.scalar
                def _(s):
                    p.wait(s, "sphve", 2)
                    p.cinc(s.copy(E["vsdbg"][0:63, :], vstg[0:63, 0:256]),
                           "svsd")

            @p.block.sync
            def _(s):
                p.wait(s, "sphve", 2)
                p.wait(s, "svz", 1)
                p.dinc(s.dma_start(out=dr["hv"], in_=vstg[:]), "shvw")
                p.wait(s, "shvw")
                dst = bass.AP(vsb[:].tensor, 65 * (8 * 17 * 128),
                              [[8 * 17 * 128, 63], [17 * 128, 8], [1, 128]])
                p.dinc(s.dma_start(
                    out=dst, in_=dr["hv"][0:63, :].rearrange(
                        "t (h e) -> t h e", e=128)), "shvr")

    p.pop_scope()
    p.push_scope()
    qsq = p.sbuf("sqsq", [128, QW], BF16)
    rqs = p.sbuf("srqs", [2, QW], BF16)
    cscr = qsq[0:16, :]
    cscr2 = p.sbuf("scscr2", [16, QW], BF16)[0:16, :]
    mnst = p.sbuf("smnst", [128, 1024], BF16)
    rowstg = p.sbuf("srowstg", [1, 1024], BF16)
    b1c = p.sbuf("sb1c", [128, 4], F32)
    b2c = p.sbuf("sb2c", [128, 8], F32)
    hid4r = p.sbuf("shid4r", [128, 4], BF16)
    hid4f = p.sbuf("shid4f", [128, 4], F32)
    hid4 = p.sbuf("shid4", [128, 4], BF16)
    mod8r = p.sbuf("smod8r", [128, 8], BF16)
    mod8 = p.sbuf("smod8", [128, 8], F32)
    ms1 = p.sbuf("sms1", [128, 8], F32)
    ms2 = p.sbuf("sms2", [128, 8], F32)

    @p.block.vector
    def _(v):
        p.wait(v, "spje", 32)
        p.wait(v, "sphe", 8)
        for h in range(8):
            p.ssync(v, v.reduce_sum(p.scol[:], qT[:, h * QW + 63:
                                                  h * QW + 63 + 2048], AX),
                    "vslf")
            v.tensor_scalar_mul(p.qcol[:], p.scol[:], 1.0 / 4096.0)
            p.ssync(v, v.reduce_sum(p.m2col[:], bass.AP(
                qT[:].tensor, h * QW + 63, [[8 * QW, 128], [8, 256]]), AX),
                "vslf")
            p.cinc(v.scalar_tensor_tensor(sigp[:, h:h + 1], p.m2col[:],
                                          0.5 / 512.0, p.qcol[:],
                                          op0=ALU.mult, op1=ALU.add), "ssgp")
        p.wait(v, "ssgp", 40)
        p.cinc(v.tensor_add(sigb[:], sigp[:], E["spp"][:]), "ssgr2")

    @p.block.gpsimd
    def _(g):
        g.wait_ge(p.sem("spv")[0], 32)
        g.wait_ge(p.sem("sphv")[0], 2)
        g.wait_ge(p.sem("shvw")[0], 16)
        g.wait_ge(p.sem("ssgp")[0], 32)
        g.wait_ge(p.sem("sAn")[0], 32)
        p.dinc(g.dma_start(out=b1c[:], in_=ins["b1c"]), "smw")
        p.dinc(g.dma_start(out=b2c[:], in_=ins["b2c"]), "smw")
        for dc in range(DC):
            if dc >= 1:
                g.wait_ge(p.sem("smm1")[0], dc)
            p.dinc(g.dma_start(out=mnst[:, 0:512],
                               in_=ins["mnw1"][ts(dc), :]), "smw1")
        for c4 in range(4):
            if c4 == 0:
                g.wait_ge(p.sem("smm1")[0], DC)
            else:
                g.wait_ge(p.sem("smm2")[0], 2 * c4)
            p.dinc(g.dma_start(out=mnst[:], in_=ins["mnw2"][ts(c4), :]),
                   "smw2")

    @p.block.tensor
    def _(t):
        t.wait_ge(p.sem("ssgr2")[0], 1)
        for dc in range(DC):
            t.wait_ge(p.sem("smw1")[0], 16 * (dc + 1))
            p.cinc(t.matmul(p.aux1[0:1, 0:512], sigb[:, dc:dc + 1],
                            mnst[:, 0:512], start=(dc == 0),
                            stop=(dc == DC - 1)), "smm1")
        t.wait_ge(p.sem("shd4")[0], 16)
        t.wait_ge(p.sem("smh2")[0], 1)
        t.wait_ge(p.sem("smh0")[0], 1)
        for c4 in range(4):
            t.wait_ge(p.sem("smw2")[0], 16 * (c4 + 1))
            for half in range(2):
                ie = t.matmul(p.aux1[0:1, half * 512:half * 512 + 512],
                              hid4[:, c4:c4 + 1], mnst[:, ts(half, 512)],
                              start=(c4 == 0), stop=(c4 == 3))
            p.cinc(ie, "smm2", 2)

    @p.block.vector
    def _(v):
        p.wait(v, "shd4", 16)
        p.wait(v, "smw", 32)
        p.cinc(v.tensor_add(hid4f[:], hid4r[:], b1c[:]), "smh1")
        p.wait(v, "shd8", 16)
        p.cinc(v.scalar_tensor_tensor(ms2[:], mod8r[:], 1.0, b2c[:],
                                      op0=ALU.mult, op1=ALU.add), "smh3")
        p.wait(v, "smh3", 1)
        p.cinc(v.scalar_tensor_tensor(ms2[:], ms2[:], 0.1, basem[:],
                                      op0=ALU.mult, op1=ALU.add)
               if False else
               v.scalar_tensor_tensor(mod8[:], ms2[:], 0.1, basem[:],
                                      op0=ALU.mult, op1=ALU.add), "smt1")
        p.wait(v, "smt3", 1)
        v.tensor_scalar_max(ms1[:], mod8[:], 0.0)
        p.ssync(v, v.tensor_add(m_sb[:], ms1[:], sigp[:]), "vslf")
        p.cinc(v.tensor_scalar_mul(negm2[:], m_sb[:], -2.0), "smdone")

    @p.block.scalar
    def _(s):
        p.wait(s, "smm1", DC)
        p.cinc(s.copy(rowstg[:, 0:512], p.aux1[0:1, 0:512]), "smh0")
        p.wait(s, "smh1", 1)
        p.cinc(s.activation(hid4[:], hid4f[:], AF.Gelu), "smh2")
        p.wait(s, "smm2", 8)
        p.cinc(s.copy(rowstg[:], p.aux1[0:1, 0:1024]), "smo0")
        p.wait(s, "smt1", 1)
        s.activation(ms1[:], mod8[:], AF.Abs)
        s.activation(sigp[:], ms1[:], AF.Exp, scale=-1.0)
        p.cinc(s.activation(sigp[:], sigp[:], AF.Ln, bias=1.0)
               if False else
               s.activation(ms2[:], sigp[:], AF.Ln, bias=1.0), "smt3x")
        p.cinc(s.copy(sigp[:], ms2[:]), "smt3")

    @p.block.sync
    def _(s):
        p.wait(s, "smh0", 1)
        p.dinc(s.dma_start(out=dr["hid"][0:512].unsqueeze(0),
                           in_=rowstg[:, 0:512]), "smhw")
        p.wait(s, "smhw")
        p.dinc(s.dma_start(out=hid4r[:], in_=bass.AP(dr["hid"].tensor, 0,
                                                     [[1, 128], [128, 4]]),
                           allow_slow_non_contiguous=True), "shd4")
        p.wait(s, "smo0", 1)
        p.dinc(s.dma_start(out=dr["mod"].unsqueeze(0), in_=rowstg[:]),
               "smodw")
        p.wait(s, "smodw")
        p.dinc(s.dma_start(out=mod8r[:], in_=bass.AP(dr["mod"].tensor, 0,
                                                     [[1, 128], [128, 8]]),
                           allow_slow_non_contiguous=True), "shd8")

    @p.block.scalar
    def _(s):
        for h in range(8):
            p.wait(s, "smdone", 1)
            if h >= 1:
                s.wait_ge(p.sem("srmm")[0], 5 * h)
                s.wait_ge(p.sem("srqw")[0], 16 * h)
            p.cinc(s.activation(qsq[:], qT[:, h * QW:(h + 1) * QW], AF.Square),
                   "srsq")
            for ck in range(5):
                s.wait_ge(p.sem("srmm")[0], 5 * h + ck + 1)
                w = 512 if ck < 4 else 64
                p.cinc(s.copy(rqs[:, ck * 512:ck * 512 + w],
                              p.aux1[0:2, 0:w]), "srev")
        p.wait(s, "srql", 16 * 8)
        p.cinc(s.activation(cscr, rq[:], AF.Sqrt), "scf1")

    @p.block.vector
    def _(v):
        for h in range(8):
            p.wait(v, "smdone", 1)
            if h >= 1:
                v.wait_ge(p.sem("srmm")[0], 5 * h)
            p.cinc(v.tensor_scalar(mo2[:, 0:1], m_sb[:, h:h + 1], 1.0, None,
                                   op0=ALU.mult), "srm2")
        p.wait(v, "scf1", 1)
        v.tensor_scalar_max(cscr2, cscr, 1e-12)
        with nc.allow_low_precision(reason="coef term is bf16-class noise"):
            p.ssync(v, v.reciprocal(coefb[:], cscr2), "vslf")
        v.tensor_scalar(cscr2, cscr, 0.2, None, op0=ALU.is_gt)
        p.ssync(v, v.tensor_scalar_mul(cscr, coefb[:], 0.1), "vslf")
        p.cinc(v.tensor_mul(coefb[:], cscr, cscr2), "scf2")

    @p.block.tensor
    def _(t):
        for h in range(8):
            t.wait_ge(p.sem("srsq")[0], h + 1)
            t.wait_ge(p.sem("srm2")[0], h + 1)
            for ck in range(5):
                if 5 * h + ck >= 1:
                    t.wait_ge(p.sem("srev")[0], 5 * h + ck)
                w = 512 if ck < 4 else 64
                p.cinc(t.matmul(p.aux1[0:2, 0:w], mo2[:],
                                qsq[:, ck * 512:ck * 512 + w],
                                start=True, stop=True), "srmm")

    @p.block.sync
    def _(s):
        for h in range(8):
            p.wait(s, "srev", 5 * (h + 1))
            p.dinc(s.dma_start(out=dr["rq"], in_=rqs[:]), "srqw")
            p.wait(s, "srqw", 16 * (h + 1))
            p.dinc(s.dma_start(out=rq[2 * h:2 * h + 2, :], in_=dr["rq"]),
                   "srql")
    p.pop_scope()


def _l0a_select(p, ins, dr, E):
    nc = p.nc
    qT, vsb, rq, coefb = E["qT"], E["vsb"], E["rq"], E["coefb"]
    negm2, Hm = E["negm2"], E["Hm"]
    selR, selC = E["selR"], E["selC"]
    clipc, clipn, clip2, clip2n = (E["clipc"], E["clipn"], E["clip2"],
                                   E["clip2n"])
    p.push_scope()
    dec_sb = p.sbuf("sdec", [128, 8 * 512], BF16)
    outg = p.sbuf("soutg", [128, 8 * 512], BF16)
    mqg = [p.sbuf(f"smqg{r}", [128, 512], BF16) for r in range(2)]
    _dstg1 = p.sbuf("sDst", [128, 4 * 192], F32)
    Dstg = [_dstg1, _dstg1]
    Dwin = [p.sbuf(f"sDw{r}", [128, 4 * 64], F32) for r in range(2)]
    Ewin = [p.sbuf(f"sEw{r}", [128, 4 * 64], F32) for r in range(2)]
    zapA = p.sbuf("szapA", [128, 4 * 64], F32)
    zapB = p.sbuf("szapB", [128, 4 * 64], F32)
    zapg = [p.sbuf(f"szapg{r}", [128, 4 * 64], F32) for r in range(2)]
    Wraw = Ewin
    Wn = [p.sbuf(f"sWn{r}", [128, 4 * 64], BF16) for r in range(2)]
    Bsb = [p.sbuf(f"sBsb{r}", [128, 4 * 256], BF16) for r in range(2)]
    WtAB = [p.sbuf(f"sWt{r}", [128, 8 * 128], BF16) for r in range(2)]
    qscb = [p.sbuf(f"sqsc{r}", [128, 512], BF16) for r in range(2)]
    s8 = p.sbuf("ss8", [128, 32], F32)
    d0c = p.sbuf("sd0", [128, 1], F32)
    e0c = p.sbuf("se0", [128, 1], F32)
    tfx = p.sbuf("stfx", [128, 64], F32)
    tfx2 = p.sbuf("stfx2", [128, 64], F32)
    tfx3 = p.sbuf("stfx3", [128, 64], F32)
    wfix = p.sbuf("swfix", [128, 64], BF16)
    zc4 = p.sbuf("szc4", [128, 4], F32)
    cnt4 = p.sbuf("scnt4", [128, 4], F32)
    ov4 = p.sbuf("sov4", [128, 4], F32)
    oe0 = p.sbuf("soe0", [128, 1], F32)
    rc4 = ov4
    s0c = p.sbuf("ss0c", [128, 1], F32)
    s0b = p.sbuf("ss0b", [128, 1], BF16)
    msk = zapA

    POFF = [0, 192, 512, 704]
    p.dbg_msb = E["m_sb"][:]
    p.dbg_sigb = E["sigb"][:]
    p.dbg_rq = E["rq"][:]
    p.dbg_coef = E["coefb"][:]
    p.dbg_qT0 = E["qT"][:, 0:QW]
    p.dbg_vsb0 = E["vsb"][:, 0:17 * 128]
    p.dbg_outg = outg[:]
    p.dbg_Wn = Wn[1][:]
    p.dbg_Ew = Ewin[1][:]
    p.dbg_Dw = Dwin[1][:]
    p.dbg_Hm = E["Hm"][:]

    @p.block.gpsimd
    def _(g):
        p.wait(g, "scf2", 1)
        for grp in range(4):
            for half in range(2):
                gh = grp * 2 + half
                if gh >= 1:
                    g.wait_ge(p.sem("sdm")[0], 4 * gh)
                for dc in range(DC):
                    p.dinc(g.dma_start(
                        out=dec_sb[:, ts(dc, 512)],
                        in_=ins["dec"][ts(dc), half * 512:half * 512 + 512]),
                        "sdcl")

    for grp in range(4):
        for h in range(8):
            k = grp * 8 + h
            r = k % 2
            _l0a_iter(p, E, locals(), grp, h, k, r, dr)
        _l0a_decode(p, E, locals(), grp, dr)
    p.pop_scope()
    p.x_ready = [("sdx", 32)]


def _l0a_iter(p, E, L, grp, h, k, r, dr):
    nc = p.nc
    qT, vsb, rq, coefb = E["qT"], E["vsb"], E["rq"], E["coefb"]
    negm2, Hm = E["negm2"], E["Hm"]
    selR, selC = E["selR"], E["selC"]
    clipc, clipn, clip2, clip2n = (E["clipc"], E["clipn"], E["clip2"],
                                   E["clip2n"])
    (mqg, Dstg, Dwin, Ewin, zapA, zapB, zapg, Wraw, Wn, Bsb, WtAB, qscb, s8,
     d0c, e0c, tfx, tfx2, wfix, zc4, cnt4, ov4, oe0, rc4, s0c, s0b, msk,
     POFF, outg, tfx3) = (L["mqg"], L["Dstg"], L["Dwin"], L["Ewin"], L["zapA"],
                          L["zapB"], L["zapg"], L["Wraw"], L["Wn"], L["Bsb"],
                          L["WtAB"], L["qscb"], L["s8"], L["d0c"], L["e0c"],
                          L["tfx"], L["tfx2"], L["wfix"], L["zc4"], L["cnt4"],
                          L["ov4"], L["oe0"], L["rc4"], L["s0c"], L["s0b"],
                          L["msk"], L["POFF"], L["outg"], L["tfx3"])
    own0 = 63 + grp * 512

    @p.block.vector
    def _(v):
        if k == 0:
            p.wait(v, "smdone", 1)
            p.wait(v, "srql", 128)
        if k >= 2:
            v.wait_ge(p.sem("spm")[0], 4 * (k - 1))
        p.cinc(v.tensor_scalar_mul(mqg[r][:], qT[:, h * QW + own0:
                                                 h * QW + own0 + 512],
                                   negm2[:, h:h + 1]), "smq")

    @p.block.tensor
    def _(t):
        t.wait_ge(p.sem("smq")[0], k + 1)
        if k >= 1:
            t.wait_ge(p.sem("sde")[0], 4 * k)
        for i4 in range(4):
            i = grp * 4 + i4
            wc = slice(128 * i, 128 * i + 191)
            t.matmul(p.aux1[:, POFF[i4]:POFF[i4] + 191],
                     mqg[r][:, ts(i4)], qT[:, h * QW + wc.start:
                                          h * QW + wc.stop],
                     start=True, stop=False)
            t.matmul(p.aux1[:, POFF[i4]:POFF[i4] + 191],
                     selR[:, h * 192:h * 192 + 128], rq[:, wc],
                     start=False, stop=False)
            p.cinc(t.matmul(p.aux1[:, POFF[i4]:POFF[i4] + 191],
                            rq[:, 63 + 128 * i:63 + 128 * i + 128],
                            selR[:, h * 192:h * 192 + 191],
                            start=False, stop=True), "spm")

    @p.block.scalar
    def _(s):
        for i4 in range(4):
            s.wait_ge(p.sem("spm")[0], 4 * k + i4 + 1)
            if k >= 1:
                s.wait_ge(p.sem("sdw")[0], 16 * k)
            p.cinc(s.activation(Dstg[r][:, 192 * i4:192 * i4 + 191],
                                p.aux1[:, POFF[i4]:POFF[i4] + 191],
                                AF.Sqrt, bias=p.eps[:, 0:1]), "sde")

    @p.block.sync
    def _(s):
        p.wait(s, "sde", 4 * (k + 1))
        base = r * 4 * 128 * 192
        p.dinc(s.dma_start(
            out=bass.AP(dr["dband"].tensor, base,
                        [[192, 128], [128 * 192, 4], [1, 192]]),
            in_=Dstg[r][:].rearrange("q (g c) -> q g c", c=192)), "sdw")
        p.wait(s, "sdw", 16 * (k + 1))
        if k >= 2:
            p.wait(s, "sxp", k - 1)
        p.dinc(s.dma_start(
            out=Dwin[r][:].rearrange("q (g j) -> q g j", j=64),
            in_=bass.AP(dr["dband"].tensor, base,
                        [[193, 128], [128 * 192, 4], [1, 64]])), "sdg")
        if grp == 0:
            if k >= 1:
                p.wait(s, "sxp", k)
            p.dinc(s.dma_start(
                out=d0c[:], in_=bass.AP(dr["dband"].tensor, base + 63,
                                        [[192, 128], [1, 1]]),
                allow_slow_non_contiguous=True), "sd0")

    @p.block.vector
    def _(v):
        v.wait_ge(p.sem("sdg")[0], 16 * (k + 1))
        va = bass.AP(Dwin[r][:].tensor, 63, [[4 * 64, 128], [64, 4], [1, 1]])
        p.cinc(v.memset(va, 1e-4), "sslf")
        if grp == 0:
            v.wait_ge(p.sem("sd0")[0], 16 * (h + 1))
            p.ssync(v, v.memset(d0c[0:1, 0:1], 1e-4), "vslf")
            p.ssync(v, v.tensor_copy(e0c[:], d0c[:]), "vslf")
            p.ssync(v, v.tensor_mul(tfx[:], Dwin[r][:, 0:64], clipn[:]),
                    "vslf")
            p.ssync(v, v.tensor_scalar_mul(tfx2[:], clipc[:], e0c[:]), "vslf")
            p.cinc(v.tensor_add(tfx3[:], tfx[:], tfx2[:]), "sfix")

    @p.block.scalar
    def _(s):
        p.wait(s, "sslf", k + 1)
        if k >= 2:
            s.wait_ge(p.sem("ssl")[0], k - 1)
        if grp == 0:
            s.wait_ge(p.sem("sfix")[0], h + 1)
            s.activation(Ewin[r][:, 0:64], tfx3[:], AF.Exp, scale=-1.0)
            s.activation(Ewin[r][:, 64:256], Dwin[r][:, 64:256], AF.Exp,
                         scale=-1.0)
            p.cinc(s.activation(e0c[:], d0c[:], AF.Exp, scale=-1.0), "sxp")
        else:
            p.cinc(s.activation(Ewin[r][:], Dwin[r][:], AF.Exp, scale=-1.0),
                   "sxp")

    @p.block.vector
    def _(v):
        v.wait_ge(p.sem("sxp")[0], k + 1)
        if k >= 2:
            v.wait_ge(p.sem("sws")[0], 16 * (k - 1))
        srcs = [Ewin[r], zapA, zapB, zapA]
        dsts = [zapA, zapB, zapA, zapg[r]]
        for rd in range(4):
            ins_t, out_t = srcs[rd], dsts[rd]
            last = None
            for i4 in range(4):
                sl = ts(i4, 64)
                s8s = s8[:, i4 * 8:i4 * 8 + 8]
                v.max(out=s8s, in_=ins_t[:, sl])
                last = v.match_replace(out=out_t[:, sl], in_to_replace=s8s,
                                       in_values=ins_t[:, sl], imm_value=2.0)
            p.ssync(v, last, "vslf")
        p.ssync(v, v.scalar_tensor_tensor(Wraw[r][:], zapg[r][:], 1.5,
                                          Ewin[r][:], op0=ALU.is_gt,
                                          op1=ALU.mult), "vslf")
        g3 = lambda tt: tt.rearrange("q (g j) -> q g j", j=64)
        p.ssync(v, v.reduce_sum(zc4[:].unsqueeze(2), g3(Wraw[r][:]), AX),
                "vslf")
        if grp == 0:
            v.tensor_scalar(msk[:], zapg[r][:], 1.5, None, op0=ALU.is_gt)
            p.ssync(v, v.reduce_sum(cnt4[:].unsqueeze(2), g3(msk[:]), AX),
                    "vslf")
            p.ssync(v, v.tensor_scalar_add(ov4[:], cnt4[:], -32.0), "vslf")
            p.ssync(v, v.tensor_scalar_max(cnt4[:], ov4[:], 0.0), "vslf")
            p.ssync(v, v.tensor_mul(oe0[:], cnt4[:, 0:1], e0c[:]), "vslf")
            p.ssync(v, v.tensor_sub(ov4[:, 0:1], zc4[:, 0:1], oe0[:]), "vslf")
            p.ssync(v, v.tensor_copy(zc4[:, 0:1], ov4[:, 0:1]), "vslf")
        p.ssync(v, v.tensor_scalar_add(rc4[:], zc4[:], 1e-8), "vslf")
        p.ssync(v, v.reciprocal(zc4[:], rc4[:]), "vslf")
        p.ssync(v, v.tensor_tensor(g3(Wn[r][:]), g3(Wraw[r][:]),
                                   zc4[:].unsqueeze(2).to_broadcast(
                                       [128, 4, 64]), ALU.mult), "vslf")
        p.cinc(v.tensor_copy(s8[0:1, 0:1], zc4[0:1, 0:1]), "swn")
        if grp == 0:
            v.tensor_mul(tfx[:], Wn[r][:, 0:64], clip2[:])
            p.ssync(v, v.reduce_sum(s0c[:], tfx[:], AX), "vslf")
            p.ssync(v, v.tensor_mul(tfx2[:, 0:1], oe0[:], zc4[:, 0:1]),
                    "vslf")
            p.ssync(v, v.tensor_sub(tfx[:, 0:1], s0c[:], tfx2[:, 0:1]),
                    "vslf")
            v.tensor_copy(s0b[:], tfx[:, 0:1])
            p.ssync(v, v.tensor_mul(wfix[:], Wn[r][:, 0:64], clip2n[:]),
                    "vslf")
            p.cinc(v.tensor_copy(Wn[r][:, 0:64], wfix[:]), "swf")
        p.cinc(v.tensor_copy(s8[0:1, 0:1], zc4[0:1, 0:1]), "ssl")

    @p.block.sync
    def _(s):
        p.wait(s, "ssl", k + 1)
        base = r * 4 * 128 * 256
        if k >= 2:
            p.wait(s, "srb", 16 * (k - 1))
        if grp == 0:
            p.wait(s, "swf", h + 1)
        p.dinc(s.dma_start(
            out=bass.AP(dr["wband"].tensor, base + 65,
                        [[257, 128], [128 * 256, 4], [1, 64]]),
            in_=Wn[r][:].rearrange("q (g j) -> q g j", j=64)), "sws")
        p.wait(s, "sws", 16 * (k + 1))
        if grp == 0:
            p.dinc(s.dma_start(
                out=bass.AP(dr["wband"].tensor, base + 128,
                            [[256, 128], [1, 1]]),
                in_=s0b[:], allow_slow_non_contiguous=True), "ss0")
            p.wait(s, "ss0", 16 * (h + 1))
        if k >= 2:
            p.wait(s, "stp", 8 * (k - 1))
        p.dinc(s.dma_start(
            out=Bsb[r][:].rearrange("q (g c) -> q g c", c=256),
            in_=bass.AP(dr["wband"].tensor, base,
                        [[256, 128], [128 * 256, 4], [1, 256]])), "srb")

    @p.block.tensor
    def _(t):
        t.wait_ge(p.sem("srb")[0], 16 * (k + 1))
        if k >= 1:
            t.wait_ge(p.sem("ste")[0], 8 * k)
        for j in range(8):
            p.cinc(t.matmul(p.tpsum8[:, ts(j)], Bsb[r][:, ts(j)], p.ident[:],
                            start=True, stop=True), "stp")
        if k >= 1:
            t.wait_ge(p.sem("sqs")[0], k)
        p.cinc(t.matmul(p.ps_b[:], selC[:, ts(h)],
                        coefb[:, own0:own0 + 512], start=True, stop=True),
               "scfb")

    @p.block.scalar
    def _(s):
        for j in range(8):
            s.wait_ge(p.sem("stp")[0], 8 * k + j + 1)
            p.cinc(s.copy(WtAB[r][:, ts(j)], p.tpsum8[:, ts(j)]), "ste")

    @p.block.vector
    def _(v):
        v.wait_ge(p.sem("scfb")[0], k + 1)
        if k >= 2:
            v.wait_ge(p.sem("sct")[0], 4 * (k - 1))
        p.cinc(v.tensor_mul(qscb[r][:], qT[:, h * QW + own0:
                                           h * QW + own0 + 512],
                            p.ps_b[:]), "sqs")

    @p.block.tensor
    def _(t):
        t.wait_ge(p.sem("ste")[0], 8 * (k + 1))
        t.wait_ge(p.sem("sqs")[0], k + 1)
        if grp == 0 and h == 0:
            t.wait_ge(p.sem("shvr")[0], 16)
        if k >= 1:
            t.wait_ge(p.sem("sot")[0], k)
        for i4 in range(4):
            i = grp * 4 + i4
            vb = 8 * 17 * 128
            t.matmul(p.ps_c[0][:, ts(i4)],
                     bass.AP(vsb[:].tensor, h * 17 * 128 + i * 128,
                             [[vb, 128], [1, 128]]),
                     WtAB[r][:, ts(2 * i4)], start=True, stop=False)
            t.matmul(p.ps_c[0][:, ts(i4)],
                     bass.AP(vsb[:].tensor, h * 17 * 128 + (i + 1) * 128,
                             [[vb, 128], [1, 128]]),
                     WtAB[r][:, ts(2 * i4 + 1)], start=False, stop=False)
            p.cinc(t.matmul(p.ps_c[0][:, ts(i4)], Hm[:, ts(h)],
                            qscb[r][:, ts(i4)], start=False, stop=True),
                   "sct")

    @p.block.vector
    def _(v):
        v.wait_ge(p.sem("sct")[0], 4 * (k + 1))
        if grp >= 1 and h == 0:
            v.wait_ge(p.sem("sdm")[0], 8 * grp)
        p.cinc(v.tensor_mul(outg[:, ts(h, 512)],
                            qT[:, h * QW + own0:h * QW + own0 + 512],
                            p.ps_c[0][:]), "sot")


def _l0a_decode(p, E, L, grp, dr):
    nc = p.nc
    dec_sb, outg = L["dec_sb"], L["outg"]

    @p.block.tensor
    def _(t):
        t.wait_ge(p.sem("sot")[0], 8 * (grp + 1))
        for half in range(2):
            gh = grp * 2 + half
            t.wait_ge(p.sem("sdcl")[0], 16 * DC * (gh + 1))
            for i4 in range(4):
                oidx = gh * 4 + i4
                if oidx >= 1:
                    t.wait_ge(p.sem("sdx")[0], oidx)
                for h in range(8):
                    ie = t.matmul(p.ps_c[1][:], outg[:, h * 512 + i4 * 128:
                                                     h * 512 + i4 * 128 + 128],
                                  dec_sb[:, ts(h, 512)],
                                  start=(h == 0), stop=(h == 7))
                p.cinc(ie, "sdm")

    @p.block.vector
    def _(v):
        for half in range(2):
            gh = grp * 2 + half
            for i4 in range(4):
                ti = grp * 4 + i4
                oidx = gh * 4 + i4
                v.wait_ge(p.sem("sdm")[0], oidx + 1)
                dst = p.xsb[:, ti * D + half * 512:ti * D + half * 512 + 512]
                p.cinc(v.scalar_tensor_tensor(dst, p.ps_c[1][:], 1.0, dst,
                                              op0=ALU.mult, op1=ALU.add),
                       "sdx")




def emit_l1_attn(p, xown_ap, dst_ap, xfull_ap, wq_ap, wk_ap, wv_ap, wo_ap,
                 bq_ap, bk_ap, bv_ap, bo_ap, sin_ap, cos_ap, partial_dram,
                 rs_out_dram, partial_only=False):
    nc = p.nc
    NCH = 8
    S4 = 4096
    QT = [p.sbuf(f"aQT{hp}", [128, S4], BF16) for hp in range(4)]
    KT = [p.sbuf(f"aKT{hp}", [128, S4], BF16) for hp in range(4)]
    vsb = p.sbuf("aV", [128, 8 * 65 * 32], BF16)
    r2m = p.sbuf("ar2", [128, 128], BF16)
    bq_sb = p.sbuf("abq", [128, 4], F32)
    bk_sb = p.sbuf("abk", [128, 4], F32)
    bv_row = p.sbuf("abvr", [1, 512], F32)
    bv_bc = p.sbuf("abvbc", [128, 512], F32)
    bo_row = p.sbuf("abor", [1, D], F32)

    p.push_scope()
    lnTc = [p.sbuf(f"alnT{dc}", [128, 2 * 512], BF16) for dc in range(DC)]
    xstage = p.sbuf("axst", [128, 2 * D], F32)
    wq_sb = p.sbuf("awq", [128, DC * 512], BF16)
    wk_sb = p.sbuf("awk", [128, DC * 512], BF16)
    wv_sb = p.sbuf("awv", [128, DC * 512], BF16)
    sc_sb = p.sbuf("asc", [128, 4 * 512], F32)
    io_a = p.sbuf("aio", [128, 512], I32)
    cs1 = p.sbuf("acs1", [128, 128], F32)
    cs2 = p.sbuf("acs2", [128, 128], F32)
    cs3 = p.sbuf("acs3", [128, 128], F32)
    t1f = p.sbuf("at1", [128, 512], F32)
    t2f = p.sbuf("at2", [128, 512], F32)
    qrawb = p.sbuf("aqraw", [128, 2 * 512], BF16)

    @p.block.gpsimd
    def _(g):
        p.cinc(g.iota(io_a[:, 0:128], pattern=[[1, 128]], base=0,
                      channel_multiplier=-1,
                      allow_small_or_imprecise_dtypes=True), "acg")
        p.wait(g, "acv", 1)
        p.cinc(g.iota(io_a[:, 0:128].rearrange("q (a b c) -> q a b c",
                                               b=2, c=32),
                      pattern=[[0, 2], [1, 2], [0, 32]], base=0,
                      channel_multiplier=0,
                      allow_small_or_imprecise_dtypes=True), "acg")
        for dc in range(DC):
            p.dinc(g.dma_start(out=wq_sb[:, ts(dc, 512)],
                               in_=wq_ap[ts(dc), :]), "awl")
            p.dinc(g.dma_start(out=wk_sb[:, ts(dc, 512)],
                               in_=wk_ap[ts(dc), :]), "awl")
            p.dinc(g.dma_start(out=wv_sb[:, ts(dc, 512)],
                               in_=wv_ap[ts(dc), :]), "awl")
        p.dinc(g.dma_start(out=bq_sb[:], in_=bq_ap), "awl")
        p.dinc(g.dma_start(out=bk_sb[:], in_=bk_ap), "awl")
        p.dinc(g.dma_start(out=bv_row[:], in_=bv_ap.unsqueeze(0)), "awl")
        p.dinc(g.dma_start(out=bo_row[:], in_=bo_ap.unsqueeze(0)), "awl")
        va = bass.AP(vsb[:].tensor, 64,
                     [[8 * 65 * 32, 128], [65, 8 * 32], [1, 1]])
        p.cinc(g.memset(va, 1.0), "avone")

    @p.block.vector
    def _(v):
        p.wait(v, "acg", 1)
        v.tensor_single_scalar(cs1[:], io_a[:, 0:128], -32, ALU.is_equal)
        p.cinc(v.tensor_single_scalar(cs2[:], io_a[:, 0:128], 32,
                                      ALU.is_equal), "acv")
        p.wait(v, "acg", 2)
        p.ssync(v, v.tensor_single_scalar(t1f[:, 0:128], io_a[:, 0:128], 0,
                                          ALU.is_equal), "vslf")
        cm = t1f[:, 0:128]
        v.tensor_mul(t2f[:, 0:128], cs1[:], cm)
        p.ssync(v, v.tensor_mul(t2f[:, 128:256], cs2[:], cm), "vslf")
        v.tensor_sub(cs3[:], cs2[:], t2f[:, 128:256])
        p.cinc(v.tensor_sub(r2m[:], cs3[:], t2f[:, 0:128]), "ar2c")

    @p.block.tensor
    def _(t):
        p.wait(t, "awl")
        p.wait(t, "cst")
        p.cinc(t.matmul(p.aux1[:, 0:512], p.ones_f[:, 0:128], bv_row[:],
                        start=True, stop=True), "abm")

    @p.block.scalar
    def _(s):
        p.wait(s, "abm", 1)
        p.cinc(s.copy(bv_bc[:], p.aux1[:, 0:512]), "abe")

    @p.block.sync
    def _(s):
        for i in range(32):
            if i >= 2:
                p.wait(s, "aln", i - 1)
            p.dinc(s.dma_start(out=xstage[:, ts(i % 2, D)],
                               in_=xfull_ap[ts(i), :]), "axld")

    @p.block.gpsimd
    def _(g):
        for c in range(NCH):
            if c >= 2:
                p.wait(g, "arp", 8 * (c - 1))
            p.dinc(g.dma_start(out=sc_sb[:, (c % 2) * 1024:(c % 2) * 1024 + 512],
                               in_=sin_ap[:, ts(c, 512)]), "ascl")
            p.dinc(g.dma_start(
                out=sc_sb[:, (c % 2) * 1024 + 512:(c % 2) * 1024 + 1024],
                in_=cos_ap[:, ts(c, 512)]), "ascl")

    def pre(eng, i):
        eng.wait_ge(p.sem("axld")[0], 16 * (i + 1))
        if i == 0:
            p.wait(eng, "cst")
            p.wait(eng, "abe", 1)
            for nm, tgt in p.x_ready:
                p.wait(eng, nm, tgt)

    def cb_tensor(t, c):
        lncols = slice((c % 2) * 512, (c % 2) * 512 + 512)
        if c == 0:
            p.wait(t, "ar2c", 1)
        p.wait(t, "ale", 32 * (c + 1))
        for qk in range(2):
            wsb = wq_sb if qk == 0 else wk_sb
            for hp in range(4):
                idx = (c * 2 + qk) * 4 + hp
                if idx >= 1:
                    t.wait_ge(p.sem("aqe")[0], idx)
                for dc in range(DC):
                    ins = t.matmul(p.ps_a[:],
                                   wsb[:, dc * 512 + hp * 128:
                                       dc * 512 + hp * 128 + 128],
                                   lnTc[dc][:, lncols],
                                   start=(dc == 0), stop=(dc == DC - 1))
                p.cinc(ins, "aqm")
                t.wait_ge(p.sem("aqe")[0], idx + 1)
                if idx >= 1:
                    t.wait_ge(p.sem("arp")[0], idx)
                p.cinc(t.matmul(p.ps_b[:], r2m[:],
                                qrawb[:, ts(idx % 2, 512)],
                                start=True, stop=True), "afm")
        for sub in range(4):
            vidx = c * 4 + sub
            if vidx >= 1:
                t.wait_ge(p.sem("ave")[0], vidx)
            for dc in range(DC):
                ins = t.matmul(p.ps_c[1][:],
                               lnTc[dc][:, (c % 2) * 512 + sub * 128:
                                    (c % 2) * 512 + sub * 128 + 128],
                               wv_sb[:, ts(dc, 512)],
                               start=(dc == 0), stop=(dc == DC - 1))
            p.cinc(ins, "avm")

    def cb_vector(v, c):
        v.wait_ge(p.sem("ascl")[0], 32 * (c + 1))
        sins = sc_sb[:, (c % 2) * 1024:(c % 2) * 1024 + 512]
        coss = sc_sb[:, (c % 2) * 1024 + 512:(c % 2) * 1024 + 1024]
        for qk in range(2):
            dstT = QT if qk == 0 else KT
            bsb = bq_sb if qk == 0 else bk_sb
            for hp in range(4):
                idx = (c * 2 + qk) * 4 + hp
                v.wait_ge(p.sem("aqm")[0], idx + 1)
                if idx >= 2:
                    v.wait_ge(p.sem("afm")[0], idx - 1)
                p.cinc(v.tensor_scalar(qrawb[:, ts(idx % 2, 512)],
                                       p.ps_a[:], bsb[:, hp:hp + 1], None,
                                       op0=ALU.add), "aqe")
                v.wait_ge(p.sem("afm")[0], idx + 1)
                v.tensor_mul(t1f[:], qrawb[:, ts(idx % 2, 512)], coss)
                v.scalar_tensor_tensor(t2f[:], p.ps_b[:], 1.0, sins,
                                       op0=ALU.mult, op1=ALU.mult)
                p.cinc(v.tensor_add(dstT[hp][:, ts(c, 512)], t1f[:],
                                    t2f[:]), "arp")
        for sub in range(4):
            vidx = c * 4 + sub
            v.wait_ge(p.sem("avm")[0], vidx + 1)
            v.wait_ge(p.sem("avone")[0], 1)
            tile = c * 4 + sub
            dst = bass.AP(vsb[:].tensor, tile * 65,
                          [[8 * 65 * 32, 128], [65 * 32, 8], [1, 64]])
            p.cinc(v.scalar_tensor_tensor(
                dst, p.ps_c[1][:].rearrange("q (h e) -> q h e", e=64),
                1.0, bv_bc[:].rearrange("q (h e) -> q h e", e=64),
                op0=ALU.mult, op1=ALU.add), "ave")

    emit_ln_transpose(p, lnTc, 32, lambda i: xstage[:, ts(i % 2, D)],
                      "ln", "al", pre_wait=pre,
                      dst_col_fn=lambda i: ((i // 4) % 2) * 512 + (i % 4) * 128,
                      cb_every=4, cb_tensor=cb_tensor, cb_vector=cb_vector)

    p.pop_scope()
    p.push_scope()
    masks = p.sbuf("amask", [128, 4 * 512], BF16)
    io_b = p.sbuf("aiob", [128, 512], I32)
    wo_sb = p.sbuf("awo", [64, 8 * 1024], BF16)
    bo_bc = p.sbuf("abobc", [128, D], F32)
    PtA = [p.sbuf(f"aPtA{j}", [128, 512], BF16) for j in range(2)]
    PtB = p.sbuf("aPtB", [128, 512], BF16)
    reciprow = p.sbuf("arr", [1, 512], F32)
    recipb = p.sbuf("arrb", [1, 512], BF16)
    zraw = p.sbuf("azraw", [64, 512], BF16)
    zdiv = p.sbuf("azdiv", [64, 8 * 512], BF16)
    stage = p.sbuf("astg", [128, D], F32)
    rstage = p.sbuf("arstg", [128, 2 * D], F32)
    xrs = p.sbuf("axrs", [128, 2 * D], F32)
    ostage = p.sbuf("aost", [128, 2 * D], F32)

    QKV_DONE = [("arp", 64), ("ave", 32), ("aqe", 64)]

    @p.block.gpsimd
    def _(g):
        for nm, tgt in QKV_DONE:
            p.wait(g, nm, tgt)
        for hh in range(8):
            p.dinc(g.dma_start(out=wo_sb[:, ts(hh, 1024)],
                               in_=wo_ap[hh * 64:hh * 64 + 64, :]), "awo")
        for var in range(4):
            if var > 0:
                p.wait(g, "amk", var)
            p.cinc(g.iota(io_b[:], pattern=[[1, 512]], base=-128 * var,
                          channel_multiplier=-1,
                          allow_small_or_imprecise_dtypes=True), "amg")

    @p.block.vector
    def _(v):
        for var in range(4):
            p.wait(v, "amg", var + 1)
            p.cinc(v.tensor_single_scalar(masks[:, ts(var, 512)], io_b[:],
                                          0, ALU.is_ge), "amk")

    @p.block.tensor
    def _(t):
        for half in range(2):
            ins = t.matmul(p.aux1[:, ts(half, 512)], p.ones_f[:, 0:128],
                           bo_row[:, ts(half, 512)], start=True, stop=True)
        p.cinc(ins, "abm")

    @p.block.scalar
    def _(s):
        for nm, tgt in QKV_DONE:
            p.wait(s, nm, tgt)
        p.wait(s, "abm", 2)
        for half in range(2):
            p.cinc(s.copy(bo_bc[:, ts(half, 512)], p.aux1[:, ts(half, 512)]),
                   "abe")

    p._akidx_t = 0
    p._adiag_t = 0

    @p.block.tensor
    def _(t):
        t.wait_ge(p.sem("abe")[0], 3)
        for c in range(NCH):
            qc = slice(c * 512, c * 512 + 512)
            ntk = 4 * c + 4
            for hh in range(8):
                hp, sb = hh // 2, hh % 2
                rows = slice(64 * sb, 64 * sb + 64)
                t.wait_ge(p.sem("arp")[0], 8 * (c + 1))
                for tk in range(ntk):
                    kidx = p._akidx_t
                    p._akidx_t += 1
                    if tk >= 4 * c:
                        p._adiag_t += 1
                    if kidx >= 2:
                        t.wait_ge(p.sem("aex")[0], kidx - 1)
                    ins = t.matmul(p.ps_a[:] if kidx % 2 == 0 else p.ps_b[:],
                                   KT[hp][rows, tk * 128:tk * 128 + 128],
                                   QT[hp][rows, qc], start=True, stop=True)
                    p.cinc(ins, "asm")
                    if tk >= 4 * c:
                        t.wait_ge(p.sem("apm")[0], p._adiag_t)
                    else:
                        t.wait_ge(p.sem("aex")[0], kidx + 1)
                    if tk == 0:
                        t.wait_ge(p.sem("azd")[0], c * 8 + hh)
                    vtile = bass.AP(vsb[:].tensor, hh * 65 * 32 + tk * 65,
                                    [[8 * 65 * 32, 128], [1, 65]])
                    pt = PtB if tk >= 4 * c else PtA[kidx % 2]
                    p.cinc(t.matmul(p.aux1[0:65, 0:512], vtile, pt[:],
                                    start=(tk == 0), stop=(tk == ntk - 1)),
                           "apv")
                t.wait_ge(p.sem("arcb")[0], c * 8 + hh + 1)
                if c * 8 + hh >= 1:
                    t.wait_ge(p.sem("azd")[0], c * 8 + hh)
                p.cinc(t.matmul(p.ps_c[0][0:64, 0:512], p.ones_r[:, 0:64],
                                recipb[:], start=True, stop=True), "albc")
            t.wait_ge(p.sem("azd")[0], (c + 1) * 8)
            if c == 0:
                t.wait_ge(p.sem("awo")[0], 128)
            for sub in range(4):
                for half in range(2):
                    oidx = (c * 4 + sub) * 2 + half
                    if oidx >= 2:
                        t.wait_ge(p.sem("aoe")[0], oidx - 1)
                    for hh in range(8):
                        hp, sb = hh // 2, hh % 2
                        ins = t.matmul(
                            p.ps_c[half][:],
                            zdiv[0:64, hh * 512 + sub * 128:
                                 hh * 512 + sub * 128 + 128],
                            wo_sb[0:64, hh * 1024 + half * 512:
                                  hh * 1024 + half * 512 + 512],
                            start=(hh == 0), stop=(hh == 7))
                    p.cinc(ins, "aom")

    @p.block.scalar
    def _(s):
        for nm, tgt in QKV_DONE:
            p.wait(s, nm, tgt)
        kidx = 0
        for c in range(NCH):
            ntk = 4 * c + 4
            for hh in range(8):
                for tk in range(ntk):
                    s.wait_ge(p.sem("asm")[0], kidx + 1)
                    if kidx >= 2:
                        s.wait_ge(p.sem("apv")[0], kidx - 1)
                    p.cinc(s.activation(PtA[kidx % 2][:],
                                        p.ps_a[:] if kidx % 2 == 0
                                        else p.ps_b[:],
                                        AF.Exp, scale=0.125), "aex")
                    kidx += 1
                s.wait_ge(p.sem("apv")[0], kidx)
                s.wait_ge(p.sem("arc")[0], c * 8 + hh + 1)
                p.cinc(s.copy(recipb[:], reciprow[:]), "arcb")
                p.cinc(s.copy(zraw[:], p.aux1[0:64, 0:512]), "azr")
            for sub in range(4):
                for half in range(2):
                    oidx = (c * 4 + sub) * 2 + half
                    s.wait_ge(p.sem("aom")[0], oidx + 1)
                    if oidx >= 2:
                        s.wait_ge(p.sem("aps")[0], 16 * (oidx - 1))
                    p.cinc(s.copy(stage[:, ts(half, 512)],
                                  p.ps_c[half][:]), "aoe")

    @p.block.vector
    def _(v):
        for nm, tgt in QKV_DONE:
            p.wait(v, nm, tgt)
        kidx = 0
        for c in range(NCH):
            ntk = 4 * c + 4
            for hh in range(8):
                for tk in range(ntk):
                    if tk >= 4 * c:
                        v.wait_ge(p.sem("aex")[0], kidx + 1)
                        if kidx >= 1:
                            v.wait_ge(p.sem("apv")[0], kidx)
                        p.cinc(v.tensor_mul(PtB[:], PtA[kidx % 2][:],
                                            masks[:, ts(tk - 4 * c, 512)]),
                               "apm")
                    kidx += 1
                v.wait_ge(p.sem("apv")[0], kidx)
                p.ssync(v, v.reciprocal(reciprow[:], p.aux1[64:65, 0:512]),
                        "vslf")
                p.cinc(v.tensor_copy(recipb[:], recipb[:]) if False else
                       v.tensor_scalar_mul(p.rrcol[:], p.rrcol[:], 1.0)
                       if False else
                       v.tensor_copy(p.scol[0:1, 0:1], reciprow[0:1, 0:1]), "arc")
                v.wait_ge(p.sem("azr")[0], c * 8 + hh + 1)
                v.wait_ge(p.sem("albc")[0], c * 8 + hh + 1)
                p.cinc(v.tensor_mul(zdiv[0:64, ts(hh, 512)], zraw[:],
                                    p.ps_c[0][0:64, 0:512]), "azd")

    @p.block.sync
    def _(s):
        for c in range(NCH):
            for sub in range(4):
                for half in range(2):
                    oidx = (c * 4 + sub) * 2 + half
                    p.wait(s, "aoe", oidx + 1)
                    p.dinc(s.dma_start(
                        out=partial_dram[c * 512 + sub * 128:
                                         c * 512 + sub * 128 + 128,
                                         half * 512:half * 512 + 512],
                        in_=stage[:, ts(half, 512)]), "aps")
        if partial_only:
            p.wait(s, "aps")

    if partial_only:
        p.pop_scope()
        p.x_ready = []
        return

    bar_out = nc.dram_tensor("abarout", [2, D], F32).ap()

    @p.block.gpsimd
    def _(g):
        p.wait(g, "aps")
        p.cinc(g.collective_compute(
            "ReduceScatter", ALU.add,
            replica_groups=[[0, 1], [2, 3], [4, 5], [6, 7]],
            ins=[partial_dram.opt()], outs=[rs_out_dram.opt()]), "acc")
        p.cinc(g.collective_compute(
            "AllReduce", ALU.add,
            replica_groups=[[0, 1], [2, 3], [4, 5], [6, 7]],
            ins=[partial_dram[0:2, :].opt()], outs=[bar_out.opt()]), "acc")

    dly = nc.dram_tensor("adly", [T_OWN, D], F32).ap()

    @p.block.sync
    def _(s):
        for i in range(NT):
            p.dinc(s.dma_start(out=xrs[:, ts(i % 2, D)],
                               in_=xown_ap[ts(i), :])
                   if False else
                   s.dma_start(out=dly[ts(i), :], in_=xown_ap[ts(i), :]),
                   "axw")
        p.wait(s, "acc", 2)
        for r in range(256):
            src = dly[ts(r % 4, 32), 0:D]
            dst_d = dly[ts(12 + (r % 4), 32), 0:D]
            p.dinc(s.dma_start(out=dst_d, in_=src), "adw")
            p.wait(s, "adw", 16 * (r + 1))
        for i in range(NT):
            if i >= 2:
                p.wait(s, "aad", i - 1)
            p.dinc(s.dma_start(out=rstage[:, ts(i % 2, D)],
                               in_=rs_out_dram[ts(i), :]), "ars")
            p.dinc(s.dma_start(out=xrs[:, ts(i % 2, D)],
                               in_=xown_ap[ts(i), :]), "ars")

    @p.block.vector
    def _(v):
        p.wait(v, "abe", 3)
        for i in range(NT):
            v.wait_ge(p.sem("ars")[0], 32 * (i + 1))
            if i >= 2:
                v.wait_ge(p.sem("aout")[0], 16 * (i - 1))
            osl = ostage[:, ts(i % 2, D)]
            v.scalar_tensor_tensor(osl, rstage[:, ts(i % 2, D)], 1.0,
                                   xrs[:, ts(i % 2, D)], op0=ALU.mult,
                                   op1=ALU.add)
            p.cinc(v.tensor_add(osl, osl, bo_bc[:]) if False else
                   v.scalar_tensor_tensor(osl, bo_bc[:], 1.0, osl,
                                          op0=ALU.mult, op1=ALU.add), "aad")

    @p.block.gpsimd
    def _(g):
        for i in range(NT):
            p.wait(g, "aad", i + 1)
            p.dinc(g.dma_start(out=dst_ap[ts(i), :],
                               in_=ostage[:, ts(i % 2, D)]), "aout")
        p.wait(g, "aout")
    p.pop_scope()
    p.x_ready = [("aout", 16 * NT)]


def emit_l1_sum(p, pa_ap, pb_ap, x_ap, bo_ap, out_ap):
    nc = p.nc
    pa = p.sbuf("spa", [128, 2 * D], F32)
    pb = p.sbuf("spb", [128, 2 * D], F32)
    xr = p.sbuf("spx", [128, 2 * D], F32)
    ost = p.sbuf("spo", [128, 2 * D], F32)
    bo_row = p.sbuf("spbo", [1, D], F32)
    bo_bc = p.sbuf("spbobc", [128, D], F32)

    @p.block.gpsimd
    def _(g):
        p.dinc(g.dma_start(out=bo_row[:], in_=bo_ap.unsqueeze(0)), "pbo")

    @p.block.tensor
    def _(t):
        p.wait(t, "cst")
        p.wait(t, "pbo", 16)
        for half in range(2):
            ins = t.matmul(p.aux1[:, ts(half, 512)], p.ones_f[:, 0:128],
                           bo_row[:, ts(half, 512)], start=True, stop=True)
        p.cinc(ins, "pbm")

    @p.block.scalar
    def _(s):
        p.wait(s, "pbm", 1)
        for half in range(2):
            p.cinc(s.copy(bo_bc[:, ts(half, 512)], p.aux1[:, ts(half, 512)]),
                   "pbe")

    @p.block.sync
    def _(s):
        for i in range(NT):
            if i >= 2:
                p.wait(s, "pad", i - 1)
            p.dinc(s.dma_start(out=pa[:, ts(i % 2, D)], in_=pa_ap[ts(i), :]),
                   "pld")
            p.dinc(s.dma_start(out=pb[:, ts(i % 2, D)], in_=pb_ap[ts(i), :]),
                   "pld")
            p.dinc(s.dma_start(out=xr[:, ts(i % 2, D)], in_=x_ap[ts(i), :]),
                   "pld")

    @p.block.vector
    def _(v):
        p.wait(v, "pbe", 2)
        for i in range(NT):
            v.wait_ge(p.sem("pld")[0], 48 * (i + 1))
            if i >= 2:
                v.wait_ge(p.sem("pout")[0], 16 * (i - 1))
            o = ost[:, ts(i % 2, D)]
            v.tensor_add(o, pa[:, ts(i % 2, D)], pb[:, ts(i % 2, D)])
            v.tensor_add(o, xr[:, ts(i % 2, D)], o)
            p.cinc(v.scalar_tensor_tensor(o, bo_bc[:], 1.0, o, op0=ALU.mult,
                                          op1=ALU.add), "pad")

    @p.block.gpsimd
    def _(g):
        for i in range(NT):
            p.wait(g, "pad", i + 1)
            p.dinc(g.dma_start(out=out_ap[ts(i), :], in_=ost[:, ts(i % 2, D)]),
                   "pout")
        p.wait(g, "pout")
    p.x_ready = []




def emit_l1_ffn(p, wg_ap, wu_ap, wd_ap):
    nc = p.nc
    HC = 24
    TCN = 4
    SC = 1.0 / 127.0
    xnT = [p.sbuf(f"f1xnT{dc}", [128, T_OWN], BF16) for dc in range(DC)]

    def pre(v, i):
        if i == 0:
            p.wait(v, "cst")
            for nm, tgt in p.x_ready:
                p.wait(v, nm, tgt)

    emit_ln_transpose(p, xnT, NT, lambda i: p.xsb[:, ts(i, D)], "ln", "f1",
                      pre_wait=pre)

    wd_sb = p.sbuf("f1wd", [128, HC * D], BF16)
    wgu = [p.sbuf(f"f1wgu{j}", [128, 2 * D], BF16) for j in range(2)]
    h_all = p.sbuf("f1h", [128, HC * 512], BF16)
    s_sb = p.sbuf("f1s", [128, 512], BF16)
    ps_s, ps_u, ps_d = p.ps_a, p.ps_b, p.ps_c
    p.dbg_hall = h_all[:]

    def mm_mark(it):
        return it + 1

    def dn_mark(tc, sub, half):
        return (tc * 4 + sub) * 2 + half + 1

    @p.block.gpsimd
    def _(g):
        for hc in range(HC):
            p.dinc(g.dma_start(out=wd_sb[:, ts(hc, D)], in_=wd_ap[ts(hc), :]),
                   "f1wd")
        for tc in range(TCN):
            for hc in range(HC):
                it = tc * HC + hc
                buf = wgu[it % 2]
                if it >= 2:
                    g.wait_ge(p.sem("f1mm")[0], mm_mark(it - 2))
                src_g = bass.AP(wg_ap.tensor, hc * 128,
                                [[3072, 128], [128 * 3072, 8], [1, 128]])
                src_u = bass.AP(wu_ap.tensor, hc * 128,
                                [[3072, 128], [128 * 3072, 8], [1, 128]])
                p.dinc(g.dma_start(out=buf[:, 0:D].rearrange(
                    "q (c m) -> q c m", m=128), in_=src_g), "f1w")
                p.dinc(g.dma_start(out=buf[:, D:2 * D].rearrange(
                    "q (c m) -> q c m", m=128), in_=src_u), "f1w")

    @p.block.tensor
    def _(t):
        for tc in range(TCN):
            tcs = slice(tc * 512, tc * 512 + 512)
            for hc in range(HC):
                it = tc * HC + hc
                buf = wgu[it % 2]
                t.wait_ge(p.sem("f1w")[0], (it + 1) * 32)
                if it == 0:
                    p.wait(t, "f1e", NT * DC)
                if it >= 1:
                    t.wait_ge(p.sem("f1hh")[0], 2 * it)
                for dc in range(DC):
                    t.matmul(ps_s[:], buf[:, ts(dc)], xnT[dc][:, tcs],
                             start=(dc == 0), stop=(dc == DC - 1))
                for dc in range(DC):
                    ins = t.matmul(ps_u[:], buf[:, D + dc * 128:D + dc * 128 + 128],
                                   xnT[dc][:, tcs], start=(dc == 0),
                                   stop=(dc == DC - 1))
                p.cinc(ins, "f1mm")
            t.wait_ge(p.sem("f1hh")[0], 2 * (tc + 1) * HC)
            if tc == 0:
                t.wait_ge(p.sem("f1wd")[0], 16 * HC)
            for sub in range(4):
                for half in range(2):
                    if tc * 4 + sub >= 1:
                        t.wait_ge(p.sem("f1d")[0], dn_mark(tc, sub, half) - 2)
                    pd = ps_d[half]
                    for hc in range(HC):
                        ins = t.matmul(pd[:],
                                       h_all[:, hc * 512 + sub * 128:
                                             hc * 512 + sub * 128 + 128],
                                       wd_sb[:, hc * D + half * 512:
                                             hc * D + half * 512 + 512],
                                       start=(hc == 0), stop=(hc == HC - 1))
                    p.cinc(ins, "f1dm")

    @p.block.scalar
    def _(s):
        for tc in range(TCN):
            for hc in range(HC):
                it = tc * HC + hc
                s.wait_ge(p.sem("f1mm")[0], mm_mark(it))
                if it >= 1:
                    s.wait_ge(p.sem("f1hh")[0], 2 * it)
                p.cinc(s.activation(s_sb[:], ps_s[:], AF.Silu, scale=SC), "f1hh")

    @p.block.vector
    def _(v):
        for tc in range(TCN):
            for hc in range(HC):
                it = tc * HC + hc
                v.wait_ge(p.sem("f1hh")[0], 2 * it + 1)
                if tc >= 1 and hc == 0:
                    v.wait_ge(p.sem("f1dm")[0], dn_mark(tc - 1, 3, 1))
                p.cinc(v.scalar_tensor_tensor(h_all[:, ts(hc, 512)], ps_u[:],
                                              SC, s_sb[:], op0=ALU.mult,
                                              op1=ALU.mult), "f1hh")
            for sub in range(4):
                ti = (tc * 512 + sub * 128) // 128
                for half in range(2):
                    v.wait_ge(p.sem("f1dm")[0], dn_mark(tc, sub, half))
                    dst = p.xsb[:, ti * D + half * 512: ti * D + half * 512 + 512]
                    p.cinc(v.scalar_tensor_tensor(dst, ps_d[half][:], SC, dst,
                                                  op0=ALU.mult, op1=ALU.add),
                           "f1d")
    p.x_ready = [("f1d", 8 * TCN)]




def build_program(blocks, inputs_spec, with_halo=False, debug_dump=None):
    p = Prog()
    nc = p.nc
    dram = {}
    for name, shape, dt in inputs_spec:
        dram[name] = nc.dram_tensor(name, list(shape), dt,
                                    kind="ExternalInput").ap()
    out = nc.dram_tensor("out", [T_OWN, D], F32, kind="ExternalOutput").ap()

    use_xsb = bool(set(blocks) & {"l0a", "l0f", "l0fo", "l1f"})
    with nc.Block() as block:
        p.block = block
        emit_consts(p)
        if use_xsb:
            emit_load_x(p, dram["x_own"], None)
        alloc_ln_scratch(p)
        p.ps_a = p.psum("ps_a", [128, 512], F32)
        p.ps_b = p.psum("ps_b", [128, 512], F32)
        p.ps_c = [p.psum(f"ps_c{j}", [128, 512], F32) for j in range(2)]
        p.aux1 = p.psum("aux1", [128, 1024], F32)
        p.x_ready = [("xld", None)] if use_xsb else []
        if "l0a" in blocks:
            dr = {
                "dband": nc.dram_tensor("sdband", [2 * 4 * 128 * 192], F32).ap(),
                "wband": nc.dram_tensor("swband", [2 * 4 * 128 * 256],
                                        BF16).ap(),
                "rq": nc.dram_tensor("srqd", [2, QW], BF16).ap(),
                "hv": nc.dram_tensor("shvd", [64, D], BF16).ap(),
                "hid": nc.dram_tensor("shidd", [512], BF16).ap(),
                "mod": nc.dram_tensor("smodd", [1024], BF16).ap(),
            }
            emit_l0_attn(p, dram, dr)
        if "l0f" in blocks or "l0fo" in blocks:
            emit_l0_ffn(p, dram["nw"], dram["rw"], dram["w1"], dram["w3"],
                        dram["w2"],
                        x_ap=dram["x_own"] if "l0fo" in blocks else None)
        if "l1a" in blocks:
            partial = nc.dram_tensor("apartial", [4096, D], F32).ap()
            rs_out = nc.dram_tensor("arsout", [T_OWN, D], F32).ap()
            emit_l1_attn(p, dram["x_own"], out, dram["xfull"], dram["wq"],
                         dram["wk"], dram["wv"], dram["wo"], dram["bq"],
                         dram["bk"], dram["bv"], dram["bo"], dram["sin"],
                         dram["cos"], partial, rs_out)
        if "l1ap" in blocks:
            pout = nc.dram_tensor("pout", [4096, D], F32,
                                  kind="ExternalOutput").ap()
            emit_l1_attn(p, dram["x_own"], out, dram["xfull"], dram["wq"],
                         dram["wk"], dram["wv"], dram["wo"], dram["bq"],
                         dram["bk"], dram["bv"], dram["bo"], dram["sin"],
                         dram["cos"], pout, None, partial_only=True)
        if "l1s" in blocks:
            emit_l1_sum(p, dram["pa"], dram["pb"], dram["x_own"], dram["bo"],
                        out)
        if "l1f" in blocks:
            emit_l1_ffn(p, dram["wgT"], dram["wuT"], dram["wdT"])
        done = list(p.x_ready) if use_xsb else []
        if debug_dump:
            dbg_outs = {}
            for name in debug_dump:
                ap = getattr(p, "dbg_" + name)
                sh = [ap.shape[0], int(np.prod(ap.shape[1:]))]
                dbg_outs[name] = nc.dram_tensor(
                    "dbg_" + name, sh, ap.dtype, kind="ExternalOutput").ap()

            @p.block.sync
            def _(s):
                for nm, tg in done:
                    p.wait(s, nm, tg)
                for name in debug_dump:
                    p.dinc(s.dma_start(out=dbg_outs[name],
                                       in_=getattr(p, "dbg_" + name)), "dbg")
                p.wait(s, "dbg")
        if use_xsb:
            emit_store_x(p, out, done)
    p.es.close()
    return nc


def run_l1f(x, wg, wu, wd):
    spec = [("x_own", (T_OWN, D), F32),
            ("wgT", (D, 3072), I32),
            ("wuT", (D, 3072), I32),
            ("wdT", (3072, D), I32)]
    import os
    dbg = os.environ.get("DBG", "")
    nc = build_program(["l1f"], spec,
                       debug_dump=dbg.split(",") if dbg else None)
    wgT = np.ascontiguousarray(wg.T)
    wuT = np.ascontiguousarray(wu.T)
    wdT = np.ascontiguousarray(wd.T)
    in_maps = [{"x_own": np.ascontiguousarray(x[c * T_OWN:(c + 1) * T_OWN]),
                "wgT": wgT, "wuT": wuT, "wdT": wdT} for c in range(8)]
    res = run_bass_kernel_spmd(nc, in_maps, core_ids=list(range(8)))
    out = np.concatenate([res.results[c]["out"] for c in range(8)], 0)
    if dbg:
        return out, res.results[0]
    return out


def run_l0f(x, nw, rw, w1, w3, w2):
    spec = [("x_own", (T_OWN, D), F32),
            ("nw", (D,), F32),
            ("rw", (D, 8), F32),
            ("w1", (8, D, 2048), F32),
            ("w3", (8, D, 2048), F32),
            ("w2", (8, 2048, D), F32)]
    import os
    dbg = os.environ.get("DBG", "")
    nc = build_program(["l0f"], spec,
                       debug_dump=dbg.split(",") if dbg else None)
    in_maps = [{"x_own": np.ascontiguousarray(x[c * T_OWN:(c + 1) * T_OWN]),
                "nw": nw, "rw": rw, "w1": w1, "w3": w3, "w2": w2}
               for c in range(8)]
    res = run_bass_kernel_spmd(nc, in_maps, core_ids=list(range(8)))
    out = np.concatenate([res.results[c]["out"] for c in range(8)], 0)
    if dbg:
        return out, res.results[0]
    return out


def run_blocks(blocks, x, extra_spec, extra_maps):
    spec = [("x_own", (T_OWN, D), F32)] + extra_spec
    import os
    dbg = os.environ.get("DBG", "")
    nc = build_program(blocks, spec, debug_dump=dbg.split(",") if dbg else None)
    in_maps = [dict({"x_own": np.ascontiguousarray(x[c * T_OWN:(c + 1) * T_OWN])},
                    **extra_maps) for c in range(8)]
    res = run_bass_kernel_spmd(nc, in_maps, core_ids=list(range(8)))
    out = np.concatenate([res.results[c]["out"] for c in range(8)], 0)
    if dbg:
        return out, res.results[0]
    return out


L0A_SPEC = [("x_halo", (63, D), F32), ("x_pair", (T_OWN, D), F32),
            ("encq", (D, D), F32),
            ("encv", (D, D), F32), ("mnw1", (D, 512), F32),
            ("b1c", (128, 4), F32), ("mnw2", (512, D), F32),
            ("b2c", (128, 8), F32), ("basem", (128, 8), F32),
            ("dec", (D, D), F32), ("hebUT", (32, D), F32),
            ("hebV", (32, D), F32), ("clipc", (128, 64), F32),
            ("clipn", (128, 64), F32), ("clip2", (128, 64), F32),
            ("clip2n", (128, 64), F32)]


def l0a_core_maps(x, encq, encv, mnw1, mnb1, mnw2, mnb2, basem, dec, hebU,
                  hebV):
    encqT = np.ascontiguousarray(
        np.transpose(encq, (1, 0, 2)).reshape(D, D))
    encvT = np.ascontiguousarray(
        np.transpose(encv, (1, 0, 2)).reshape(D, D))
    hebUT = np.ascontiguousarray(hebU.transpose(2, 0, 1).reshape(32, D))
    hebVT = np.ascontiguousarray(hebV.transpose(1, 0, 2).reshape(32, D))
    pj = np.arange(128)[:, None] + np.arange(64)[None, :]
    maps = []
    for c in range(8):
        b, hf = c // 2, c % 2
        if hf == 0:
            halo = np.zeros((63, D), np.float32)
            cc = (pj < 63).astype(np.float32)
            c2 = (pj <= 63).astype(np.float32)
        else:
            halo = np.ascontiguousarray(
                x[b * 4096 + 2048 - 63:b * 4096 + 2048])
            cc = np.zeros((128, 64), np.float32)
            c2 = (pj == 63).astype(np.float32)
        cp = b * 2 + (1 - hf)
        maps.append({
            "x_own": np.ascontiguousarray(x[c * T_OWN:(c + 1) * T_OWN]),
            "x_pair": np.ascontiguousarray(x[cp * T_OWN:(cp + 1) * T_OWN]),
            "x_halo": halo, "encq": encqT, "encv": encvT,
            "mnw1": mnw1, "b1c": np.ascontiguousarray(
                mnb1.reshape(4, 128).T),
            "mnw2": mnw2, "b2c": np.ascontiguousarray(
                mnb2.reshape(8, 128).T),
            "basem": np.ascontiguousarray(basem.T), "dec": dec,
            "hebUT": hebUT, "hebV": hebVT,
            "clipc": cc, "clipn": 1.0 - cc,
            "clip2": c2, "clip2n": 1.0 - c2,
        })
    return maps


def run_l0a(x, encq, encv, mnw1, mnb1, mnw2, mnb2, basem, dec, hebU, hebV):
    spec = [("x_own", (T_OWN, D), F32)] + L0A_SPEC
    import os
    dbg = os.environ.get("DBG", "")
    nc = build_program(["l0a"], spec, debug_dump=dbg.split(",") if dbg else None)
    in_maps = l0a_core_maps(x, encq, encv, mnw1, mnb1, mnw2, mnb2, basem,
                            dec, hebU, hebV)
    res = run_bass_kernel_spmd(nc, in_maps, core_ids=list(range(8)))
    out = np.concatenate([res.results[c]["out"] for c in range(8)], 0)
    if dbg:
        return out, res.results
    return out


def rope_tables():
    pos = np.arange(4096, dtype=np.float64)
    dim = np.arange(32, dtype=np.float64)
    freq = 10000.0 ** (dim / 32.0)
    freq = np.concatenate([freq, freq])
    ang = pos[:, None] / freq[None, :]
    sin = np.sin(ang).T.astype(np.float32)
    cos = np.cos(ang).T.astype(np.float32)
    return (np.concatenate([sin, sin], 0).copy(),
            np.concatenate([cos, cos], 0).copy())


def l1a_core_maps(x, WQ, bQ, WK, bK, WV, bV, WO, bO):
    sin, cos = rope_tables()
    maps = []
    for c in range(8):
        b, hp = c // 2, c % 2
        hsl = slice(8 * hp, 8 * hp + 8)
        xb = np.ascontiguousarray(x[b * 4096:(b + 1) * 4096])
        maps.append({
            "x_own": np.ascontiguousarray(x[c * T_OWN:(c + 1) * T_OWN]),
            "xfull": xb,
            "wq": np.ascontiguousarray(
                np.transpose(WQ[hsl], (1, 0, 2)).reshape(D, 512)),
            "wk": np.ascontiguousarray(
                np.transpose(WK[hsl], (1, 0, 2)).reshape(D, 512)),
            "wv": np.ascontiguousarray(
                np.transpose(WV[hsl], (1, 0, 2)).reshape(D, 512)),
            "wo": np.ascontiguousarray(WO[hsl].reshape(512, D)),
            "bq": np.ascontiguousarray(bQ[hsl].reshape(4, 128).T),
            "bk": np.ascontiguousarray(bK[hsl].reshape(4, 128).T),
            "bv": np.ascontiguousarray(bV[hsl].reshape(512)),
            "bo": bO if hp == 99 else bO,
            "sin": sin, "cos": cos,
        })
    return maps


L1A_SPEC = [("xfull", (4096, D), F32), ("wq", (D, 512), F32),
            ("wk", (D, 512), F32), ("wv", (D, 512), F32),
            ("wo", (512, D), F32), ("bq", (128, 4), F32),
            ("bk", (128, 4), F32), ("bv", (512,), F32), ("bo", (D,), F32),
            ("sin", (128, 4096), F32), ("cos", (128, 4096), F32)]


def run_l1a(x, WQ, bQ, WK, bK, WV, bV, WO, bO):
    spec = [("x_own", (T_OWN, D), F32)] + L1A_SPEC
    nc = build_program(["l1ap"], spec)
    in_maps = l1a_core_maps(x, WQ, bQ, WK, bK, WV, bV, WO, bO)
    res = run_bass_kernel_spmd(nc, in_maps, core_ids=list(range(8)))
    pouts = [np.asarray(res.results[c]["pout"]) for c in range(8)]
    specB = [("x_own", (T_OWN, D), F32), ("pa", (T_OWN, D), F32),
             ("pb", (T_OWN, D), F32), ("bo", (D,), F32)]
    ncB = build_program(["l1s"], specB)
    mapsB = []
    for c in range(8):
        hf = c % 2
        rows = slice(hf * T_OWN, hf * T_OWN + T_OWN)
        mapsB.append({
            "x_own": np.ascontiguousarray(x[c * T_OWN:(c + 1) * T_OWN]),
            "pa": np.ascontiguousarray(pouts[c][rows]),
            "pb": np.ascontiguousarray(pouts[c ^ 1][rows]),
            "bo": bO,
        })
    resB = run_bass_kernel_spmd(ncB, mapsB, core_ids=list(range(8)))
    return np.concatenate([resB.results[c]["out"] for c in range(8)], 0)


def run_l0f_outer(x, nw, rw, w1, w3, w2):
    spec = [("x_own", (T_OWN, D), F32),
            ("nw", (D,), F32),
            ("rw", (D, 8), F32),
            ("w1", (8, D, 2048), F32),
            ("w3", (8, D, 2048), F32),
            ("w2", (8, 2048, D), F32)]
    nc = build_program(["l0fo"], spec)
    in_maps = [{"x_own": np.ascontiguousarray(x[c * T_OWN:(c + 1) * T_OWN]),
                "nw": nw, "rw": rw, "w1": w1, "w3": w3, "w2": w2}
               for c in range(8)]
    res = run_bass_kernel_spmd(nc, in_maps, core_ids=list(range(8)))
    return np.concatenate([res.results[c]["out"] for c in range(8)], 0)


def kernel(**inputs):
    g = {k: np.asarray(v) for k, v in inputs.items()}
    x0 = np.ascontiguousarray(g["L0_attn_x"].reshape(16384, D))
    x1 = run_l0a(x0, g["L0_attn_encoder_query"], g["L0_attn_encoder_value"],
                 g["L0_attn_mn_w1"], g["L0_attn_mn_b1"], g["L0_attn_mn_w2"],
                 g["L0_attn_mn_b2"], g["L0_attn_base_metric"],
                 g["L0_attn_decoder"], g["L0_attn_hebbian_U"],
                 g["L0_attn_hebbian_V"])
    x2 = run_l0f_outer(x1, g["L0_ffn_norm_w"], g["L0_ffn_router_w"],
                       g["L0_ffn_w1"], g["L0_ffn_w3"], g["L0_ffn_w2"])
    x3 = run_l1a(x2, g["L1_attn_W_Q"], g["L1_attn_b_Q"], g["L1_attn_W_K"],
                 g["L1_attn_b_K"], g["L1_attn_W_V"], g["L1_attn_b_V"],
                 g["L1_attn_W_O"], g["L1_attn_b_O"])
    x4 = run_l1f(x3, g["L1_ffn_w_gate"], g["L1_ffn_w_up"], g["L1_ffn_w_down"])
    return np.ascontiguousarray(x4.reshape(4, 4096, D).astype(np.float32))


# version 07 (final)
# speedup vs baseline: 1.0148x; 1.0148x over previous
import numpy as np
from contextlib import ExitStack

import concourse.bass as bass
import concourse.mybir as mybir
from concourse.bass_utils import run_bass_kernel_spmd

F32 = mybir.dt.float32
BF16 = mybir.dt.bfloat16
I32 = mybir.dt.int32
AF = mybir.ActivationFunctionType
ALU = mybir.AluOpType
AX = mybir.AxisListType.X

T_OWN = 2048
NT = T_OWN // 128
D = 1024
DC = D // 128
HALO = 63


class Prog:
    def __init__(self):
        self.nc = bass.Bass("TRN2", target_bir_lowering=False)
        self.es = ExitStack()
        self._sems = {}
        self.block = None

    def sem(self, name):
        if name not in self._sems:
            h = self.es.enter_context(self.nc.semaphore(name))
            self._sems[name] = [h, 0]
        return self._sems[name]

    def n(self, name):
        return self.sem(name)[1]

    def dinc(self, ins, name):
        s = self.sem(name)
        ins.then_inc(s[0], 16)
        s[1] += 16
        return s[1]

    def cinc(self, ins, name, k=1):
        s = self.sem(name)
        ins.then_inc(s[0], k)
        s[1] += k
        return s[1]

    def wait(self, eng, name, target=None):
        s = self.sem(name)
        t = s[1] if target is None else target
        if t > 0:
            eng.wait_ge(s[0], t)

    def ssync(self, eng, ins, name):
        s = self.sem(name)
        ins.then_inc(s[0], 1)
        s[1] += 1
        eng.wait_ge(s[0], s[1])

    def sbuf(self, name, shape, dtype):
        st = self._scopes[-1] if getattr(self, "_scopes", None) else self.es
        import os
        if os.environ.get("SBUF_DEBUG"):
            per = int(np.prod(shape[1:])) * mybir.dt.size(dtype)
            print(f"SBUF {name}: {per}B/prt, remaining {self.nc.sbuf_bytes_remaining}")
        return st.enter_context(self.nc.sbuf_tensor(name, shape, dtype))

    def push_scope(self):
        if not hasattr(self, "_scopes"):
            self._scopes = []
        self._scopes.append(ExitStack())

    def pop_scope(self):
        self._scopes.pop().close()

    def psum(self, name, shape, dtype=F32):
        return self.es.enter_context(self.nc.psum_tensor(name, shape, dtype))


def ts(i, k=128):
    return slice(i * k, (i + 1) * k)




def emit_consts(p):
    nc = p.nc
    p.ident = p.sbuf("ident", [128, 128], BF16)
    p.ones_r = p.sbuf("ones_r", [1, 512], BF16)
    p.ones_f = p.sbuf("ones_f", [1, 512], F32)
    p.eps = p.sbuf("eps", [128, 8], F32)
    p.iotai = p.sbuf("iotai", [128, 128], I32)

    @p.block.gpsimd
    def _(g):
        g.iota(p.iotai[:], pattern=[[1, 128]], base=0, channel_multiplier=-1,
               allow_small_or_imprecise_dtypes=True)
        g.memset(p.eps[:, 0:1], 1e-8)
        g.memset(p.eps[:, 1:2], 1e-5)
        g.memset(p.eps[:, 2:3], 1e-6)
        g.memset(p.eps[:, 3:4], 1e-9)
        g.memset(p.eps[:, 4:5], 1e-12)
        p.cinc(g.memset(p.eps[:, 5:6], 0.0), "cst")

    @p.block.vector
    def _(v):
        p.wait(v, "cst")
        v.tensor_single_scalar(p.ident[:], p.iotai[:], 0, ALU.is_equal)
        v.memset(p.ones_f[:], 1.0)
        p.cinc(v.memset(p.ones_r[:], 1.0), "cst")


def emit_load_x(p, x_ap, xhalo_ap=None):
    p.xsb = p.sbuf("xsb", [128, NT * D], F32)
    if xhalo_ap is not None:
        p.xhalo = p.sbuf("xhalo", [63, D], F32)

    @p.block.sync
    def _(s):
        for i in range(NT):
            p.dinc(s.dma_start(out=p.xsb[:, ts(i, D)], in_=x_ap[ts(i), :]), "xld")
        if xhalo_ap is not None:
            p.dinc(s.dma_start(out=p.xhalo[:], in_=xhalo_ap), "xld")


def emit_store_x(p, out_ap, wait_sems):
    @p.block.sync
    def _(s):
        for name, tgt in wait_sems:
            p.wait(s, name, tgt)
        for i in range(NT):
            p.dinc(s.dma_start(out=out_ap[ts(i), :], in_=p.xsb[:, ts(i, D)]), "xst")
        p.wait(s, "xst")


def alloc_ln_scratch(p):
    p.scol = p.sbuf("scol", [128, 1], F32)
    p.qcol = p.sbuf("qcol", [128, 1], F32)
    p.m2col = p.sbuf("m2col", [128, 1], F32)
    p.mcol2 = p.sbuf("mcol2", [128, 2], F32)
    p.vcol = p.sbuf("vcol", [128, 2], F32)
    p.rscol = p.sbuf("rscol", [128, 2], F32)
    p.rrcol = p.sbuf("rrcol", [128, 1], F32)
    p.sqf = p.sbuf("sqf", [128, 2 * D], BF16)
    p.ln2 = p.sbuf("ln2", [128, 2 * D], BF16)
    p.tpsum8 = p.psum("tps8", [128, 8 * 128], F32)


def emit_ln_transpose(p, dst_T, n_tiles, src_tile_fn, kind, tag, dst_col0=0,
                      gamma_bc=None, pre_wait=None, evac_wait_extra=None,
                      dst_col_fn=None, cb_every=None, cb_tensor=None,
                      cb_vector=None, cb_scalar=None, base=0):
    nc = p.nc
    eps = p.eps[:, 1:2] if kind == "ln" else p.eps[:, 2:3]
    if dst_col_fn is None:
        dst_col_fn = lambda i: dst_col0 + i * 128

    @p.block.vector
    def _(v):
        for it in range(n_tiles + 2):
            k = it - 2
            if k >= 0:
                p.wait(v, tag + "q", base + k + 1)
                p.ssync(v, v.reciprocal(p.rrcol[:], p.rscol[:, k % 2:k % 2 + 1]),
                        "vslf")
                if k >= 2:
                    p.wait(v, tag + "t", DC * (base + k - 1))
                dst = p.ln2[:, ts(k % 2, D)]
                src = src_tile_fn(k)
                if kind == "ln":
                    ins = v.tensor_scalar(dst, src, p.mcol2[:, k % 2:k % 2 + 1],
                                          p.rrcol[:], op0=ALU.subtract,
                                          op1=ALU.mult)
                elif gamma_bc is None:
                    ins = v.tensor_scalar(dst, src, p.rrcol[:], None,
                                          op0=ALU.mult)
                else:
                    ins = v.scalar_tensor_tensor(dst, src, p.rrcol[:], gamma_bc,
                                                 op0=ALU.mult, op1=ALU.mult)
                p.cinc(ins, tag + "n")
                if cb_vector is not None and (k + 1) % cb_every == 0:
                    cb_vector(v, (k + 1) // cb_every - 1)
            if it < n_tiles:
                if pre_wait is not None:
                    pre_wait(v, it)
                src = src_tile_fn(it)
                if kind == "ln":
                    p.ssync(v, v.reduce_sum(p.scol[:], src, AX), "vslf")
                    v.tensor_scalar_mul(p.mcol2[:, it % 2:it % 2 + 1],
                                        p.scol[:], 1.0 / D)
            j = it - 1
            if 0 <= j < n_tiles:
                p.wait(v, tag + "s", base + j + 1)
                if kind == "ln":
                    v.tensor_mul(p.m2col[:], p.mcol2[:, j % 2:j % 2 + 1],
                                 p.mcol2[:, j % 2:j % 2 + 1])
                p.ssync(v, v.reduce_sum(p.qcol[:], p.sqf[:, ts(j % 2, D)], AX),
                        "vslf")
                if kind == "ln":
                    ins = v.scalar_tensor_tensor(p.vcol[:, j % 2:j % 2 + 1],
                                                 p.qcol[:], 1.0 / D,
                                                 p.m2col[:], op0=ALU.mult,
                                                 op1=ALU.subtract)
                else:
                    ins = v.tensor_scalar_mul(p.vcol[:, j % 2:j % 2 + 1],
                                              p.qcol[:], 1.0 / D)
                p.cinc(ins, tag + "v")

    def emit_evac(s, i):
        p.wait(s, tag + "t", DC * (base + i + 1))
        if evac_wait_extra is not None:
            evac_wait_extra(s, i)
        c0 = dst_col_fn(i)
        for dc in range(DC):
            p.cinc(s.copy(dst_T[dc][:, c0:c0 + 128],
                          p.tpsum8[:, ts(dc)]), tag + "e")

    @p.block.scalar
    def _(s):
        for it in range(n_tiles + 2):
            k = it - 2
            if 0 <= k < n_tiles:
                emit_evac(s, k)
                if cb_scalar is not None and (k + 1) % cb_every == 0:
                    cb_scalar(s, (k + 1) // cb_every - 1)
            if it < n_tiles:
                if pre_wait is not None:
                    pre_wait(s, it)
                if it >= 2:
                    p.wait(s, tag + "v", base + it - 1)
                p.cinc(s.activation(p.sqf[:, ts(it % 2, D)], src_tile_fn(it),
                                    AF.Square), tag + "s")
            j = it - 1
            if 0 <= j < n_tiles:
                p.wait(s, tag + "v", base + j + 1)
                p.cinc(s.activation(p.rscol[:, j % 2:j % 2 + 1],
                                    p.vcol[:, j % 2:j % 2 + 1], AF.Sqrt,
                                    bias=eps), tag + "q")

    @p.block.tensor
    def _(t):
        for i in range(n_tiles):
            p.wait(t, tag + "n", base + i + 1)
            if i >= 1:
                p.wait(t, tag + "e", DC * (base + i))
            for dc in range(DC):
                p.cinc(t.matmul(p.tpsum8[:, ts(dc)],
                                p.ln2[:, (i % 2) * D + dc * 128:
                                      (i % 2) * D + dc * 128 + 128],
                                p.ident[:], start=True, stop=True), tag + "t")
            if cb_tensor is not None and (i + 1) % cb_every == 0:
                cb_tensor(t, (i + 1) // cb_every - 1)




def emit_l0f_outer_ln(p):
    oxt = p.oxt

    @p.block.vector
    def _(v):
        for nm, tgt in p.x_ready:
            p.wait(v, nm, tgt)
        p.wait(v, "cst")
        for i in range(NT):
            src = p.xsb[:, ts(i, D)]
            tmp = oxt[:, 0:D]
            v.tensor_mul(tmp, src, src)
            p.ssync(v, v.reduce_sum(p.qcol[:], tmp, AX), "vslf")
            p.ssync(v, v.reduce_sum(p.scol[:], src, AX), "vslf")
            p.ssync(v, v.tensor_scalar_mul(p.mcol2[:, 0:1], p.scol[:], 1.0 / D),
                    "vslf")
            p.ssync(v, v.tensor_mul(p.m2col[:], p.mcol2[:, 0:1],
                                    p.mcol2[:, 0:1]), "vslf")
            p.cinc(v.scalar_tensor_tensor(p.vcol[:, 0:1], p.qcol[:], 1.0 / D,
                                          p.m2col[:], op0=ALU.mult,
                                          op1=ALU.subtract), "oLv")
            p.wait(v, "oLq", i + 1)
            p.ssync(v, v.reciprocal(p.rrcol[:], p.rscol[:, 0:1]), "vslf")
            v.tensor_scalar(tmp, src, p.mcol2[:, 0:1], p.rrcol[:],
                            op0=ALU.subtract, op1=ALU.mult)
            p.cinc(v.tensor_copy(src, tmp), "oLc")

    @p.block.scalar
    def _(s):
        p.wait(s, "cst")
        for i in range(NT):
            p.wait(s, "oLv", i + 1)
            p.cinc(s.activation(p.rscol[:, 0:1], p.vcol[:, 0:1], AF.Sqrt,
                                bias=p.eps[:, 1:2]), "oLq")
    p.x_ready = [("oLc", NT)]


def emit_l0f_outer_add(p, x_ap):
    oxt = p.oxt

    @p.block.gpsimd
    def _(g):
        for nm, tgt in p.x_ready:
            p.wait(g, nm, tgt)
        for i in range(NT):
            if i >= 2:
                g.wait_ge(p.sem("oRa")[0], i - 1)
            p.dinc(g.dma_start(out=oxt[:, ts(i % 2, D)], in_=x_ap[ts(i), :]),
                   "oRl")

    @p.block.vector
    def _(v):
        for i in range(NT):
            p.wait(v, "oRl", 16 * (i + 1))
            dst = p.xsb[:, ts(i, D)]
            p.cinc(v.tensor_tensor(dst, oxt[:, ts(i % 2, D)], dst, ALU.add),
                   "oRa")
    p.x_ready = [("oRa", NT)]


def emit_l0_ffn(p, nw_ap, rw_ap, w1_ap, w3_ap, w2_ap, x_ap=None):
    nc = p.nc
    E, FC, TCN = 8, 16, 2
    znT = [p.sbuf(f"l0znT{dc}", [128, T_OWN], BF16) for dc in range(DC)]
    normbc = p.sbuf("l0nbc", [128, D], BF16)
    nwrow = p.sbuf("l0nw", [1, D], BF16)
    rw_sb = p.sbuf("l0rw", [128, DC * 8], BF16)
    sel8 = p.sbuf("l0sel8", [8, 8 * 128], BF16)
    ioe = p.sbuf("l0ioe", [8, 8 * 128], I32)
    cw_all = p.sbuf("l0cw", [128, 128], BF16)
    cwT = p.sbuf("l0cwT", [8, T_OWN], BF16)
    esc = p.sbuf("l0esc", [128, 128], F32)
    esc2 = p.sbuf("l0esc2", [128, 128], F32)
    ssb = p.sbuf("l0ssb", [128, 128], F32)
    msk1 = p.sbuf("l0m1", [128, 128], F32)
    mskd = p.sbuf("l0md", [128, 128], F32)
    msel = p.sbuf("l0msel", [128, 128], F32)
    gcol = p.sbuf("l0gA", [128, 16], F32)
    gcol2 = p.sbuf("l0gB", [128, 16], F32)
    gcol3 = p.sbuf("l0gC", [128, 16], F32)
    scol2 = p.sbuf("l0sc2", [128, 16], F32)
    cwbc = p.sbuf("l0cwbc", [128, 1024], BF16)
    w13 = [p.sbuf(f"l0w13_{j}", [128, 2 * D], BF16) for j in range(2)]
    w2h = p.sbuf("l0w2h", [128, FC * 512], BF16)
    h_all = p.sbuf("l0h", [128, FC * 1024], BF16)
    s_sb = p.sbuf("l0s", [128, 512], BF16)
    t1_sb = p.sbuf("l0t1", [128, 512], BF16)
    p.dbg_znT0 = znT[0][:]
    p.dbg_cwsb = cw_all[:]
    p.dbg_ssb = ssb[:]
    p.dbg_msk1 = msk1[:]
    p.dbg_msel = msel[:]
    p.dbg_esc = esc[:]
    p.dbg_esc2 = esc2[:]
    p.dbg_cwT = cwT[:]
    p.dbg_hall = h_all[:]
    p.dbg_cwbc = cwbc[:]
    p.dbg_normbc = normbc[:]

    @p.block.gpsimd
    def _(g):
        g.iota(ioe[:], pattern=[[1, 8], [0, 128]], base=0, channel_multiplier=-1,
               allow_small_or_imprecise_dtypes=True)
        p.dinc(g.dma_start(out=nwrow[:], in_=nw_ap.unsqueeze(0)), "l0w0")
        for dc in range(DC):
            p.dinc(g.dma_start(out=rw_sb[:, dc * 8:(dc + 1) * 8],
                               in_=rw_ap[ts(dc), :]), "l0w0")
        p.cinc(g.memset(cw_all[:], 0.0), "l0c1")

    @p.block.vector
    def _(v):
        p.wait(v, "l0c1")
        p.cinc(v.tensor_single_scalar(sel8[:], ioe[:], 0, ALU.is_equal), "l0c2")

    @p.block.tensor
    def _(t):
        p.wait(t, "l0c2")
        p.wait(t, "cst")
        p.wait(t, "l0w0")
        for half in range(2):
            ins = t.matmul(p.aux1[:, ts(half, 512)], p.ones_r[:, 0:128],
                           nwrow[:, ts(half, 512)], start=True, stop=True)
        p.cinc(ins, "l0c3")

    @p.block.scalar
    def _(s):
        p.wait(s, "l0c3")
        for half in range(2):
            p.cinc(s.copy(normbc[:, ts(half, 512)], p.aux1[:, ts(half, 512)]),
                   "l0c4")

    if x_ap is not None:
        p.oxt = p.sbuf("l0oxt", [128, 2 * D], F32)
        emit_l0f_outer_ln(p)

    import os
    nloops = int(os.environ.get("L0F_LOOPS", "2"))
    for l in range(nloops):
        _emit_moe_loop(p, l, w1_ap, w3_ap, w2_ap, znT, normbc, rw_sb, sel8,
                       cw_all, cwT, esc, esc2, ssb, msk1, mskd, msel,
                       gcol, gcol2, gcol3, scol2, cwbc, w13, w2h, h_all,
                       s_sb, t1_sb)
    p.x_ready = [(f"m{nloops-1}ad", p.n(f"m{nloops-1}ad"))]
    if x_ap is not None:
        emit_l0f_outer_add(p, x_ap)


def _emit_moe_loop(p, l, w1_ap, w3_ap, w2_ap, znT, normbc, rw_sb, sel8,
                   cw_all, cwT, esc, esc2, ssb, msk1, mskd, msel,
                   gcol, gcol2, gcol3, scol2, cwbc, w13, w2h, h_all,
                   s_sb, t1_sb):
    nc = p.nc
    E, FC, TCN = 8, 16, 2
    tg = f"m{l}"

    def pre(eng, i):
        if i == 0:
            p.wait(eng, "l0c4")
            if l == 0:
                for nm, tgt in p.x_ready:
                    p.wait(eng, nm, tgt)
            if l == 1:
                p.wait(eng, "m0ad")

    emit_ln_transpose(p, znT, NT, lambda i: p.xsb[:, ts(i, D)], "rms", tg,
                      gamma_bc=normbc[:], pre_wait=pre)

    @p.block.tensor
    def _(t):
        for i in range(NT):
            p.wait(t, tg + "e", DC * (i + 1))
            for dc in range(DC):
                ins = t.matmul(p.aux1[:, i * 8:(i + 1) * 8], znT[dc][:, ts(i)],
                               rw_sb[:, dc * 8:(dc + 1) * 8],
                               start=(dc == 0), stop=(dc == DC - 1))
            p.cinc(ins, tg + "lg")
        p.wait(t, tg + "cw", 1)
        for i in range(NT):
            if i >= 1:
                p.wait(t, tg + "ce", i)
            p.cinc(t.matmul(p.aux1[0:8, 512:640], cw_all[:, i * 8:(i + 1) * 8],
                            p.ident[:], start=True, stop=True), tg + "ct")

    @p.block.vector
    def _(v):
        p.wait(v, tg + "lg", NT)
        lg3 = p.aux1[:, 0:128].rearrange("q (g e) -> q g e", e=8)
        def b16(col):
            return col.unsqueeze(2).to_broadcast([128, NT, 8])
        S = p.ssync
        g3 = lambda t2d: t2d.rearrange("q (g e) -> q g e", e=8)
        S(v, v.reduce_max(gcol[:], lg3, AX), "vslf")
        S(v, v.tensor_tensor(g3(esc[:]), lg3, b16(gcol[:]), ALU.subtract), "vslf")
        p.cinc(v.tensor_copy(scol2[:], gcol[:]), tg + "x1")
        p.wait(v, tg + "x2", 1)
        S(v, v.reduce_sum(gcol[:], g3(esc2[:]), AX), "vslf")
        S(v, v.reciprocal(gcol2[:], gcol[:]), "vslf")
        S(v, v.tensor_tensor(g3(ssb[:]), g3(esc2[:]), b16(gcol2[:]), ALU.mult),
          "vslf")
        S(v, v.reduce_max(gcol[:], g3(ssb[:]), AX), "vslf")
        S(v, v.tensor_tensor(g3(msk1[:]), g3(ssb[:]), b16(gcol[:]), ALU.is_ge),
          "vslf")
        S(v, v.scalar_tensor_tensor(g3(mskd[:]), g3(msk1[:]), -1e30, g3(ssb[:]),
                                    op0=ALU.mult, op1=ALU.add), "vslf")
        S(v, v.reduce_max(gcol2[:], g3(mskd[:]), AX), "vslf")
        S(v, v.tensor_tensor(g3(msel[:]), g3(mskd[:]), b16(gcol2[:]), ALU.is_ge),
          "vslf")
        S(v, v.tensor_add(g3(mskd[:]), g3(msk1[:]), g3(msel[:])), "vslf")
        S(v, v.tensor_add(gcol3[:], gcol[:], gcol2[:]), "vslf")
        S(v, v.tensor_scalar(gcol[:], gcol3[:], 1e-9, None, op0=ALU.add), "vslf")
        S(v, v.reciprocal(gcol2[:], gcol[:]), "vslf")
        S(v, v.tensor_mul(g3(msk1[:]), g3(ssb[:]), g3(mskd[:])), "vslf")
        p.cinc(v.tensor_tensor(g3(cw_all[:]), g3(msk1[:]), b16(gcol2[:]),
                               ALU.mult), tg + "cw")

    @p.block.scalar
    def _(s):
        p.wait(s, tg + "x1", 1)
        p.cinc(s.activation(esc2[:], esc[:], AF.Exp), tg + "x2")
        for i in range(NT):
            p.wait(s, tg + "ct", i + 1)
            p.cinc(s.copy(cwT[:, ts(i)], p.aux1[0:8, 512:640]), tg + "ce")

    def up_mark(gi, tq):
        return gi * 2 + tq + 1

    def dn_mark(hi, sub):
        return hi * 8 + sub + 1

    @p.block.gpsimd
    def _(g):
        for e in range(E):
            for tc in range(TCN):
                et = e * TCN + tc
                for fc in range(FC):
                    gi = et * FC + fc
                    buf = w13[gi % 2]
                    if gi >= 2:
                        g.wait_ge(p.sem(tg + "mm")[0], up_mark(gi - 2, 1))
                    src1 = bass.AP(w1_ap.tensor, e * D * 2048 + fc * 128,
                                   [[2048, 128], [128 * 2048, 8], [1, 128]])
                    src3 = bass.AP(w3_ap.tensor, e * D * 2048 + fc * 128,
                                   [[2048, 128], [128 * 2048, 8], [1, 128]])
                    p.dinc(g.dma_start(out=buf[:, 0:D].rearrange(
                        "q (c m) -> q c m", m=128), in_=src1), tg + "wa")
                    p.dinc(g.dma_start(out=buf[:, D:2 * D].rearrange(
                        "q (c m) -> q c m", m=128), in_=src3), tg + "wa")
                for half in range(2):
                    hi = et * 2 + half
                    if hi >= 1:
                        g.wait_ge(p.sem(tg + "dm")[0], dn_mark(hi - 1, 7))
                    for fc2 in range(FC):
                        src2 = bass.AP(w2_ap.tensor,
                                       e * 2048 * D + fc2 * 128 * D + half * 512,
                                       [[D, 128], [1, 512]])
                        p.dinc(g.dma_start(out=w2h[:, fc2 * 512:fc2 * 512 + 512],
                                           in_=src2), tg + "wb")

    @p.block.tensor
    def _(t):
        for e in range(E):
            for tc in range(TCN):
                et = e * TCN + tc
                if et == 0:
                    p.wait(t, tg + "ce", NT)
                else:
                    p.wait(t, tg + "cbe", 2 * et)
                for tq in range(2):
                    p.cinc(t.matmul(p.aux1[:, ts(tq, 512)], sel8[:, ts(e)],
                                    cwT[:, tc * 1024 + tq * 512:
                                        tc * 1024 + tq * 512 + 512],
                                    start=True, stop=True), tg + "cb")
                for fc in range(FC):
                    gi = et * FC + fc
                    buf = w13[gi % 2]
                    t.wait_ge(p.sem(tg + "wa")[0], 32 * (gi + 1))
                    for tq in range(2):
                        idx = gi * 2 + tq
                        if idx >= 1:
                            t.wait_ge(p.sem(tg + "h")[0], 2 * idx)
                        cols = slice(tc * 1024 + tq * 512,
                                     tc * 1024 + tq * 512 + 512)
                        for dc in range(DC):
                            t.matmul(p.ps_a[:], buf[:, ts(dc)], znT[dc][:, cols],
                                     start=(dc == 0), stop=(dc == DC - 1))
                        for dc in range(DC):
                            ins = t.matmul(p.ps_b[:],
                                           buf[:, D + dc * 128:D + dc * 128 + 128],
                                           znT[dc][:, cols],
                                           start=(dc == 0), stop=(dc == DC - 1))
                        p.cinc(ins, tg + "mm")
                t.wait_ge(p.sem(tg + "h")[0], 2 * (et + 1) * FC * 2)
                for half in range(2):
                    hi = et * 2 + half
                    t.wait_ge(p.sem(tg + "wb")[0], 16 * 16 * (hi + 1))
                    for sub in range(8):
                        dmark = dn_mark(hi, sub)
                        if dmark > 2:
                            t.wait_ge(p.sem(tg + "ad")[0], dmark - 2)
                        pd = p.ps_c[sub % 2]
                        for fc in range(FC):
                            ins = t.matmul(pd[:],
                                           h_all[:, fc * 1024 + sub * 128:
                                                 fc * 1024 + sub * 128 + 128],
                                           w2h[:, fc * 512:fc * 512 + 512],
                                           start=(fc == 0), stop=(fc == FC - 1))
                        p.cinc(ins, tg + "dm")

    @p.block.scalar
    def _(s):
        for e in range(E):
            for tc in range(TCN):
                et = e * TCN + tc
                p.wait(s, tg + "cb", 2 * (et + 1))
                if et >= 1:
                    s.wait_ge(p.sem(tg + "h")[0], 2 * et * FC * 2)
                for tq in range(2):
                    p.cinc(s.copy(cwbc[:, ts(tq, 512)], p.aux1[:, ts(tq, 512)]),
                           tg + "cbe")
                for fc in range(FC):
                    for tq in range(2):
                        idx = (et * FC + fc) * 2 + tq
                        s.wait_ge(p.sem(tg + "mm")[0], idx + 1)
                        if idx >= 1:
                            s.wait_ge(p.sem(tg + "h")[0], 2 * idx - 1)
                        p.cinc(s.activation(s_sb[:], p.ps_a[:], AF.Silu),
                               tg + "h")

    @p.block.vector
    def _(v):
        for e in range(E):
            for tc in range(TCN):
                et = e * TCN + tc
                if et >= 1:
                    v.wait_ge(p.sem(tg + "dm")[0], 16 * et)
                v.wait_ge(p.sem(tg + "cbe")[0], 2 * (et + 1))
                for fc in range(FC):
                    for tq in range(2):
                        idx = (et * FC + fc) * 2 + tq
                        v.wait_ge(p.sem(tg + "h")[0], 2 * idx + 1)
                        v.scalar_tensor_tensor(t1_sb[:], p.ps_b[:], 1.0, s_sb[:],
                                               op0=ALU.mult, op1=ALU.mult)
                        p.cinc(v.tensor_mul(
                            h_all[:, fc * 1024 + tq * 512:
                                  fc * 1024 + tq * 512 + 512],
                            t1_sb[:], cwbc[:, ts(tq, 512)]), tg + "h")
                for half in range(2):
                    hi = et * 2 + half
                    for sub in range(8):
                        v.wait_ge(p.sem(tg + "dm")[0], dn_mark(hi, sub))
                        ti = tc * 8 + sub
                        dst = p.xsb[:, ti * D + half * 512:
                                    ti * D + half * 512 + 512]
                        p.cinc(v.scalar_tensor_tensor(dst, p.ps_c[sub % 2][:],
                                                      1.0, dst, op0=ALU.mult,
                                                      op1=ALU.add), tg + "ad")






QW = 2112


def emit_l0_attn(p, ins, dr):
    nc = p.nc
    qT = p.sbuf("sqT", [128, 8 * QW], BF16)
    vsb = p.sbuf("sV", [128, 8 * 17 * 128], BF16)
    rq = p.sbuf("srq", [16, QW], BF16)
    coefb = p.sbuf("scoef", [16, QW], BF16)
    m_sb = p.sbuf("sm", [128, 8], F32)
    negm2 = p.sbuf("snegm2", [128, 8], F32)
    mo2 = p.sbuf("smo2", [128, 2], BF16)
    Hm = p.sbuf("sHm", [128, 8 * 128], BF16)
    selR = p.sbuf("sselR", [16, 8 * 192], BF16)
    selC = p.sbuf("sselC", [16, 8 * 128], BF16)
    basem = p.sbuf("sbasem", [128, 8], F32)
    clipc = p.sbuf("sclipc", [128, 64], F32)
    clipn = p.sbuf("sclipn", [128, 64], F32)
    clip2 = p.sbuf("sclip2", [128, 64], F32)
    clip2n = p.sbuf("sclip2n", [128, 64], F32)
    sigp = p.sbuf("ssigp", [128, 8], F32)
    sigb = p.sbuf("ssigb", [128, 8], BF16)
    spp = p.sbuf("sspp", [128, 8], F32)
    import os
    vsdbg = None
    if "hvchk" in os.environ.get("DBG", ""):
        vsdbg = p.sbuf("svsdbg", [64, 256], BF16)
        p.dbg_vstgchk = vsdbg[:]

    _l0a_consts(p, ins, dr, locals())
    _l0a_proj(p, ins, dr, locals())
    _l0a_select(p, ins, dr, locals())
    import os
    if "hvchk" in os.environ.get("DBG", ""):
        hvdbg = p.sbuf("shvdbg", [63, D], BF16)
        p.dbg_hvchk = hvdbg[:]

        @p.block.sync
        def _(s):
            p.wait(s, "sdx", 32)
            p.dinc(s.dma_start(out=hvdbg[:], in_=dr["hv"][0:63, :]), "shvc")
            p.wait(s, "shvc")


def _l0a_consts(p, ins, dr, E):
    nc = p.nc
    selR, selC, Hm = E["selR"], E["selC"], E["Hm"]
    basem, clipc, clipn = E["basem"], E["clipc"], E["clipn"]
    clip2, clip2n, mo2 = E["clip2"], E["clip2n"], E["mo2"]
    p.push_scope()
    ioc = p.sbuf("sioc", [16, 192], I32)
    hebu = p.sbuf("shebu", [32, 1024], BF16)
    hebv = p.sbuf("shebv", [32, 1024], BF16)
    zb16 = p.sbuf("szb16", [128, 256], BF16)

    @p.block.gpsimd
    def _(g):
        for h in range(8):
            if h >= 1:
                p.wait(g, "ssel", 2 * h)
            p.cinc(g.iota(ioc[:], pattern=[[0, 192]], base=2 * h,
                          channel_multiplier=-1,
                          allow_small_or_imprecise_dtypes=True), "sio")
        p.dinc(g.dma_start(out=basem[:], in_=ins["basem"]), "scl")
        p.dinc(g.dma_start(out=clipc[:], in_=ins["clipc"]), "scl")
        p.dinc(g.dma_start(out=clipn[:], in_=ins["clipn"]), "scl")
        p.dinc(g.dma_start(out=clip2[:], in_=ins["clip2"]), "scl")
        p.dinc(g.dma_start(out=clip2n[:], in_=ins["clip2n"]), "scl")
        p.dinc(g.dma_start(out=hebu[:], in_=ins["hebUT"]), "scl")
        p.dinc(g.dma_start(out=hebv[:], in_=ins["hebV"]), "scl")
        p.cinc(g.memset(E["vsb"][:], 0.0), "svz")
        p.cinc(g.memset(zb16[:], 0.0), "szf")

    @p.block.vector
    def _(v):
        for h in range(8):
            p.wait(v, "sio", h + 1)
            p.cinc(v.tensor_single_scalar(selR[:, h * 192:h * 192 + 192],
                                          ioc[:], 0, ALU.is_equal), "ssel")
            p.cinc(v.tensor_single_scalar(selC[:, ts(h)], ioc[:, 0:128],
                                          -1, ALU.is_equal), "ssel")
        v.memset(mo2[:, 1:2], 1.0)
        v.memset(E["spp"][:], 0.0)
        p.cinc(v.memset(mo2[:, 0:1], 0.0), "ssel")

    @p.block.sync
    def _(s):
        p.wait(s, "szf")
        for j in range(8):
            p.dinc(s.dma_start(out=bass.AP(dr["wband"].tensor, j * 128 * 256,
                                           [[256, 128], [1, 256]]),
                               in_=zb16[:]), "szf2")

    @p.block.tensor
    def _(t):
        p.wait(t, "scl")
        p.wait(t, "cst")
        for h in range(8):
            if h >= 1:
                p.wait(t, "shme", h)
            p.cinc(t.matmul(p.aux1[:, 0:128], hebu[:, ts(h)], hebv[:, ts(h)],
                            start=True, stop=True), "shm")

    @p.block.scalar
    def _(s):
        for h in range(8):
            p.wait(s, "shm", h + 1)
            p.cinc(s.copy(Hm[:, ts(h)], p.aux1[:, 0:128]), "shme")
    p.pop_scope()


def _l0a_proj(p, ins, dr, E):
    nc = p.nc
    qT, vsb, rq, coefb = E["qT"], E["vsb"], E["rq"], E["coefb"]
    m_sb, negm2, mo2 = E["m_sb"], E["negm2"], E["mo2"]
    basem, sigp, sigb = E["basem"], E["sigp"], E["sigb"]
    p.push_scope()
    lnTr = [p.sbuf(f"slnT{dc}", [128, 2 * 512], BF16) for dc in range(DC)]
    hT = [p.sbuf(f"shT{dc}", [128, 63], BF16) for dc in range(DC)]
    enc_sb = p.sbuf("senc", [128, 8 * 1024], BF16)
    hx = p.sbuf("shx", [63, D], BF16)
    hsq = p.sbuf("shsq", [63, D], BF16)
    hln = p.sbuf("shln", [63, D], BF16)
    hcol = p.sbuf("shcol", [63, 4], F32)
    vstg = p.sbuf("svstg", [64, D], BF16)

    @p.block.gpsimd
    def _(g):
        p.dinc(g.dma_start(out=hx[:], in_=ins["x_halo"]), "shxl")

    @p.block.scalar
    def _(s):
        p.wait(s, "shxl")
        p.cinc(s.activation(hsq[:], hx[:], AF.Square), "shs")
        p.wait(s, "shv", 1)
        p.cinc(s.activation(hcol[:, 0:1], hcol[:, 3:4], AF.Sqrt,
                            bias=p.eps[0:63, 1:2]), "shq")

    @p.block.vector
    def _(v):
        p.wait(v, "shxl")
        p.wait(v, "cst")
        p.ssync(v, v.reduce_sum(hcol[:, 0:1], hx[:], AX), "vslf")
        p.ssync(v, v.tensor_scalar_mul(hcol[:, 1:2], hcol[:, 0:1], 1.0 / D),
                "vslf")
        v.tensor_mul(hcol[:, 2:3], hcol[:, 1:2], hcol[:, 1:2])
        p.wait(v, "shs", 1)
        p.ssync(v, v.reduce_sum(hcol[:, 0:1], hsq[:], AX), "vslf")
        p.ssync(v, v.scalar_tensor_tensor(hcol[:, 3:4], hcol[:, 0:1], 1.0 / D,
                                          hcol[:, 2:3], op0=ALU.mult,
                                          op1=ALU.subtract), "vslf")
        p.cinc(v.tensor_copy(p.scol[0:1, 0:1], hcol[0:1, 0:1]), "shv")
        p.wait(v, "shq", 1)
        p.ssync(v, v.reciprocal(hcol[:, 2:3], hcol[:, 0:1]), "vslf")
        p.cinc(v.tensor_scalar(hln[:], hx[:], hcol[:, 1:2], hcol[:, 2:3],
                               op0=ALU.subtract, op1=ALU.mult), "shn")

    @p.block.tensor
    def _(t):
        p.wait(t, "shn", 1)
        for dc in range(DC):
            if dc >= 1:
                p.wait(t, "shte", dc)
            p.cinc(t.matmul(p.ps_c[0][:, 0:63], hln[:, ts(dc)],
                            p.ident[0:63, 0:63], start=True, stop=True),
                   "sht")

    @p.block.scalar
    def _(s):
        for dc in range(DC):
            p.wait(s, "sht", dc + 1)
            p.cinc(s.copy(hT[dc][:], p.ps_c[0][:, 0:63]), "shte")

    for qk in range(2):
        wsrc = ins["encq"] if qk == 0 else ins["encv"]
        tgl = "sA" if qk == 0 else "sB"

        @p.block.gpsimd
        def _(g, qk=qk):
            if qk == 1:
                g.wait_ge(p.sem("spj")[0], 32)
                g.wait_ge(p.sem("sph")[0], 8)
                g.wait_ge(p.sem("spj")[0], 64)
            for dc in range(DC):
                p.dinc(g.dma_start(out=enc_sb[:, ts(dc, 1024)],
                                   in_=wsrc[ts(dc), :]), "senl")

        def cbt(t, ch, qk=qk):
            t.wait_ge(p.sem("senl")[0], 16 * DC * (qk + 1))
            t.wait_ge(p.sem(tgl + "e")[0], DC * 4 * (ch + 1))
            if qk == 0:
                t.wait_ge(p.sem("shte")[0], 8)
                for h in range(8):
                    idx = ch * 8 + h
                    if idx >= 2:
                        t.wait_ge(p.sem("spje")[0], idx - 1)
                    ps = p.ps_a if idx % 2 == 0 else p.ps_b
                    for dc in range(DC):
                        ie = t.matmul(ps[:], enc_sb[:, dc * 1024 + h * 128:
                                                    dc * 1024 + h * 128 + 128],
                                      lnTr[dc][:, ts(ch % 2, 512)],
                                      start=(dc == 0), stop=(dc == DC - 1))
                    p.cinc(ie, "spj")
                if ch == 3:
                    for h in range(8):
                        if h >= 1:
                            t.wait_ge(p.sem("sphe")[0], h)
                        for dc in range(DC):
                            ie = t.matmul(p.ps_c[1][:, 0:63],
                                          enc_sb[:, dc * 1024 + h * 128:
                                                  dc * 1024 + h * 128 + 128],
                                          hT[dc][:],
                                          start=(dc == 0), stop=(dc == DC - 1))
                        p.cinc(ie, "sph")
            else:
                for sub in range(4):
                    for half in range(2):
                        vidx = (ch * 4 + sub) * 2 + half
                        if vidx >= 2:
                            t.wait_ge(p.sem("spve")[0], vidx - 1)
                        ps = p.ps_a if vidx % 2 == 0 else p.ps_b
                        for dc in range(DC):
                            ie = t.matmul(ps[:],
                                          lnTr[dc][:, (ch % 2) * 512 + sub * 128:
                                               (ch % 2) * 512 + sub * 128 + 128],
                                          enc_sb[:, dc * 1024 + half * 512:
                                                  dc * 1024 + half * 512 + 512],
                                          start=(dc == 0), stop=(dc == DC - 1))
                        p.cinc(ie, "spv")
                if ch == 3:
                    for half in range(2):
                        if half == 1:
                            t.wait_ge(p.sem("sphve")[0], 1)
                        for dc in range(DC):
                            ie = t.matmul(p.ps_c[1][0:63, :], hT[dc][:],
                                          enc_sb[:, dc * 1024 + half * 512:
                                                  dc * 1024 + half * 512 + 512],
                                          start=(dc == 0), stop=(dc == DC - 1))
                        p.cinc(ie, "sphv")

        def cbv(v, ch, qk=qk):
            if qk == 0:
                return
            if ch == 0:
                p.wait(v, "svz", 1)
            for sub in range(4):
                for half in range(2):
                    vidx = (ch * 4 + sub) * 2 + half
                    v.wait_ge(p.sem("spv")[0], vidx + 1)
                    ps = p.ps_a if vidx % 2 == 0 else p.ps_b
                    tile = ch * 4 + sub + 1
                    dst = bass.AP(vsb[:].tensor,
                                  tile * 128 + half * 4 * 17 * 128,
                                  [[8 * 17 * 128, 128], [17 * 128, 4], [1, 128]])
                    p.cinc(v.tensor_relu(
                        dst, ps[:].rearrange("q (h e) -> q h e", e=128)),
                        "spve")

        def pre(eng, i, qk=qk):
            if i == 0:
                p.wait(eng, "cst")
                if qk == 0:
                    for nm, tgt in p.x_ready:
                        p.wait(eng, nm, tgt)

        def cbs(s, ch, qk=qk):
            if qk != 0:
                return
            for h in range(8):
                idx = ch * 8 + h
                s.wait_ge(p.sem("spj")[0], idx + 1)
                ps = p.ps_a if idx % 2 == 0 else p.ps_b
                p.cinc(s.activation(
                    qT[:, h * QW + 63 + ch * 512:
                       h * QW + 63 + ch * 512 + 512],
                    ps[:], AF.Relu), "spje")
            if ch == 3:
                for h in range(8):
                    s.wait_ge(p.sem("sph")[0], h + 1)
                    p.cinc(s.activation(qT[:, h * QW:h * QW + 63],
                                        p.ps_c[1][:, 0:63], AF.Relu), "sphe")

        emit_ln_transpose(p, lnTr, NT, lambda i: p.xsb[:, ts(i, D)], "ln", tgl,
                          pre_wait=pre,
                          dst_col_fn=lambda i: ((i // 4) % 2) * 512 + (i % 4) * 128,
                          cb_every=4, cb_tensor=cbt, cb_vector=cbv,
                          cb_scalar=cbs)

        if qk == 0:
            xpst = p.sbuf("sxpst", [128, 2 * D], BF16)
            qstg = p.sbuf("sqstg", [128, 2 * 512], BF16)
            spp = E["spp"]

            @p.block.gpsimd
            def _(g):
                for i in range(NT):
                    if i >= 2:
                        p.wait(g, "sAn", 16 + i - 1)
                    p.dinc(g.dma_start(out=xpst[:, ts(i % 2, D)],
                                       in_=ins["x_pair"][ts(i), :]), "sxpl")

            def cpre(eng, i):
                eng.wait_ge(p.sem("sxpl")[0], 16 * (i + 1))
                if i == 0:
                    p.wait(eng, "cst")

            def ccbt(t, ch):
                t.wait_ge(p.sem("sAe")[0], DC * (16 + 4 * (ch + 1)))
                for h in range(8):
                    idx = ch * 8 + h
                    t.wait_ge(p.sem("spje")[0], 31 + idx)
                    ps = p.ps_a if idx % 2 == 0 else p.ps_b
                    for dc in range(DC):
                        ie = t.matmul(ps[:], enc_sb[:, dc * 1024 + h * 128:
                                                    dc * 1024 + h * 128 + 128],
                                      lnTr[dc][:, ts(ch % 2, 512)],
                                      start=(dc == 0), stop=(dc == DC - 1))
                    p.cinc(ie, "spj")

            def ccbs(s, ch):
                for h in range(8):
                    idx = ch * 8 + h
                    s.wait_ge(p.sem("spj")[0], 33 + idx)
                    if idx >= 2:
                        s.wait_ge(p.sem("ssgp")[0], idx - 1)
                    ps = p.ps_a if idx % 2 == 0 else p.ps_b
                    p.cinc(s.activation(qstg[:, ts(idx % 2, 512)], ps[:],
                                        AF.Relu), "spje")

            def ccbv(v, ch):
                for h in range(8):
                    idx = ch * 8 + h
                    v.wait_ge(p.sem("spje")[0], 33 + idx)
                    S = p.ssync
                    S(v, v.reduce_sum(p.scol[:], qstg[:, ts(idx % 2, 512)],
                                      AX), "vslf")
                    S(v, v.reduce_sum(p.m2col[:], bass.AP(
                        qstg[:].tensor, (idx % 2) * 512,
                        [[2 * 512, 128], [8, 64]]), AX), "vslf")
                    S(v, v.scalar_tensor_tensor(
                        spp[:, h:h + 1], p.scol[:], 1.0 / 4096.0,
                        spp[:, h:h + 1], op0=ALU.mult, op1=ALU.add), "vslf")
                    p.cinc(v.scalar_tensor_tensor(
                        spp[:, h:h + 1], p.m2col[:], 0.5 / 512.0,
                        spp[:, h:h + 1], op0=ALU.mult, op1=ALU.add), "ssgp")

            emit_ln_transpose(p, lnTr, NT, lambda i: xpst[:, ts(i % 2, D)],
                              "ln", "sA", base=16, pre_wait=cpre,
                              dst_col_fn=lambda i: ((i // 4) % 2) * 512 +
                              (i % 4) * 128,
                              cb_every=4, cb_tensor=ccbt, cb_vector=ccbv,
                              cb_scalar=ccbs)

        if qk == 1:
            @p.block.scalar
            def _(s):
                for half in range(2):
                    s.wait_ge(p.sem("sphv")[0], half + 1)
                    p.cinc(s.activation(vstg[0:63, ts(half, 512)],
                                        p.ps_c[1][0:63, :], AF.Relu), "sphve")

            if E.get("vsdbg") is not None:
                @p.block.scalar
                def _(s):
                    p.wait(s, "sphve", 2)
                    p.cinc(s.copy(E["vsdbg"][0:63, :], vstg[0:63, 0:256]),
                           "svsd")

            @p.block.sync
            def _(s):
                p.wait(s, "sphve", 2)
                p.wait(s, "svz", 1)
                p.dinc(s.dma_start(out=dr["hv"], in_=vstg[:]), "shvw")
                p.wait(s, "shvw")
                dst = bass.AP(vsb[:].tensor, 65 * (8 * 17 * 128),
                              [[8 * 17 * 128, 63], [17 * 128, 8], [1, 128]])
                p.dinc(s.dma_start(
                    out=dst, in_=dr["hv"][0:63, :].rearrange(
                        "t (h e) -> t h e", e=128)), "shvr")

    p.pop_scope()
    p.push_scope()
    qsq = p.sbuf("sqsq", [128, QW], BF16)
    rqs = p.sbuf("srqs", [2, QW], BF16)
    cscr = qsq[0:16, :]
    cscr2 = p.sbuf("scscr2", [16, QW], BF16)[0:16, :]
    mnst = p.sbuf("smnst", [128, 1024], BF16)
    rowstg = p.sbuf("srowstg", [1, 1024], BF16)
    b1c = p.sbuf("sb1c", [128, 4], F32)
    b2c = p.sbuf("sb2c", [128, 8], F32)
    hid4r = p.sbuf("shid4r", [128, 4], BF16)
    hid4f = p.sbuf("shid4f", [128, 4], F32)
    hid4 = p.sbuf("shid4", [128, 4], BF16)
    mod8r = p.sbuf("smod8r", [128, 8], BF16)
    mod8 = p.sbuf("smod8", [128, 8], F32)
    ms1 = p.sbuf("sms1", [128, 8], F32)
    ms2 = p.sbuf("sms2", [128, 8], F32)

    @p.block.vector
    def _(v):
        p.wait(v, "spje", 32)
        p.wait(v, "sphe", 8)
        for h in range(8):
            p.ssync(v, v.reduce_sum(p.scol[:], qT[:, h * QW + 63:
                                                  h * QW + 63 + 2048], AX),
                    "vslf")
            v.tensor_scalar_mul(p.qcol[:], p.scol[:], 1.0 / 4096.0)
            p.ssync(v, v.reduce_sum(p.m2col[:], bass.AP(
                qT[:].tensor, h * QW + 63, [[8 * QW, 128], [8, 256]]), AX),
                "vslf")
            p.cinc(v.scalar_tensor_tensor(sigp[:, h:h + 1], p.m2col[:],
                                          0.5 / 512.0, p.qcol[:],
                                          op0=ALU.mult, op1=ALU.add), "ssgp")
        p.wait(v, "ssgp", 40)
        p.cinc(v.tensor_add(sigb[:], sigp[:], E["spp"][:]), "ssgr2")

    @p.block.gpsimd
    def _(g):
        g.wait_ge(p.sem("spv")[0], 32)
        g.wait_ge(p.sem("sphv")[0], 2)
        g.wait_ge(p.sem("shvw")[0], 16)
        g.wait_ge(p.sem("ssgp")[0], 32)
        g.wait_ge(p.sem("sAn")[0], 32)
        p.dinc(g.dma_start(out=b1c[:], in_=ins["b1c"]), "smw")
        p.dinc(g.dma_start(out=b2c[:], in_=ins["b2c"]), "smw")
        for dc in range(DC):
            if dc >= 1:
                g.wait_ge(p.sem("smm1")[0], dc)
            p.dinc(g.dma_start(out=mnst[:, 0:512],
                               in_=ins["mnw1"][ts(dc), :]), "smw1")
        for c4 in range(4):
            if c4 == 0:
                g.wait_ge(p.sem("smm1")[0], DC)
            else:
                g.wait_ge(p.sem("smm2")[0], 2 * c4)
            p.dinc(g.dma_start(out=mnst[:], in_=ins["mnw2"][ts(c4), :]),
                   "smw2")

    @p.block.tensor
    def _(t):
        t.wait_ge(p.sem("ssgr2")[0], 1)
        for dc in range(DC):
            t.wait_ge(p.sem("smw1")[0], 16 * (dc + 1))
            p.cinc(t.matmul(p.aux1[0:1, 0:512], sigb[:, dc:dc + 1],
                            mnst[:, 0:512], start=(dc == 0),
                            stop=(dc == DC - 1)), "smm1")
        t.wait_ge(p.sem("shd4")[0], 16)
        t.wait_ge(p.sem("smh2")[0], 1)
        t.wait_ge(p.sem("smh0")[0], 1)
        for c4 in range(4):
            t.wait_ge(p.sem("smw2")[0], 16 * (c4 + 1))
            for half in range(2):
                ie = t.matmul(p.aux1[0:1, half * 512:half * 512 + 512],
                              hid4[:, c4:c4 + 1], mnst[:, ts(half, 512)],
                              start=(c4 == 0), stop=(c4 == 3))
            p.cinc(ie, "smm2", 2)

    @p.block.vector
    def _(v):
        p.wait(v, "shd4", 16)
        p.wait(v, "smw", 32)
        p.cinc(v.tensor_add(hid4f[:], hid4r[:], b1c[:]), "smh1")
        p.wait(v, "shd8", 16)
        p.cinc(v.scalar_tensor_tensor(ms2[:], mod8r[:], 1.0, b2c[:],
                                      op0=ALU.mult, op1=ALU.add), "smh3")
        p.wait(v, "smh3", 1)
        p.cinc(v.scalar_tensor_tensor(ms2[:], ms2[:], 0.1, basem[:],
                                      op0=ALU.mult, op1=ALU.add)
               if False else
               v.scalar_tensor_tensor(mod8[:], ms2[:], 0.1, basem[:],
                                      op0=ALU.mult, op1=ALU.add), "smt1")
        p.wait(v, "smt3", 1)
        v.tensor_scalar_max(ms1[:], mod8[:], 0.0)
        p.ssync(v, v.tensor_add(m_sb[:], ms1[:], sigp[:]), "vslf")
        p.cinc(v.tensor_scalar_mul(negm2[:], m_sb[:], -2.0), "smdone")

    @p.block.scalar
    def _(s):
        p.wait(s, "smm1", DC)
        p.cinc(s.copy(rowstg[:, 0:512], p.aux1[0:1, 0:512]), "smh0")
        p.wait(s, "smh1", 1)
        p.cinc(s.activation(hid4[:], hid4f[:], AF.Gelu), "smh2")
        p.wait(s, "smm2", 8)
        p.cinc(s.copy(rowstg[:], p.aux1[0:1, 0:1024]), "smo0")
        p.wait(s, "smt1", 1)
        s.activation(ms1[:], mod8[:], AF.Abs)
        s.activation(sigp[:], ms1[:], AF.Exp, scale=-1.0)
        p.cinc(s.activation(sigp[:], sigp[:], AF.Ln, bias=1.0)
               if False else
               s.activation(ms2[:], sigp[:], AF.Ln, bias=1.0), "smt3x")
        p.cinc(s.copy(sigp[:], ms2[:]), "smt3")

    @p.block.sync
    def _(s):
        p.wait(s, "smh0", 1)
        p.dinc(s.dma_start(out=dr["hid"][0:512].unsqueeze(0),
                           in_=rowstg[:, 0:512]), "smhw")
        p.wait(s, "smhw")
        p.dinc(s.dma_start(out=hid4r[:], in_=bass.AP(dr["hid"].tensor, 0,
                                                     [[1, 128], [128, 4]]),
                           allow_slow_non_contiguous=True), "shd4")
        p.wait(s, "smo0", 1)
        p.dinc(s.dma_start(out=dr["mod"].unsqueeze(0), in_=rowstg[:]),
               "smodw")
        p.wait(s, "smodw")
        p.dinc(s.dma_start(out=mod8r[:], in_=bass.AP(dr["mod"].tensor, 0,
                                                     [[1, 128], [128, 8]]),
                           allow_slow_non_contiguous=True), "shd8")

    @p.block.scalar
    def _(s):
        for h in range(8):
            p.wait(s, "smdone", 1)
            if h >= 1:
                s.wait_ge(p.sem("srmm")[0], 5 * h)
                s.wait_ge(p.sem("srqw")[0], 16 * h)
            p.cinc(s.activation(qsq[:], qT[:, h * QW:(h + 1) * QW], AF.Square),
                   "srsq")
            for ck in range(5):
                s.wait_ge(p.sem("srmm")[0], 5 * h + ck + 1)
                w = 512 if ck < 4 else 64
                p.cinc(s.copy(rqs[:, ck * 512:ck * 512 + w],
                              p.aux1[0:2, 0:w]), "srev")
        p.wait(s, "srql", 16 * 8)
        p.cinc(s.activation(cscr, rq[:], AF.Sqrt), "scf1")

    @p.block.vector
    def _(v):
        for h in range(8):
            p.wait(v, "smdone", 1)
            if h >= 1:
                v.wait_ge(p.sem("srmm")[0], 5 * h)
            p.cinc(v.tensor_scalar(mo2[:, 0:1], m_sb[:, h:h + 1], 1.0, None,
                                   op0=ALU.mult), "srm2")
        p.wait(v, "scf1", 1)
        v.tensor_scalar_max(cscr2, cscr, 1e-12)
        with nc.allow_low_precision(reason="coef term is bf16-class noise"):
            p.ssync(v, v.reciprocal(coefb[:], cscr2), "vslf")
        v.tensor_scalar(cscr2, cscr, 0.2, None, op0=ALU.is_gt)
        p.ssync(v, v.tensor_scalar_mul(cscr, coefb[:], 0.1), "vslf")
        p.cinc(v.tensor_mul(coefb[:], cscr, cscr2), "scf2")

    @p.block.tensor
    def _(t):
        for h in range(8):
            t.wait_ge(p.sem("srsq")[0], h + 1)
            t.wait_ge(p.sem("srm2")[0], h + 1)
            for ck in range(5):
                if 5 * h + ck >= 1:
                    t.wait_ge(p.sem("srev")[0], 5 * h + ck)
                w = 512 if ck < 4 else 64
                p.cinc(t.matmul(p.aux1[0:2, 0:w], mo2[:],
                                qsq[:, ck * 512:ck * 512 + w],
                                start=True, stop=True), "srmm")

    @p.block.sync
    def _(s):
        for h in range(8):
            p.wait(s, "srev", 5 * (h + 1))
            p.dinc(s.dma_start(out=dr["rq"], in_=rqs[:]), "srqw")
            p.wait(s, "srqw", 16 * (h + 1))
            p.dinc(s.dma_start(out=rq[2 * h:2 * h + 2, :], in_=dr["rq"]),
                   "srql")
    p.pop_scope()


def _l0a_select(p, ins, dr, E):
    nc = p.nc
    qT, vsb, rq, coefb = E["qT"], E["vsb"], E["rq"], E["coefb"]
    negm2, Hm = E["negm2"], E["Hm"]
    selR, selC = E["selR"], E["selC"]
    clipc, clipn, clip2, clip2n = (E["clipc"], E["clipn"], E["clip2"],
                                   E["clip2n"])
    p.push_scope()
    dec_sb = p.sbuf("sdec", [128, 8 * 512], BF16)
    outg = p.sbuf("soutg", [128, 8 * 512], BF16)
    mqg = [p.sbuf(f"smqg{r}", [128, 512], BF16) for r in range(2)]
    Dstg = [p.sbuf(f"sDst{r}", [128, 4 * 192], BF16) for r in range(2)]
    Dwin = [p.sbuf(f"sDw{r}", [128, 4 * 64], BF16) for r in range(2)]
    Ewin = [p.sbuf(f"sEw{r}", [128, 4 * 64], F32) for r in range(2)]
    zapA = p.sbuf("szapA", [128, 4 * 64], F32)
    zapB = p.sbuf("szapB", [128, 4 * 64], F32)
    zapg = [p.sbuf(f"szapg{r}", [128, 4 * 64], F32) for r in range(2)]
    Wraw = p.sbuf("sWraw", [128, 4 * 64], F32)
    Wn = [p.sbuf(f"sWn{r}", [128, 4 * 64], BF16) for r in range(2)]
    Bsb = [p.sbuf(f"sBsb{r}", [128, 4 * 256], BF16) for r in range(2)]
    WtAB = [p.sbuf(f"sWt{r}", [128, 8 * 128], BF16) for r in range(2)]
    qscb = [p.sbuf(f"sqsc{r}", [128, 512], BF16) for r in range(2)]
    s8 = p.sbuf("ss8", [128, 32], F32)
    d0c = p.sbuf("sd0", [128, 1], BF16)
    e0c = p.sbuf("se0", [128, 1], F32)
    tfx = p.sbuf("stfx", [128, 64], F32)
    tfx2 = p.sbuf("stfx2", [128, 64], F32)
    tfx3 = p.sbuf("stfx3", [128, 64], F32)
    wfix = p.sbuf("swfix", [128, 64], BF16)
    zc4 = p.sbuf("szc4", [128, 4], F32)
    cnt4 = p.sbuf("scnt4", [128, 4], F32)
    ov4 = p.sbuf("sov4", [128, 4], F32)
    oe0 = p.sbuf("soe0", [128, 1], F32)
    rc4 = p.sbuf("src4", [128, 4], F32)
    s0c = p.sbuf("ss0c", [128, 1], F32)
    s0b = p.sbuf("ss0b", [128, 1], BF16)
    msk = p.sbuf("smsk", [128, 4 * 64], F32)

    POFF = [0, 192, 512, 704]
    p.dbg_msb = E["m_sb"][:]
    p.dbg_sigb = E["sigb"][:]
    p.dbg_rq = E["rq"][:]
    p.dbg_coef = E["coefb"][:]
    p.dbg_qT0 = E["qT"][:, 0:QW]
    p.dbg_vsb0 = E["vsb"][:, 0:17 * 128]
    p.dbg_outg = outg[:]
    p.dbg_Wn = Wn[1][:]
    p.dbg_Ew = Ewin[1][:]
    p.dbg_Dw = Dwin[1][:]
    p.dbg_Hm = E["Hm"][:]

    @p.block.gpsimd
    def _(g):
        p.wait(g, "scf2", 1)
        for grp in range(4):
            for half in range(2):
                gh = grp * 2 + half
                if gh >= 1:
                    g.wait_ge(p.sem("sdm")[0], 4 * gh)
                for dc in range(DC):
                    p.dinc(g.dma_start(
                        out=dec_sb[:, ts(dc, 512)],
                        in_=ins["dec"][ts(dc), half * 512:half * 512 + 512]),
                        "sdcl")

    for grp in range(4):
        for h in range(8):
            k = grp * 8 + h
            r = k % 2
            _l0a_iter(p, E, locals(), grp, h, k, r, dr)
        _l0a_decode(p, E, locals(), grp, dr)
    p.pop_scope()
    p.x_ready = [("sdx", 32)]


def _l0a_iter(p, E, L, grp, h, k, r, dr):
    nc = p.nc
    qT, vsb, rq, coefb = E["qT"], E["vsb"], E["rq"], E["coefb"]
    negm2, Hm = E["negm2"], E["Hm"]
    selR, selC = E["selR"], E["selC"]
    clipc, clipn, clip2, clip2n = (E["clipc"], E["clipn"], E["clip2"],
                                   E["clip2n"])
    (mqg, Dstg, Dwin, Ewin, zapA, zapB, zapg, Wraw, Wn, Bsb, WtAB, qscb, s8,
     d0c, e0c, tfx, tfx2, wfix, zc4, cnt4, ov4, oe0, rc4, s0c, s0b, msk,
     POFF, outg, tfx3) = (L["mqg"], L["Dstg"], L["Dwin"], L["Ewin"], L["zapA"],
                          L["zapB"], L["zapg"], L["Wraw"], L["Wn"], L["Bsb"],
                          L["WtAB"], L["qscb"], L["s8"], L["d0c"], L["e0c"],
                          L["tfx"], L["tfx2"], L["wfix"], L["zc4"], L["cnt4"],
                          L["ov4"], L["oe0"], L["rc4"], L["s0c"], L["s0b"],
                          L["msk"], L["POFF"], L["outg"], L["tfx3"])
    own0 = 63 + grp * 512

    @p.block.vector
    def _(v):
        if k == 0:
            p.wait(v, "smdone", 1)
            p.wait(v, "srql", 128)
        if k >= 2:
            v.wait_ge(p.sem("spm")[0], 4 * (k - 1))
        p.cinc(v.tensor_scalar_mul(mqg[r][:], qT[:, h * QW + own0:
                                                 h * QW + own0 + 512],
                                   negm2[:, h:h + 1]), "smq")

    @p.block.tensor
    def _(t):
        t.wait_ge(p.sem("smq")[0], k + 1)
        if k >= 1:
            t.wait_ge(p.sem("sde")[0], 4 * k)
        for i4 in range(4):
            i = grp * 4 + i4
            wc = slice(128 * i, 128 * i + 191)
            t.matmul(p.aux1[:, POFF[i4]:POFF[i4] + 191],
                     mqg[r][:, ts(i4)], qT[:, h * QW + wc.start:
                                          h * QW + wc.stop],
                     start=True, stop=False)
            t.matmul(p.aux1[:, POFF[i4]:POFF[i4] + 191],
                     selR[:, h * 192:h * 192 + 128], rq[:, wc],
                     start=False, stop=False)
            p.cinc(t.matmul(p.aux1[:, POFF[i4]:POFF[i4] + 191],
                            rq[:, 63 + 128 * i:63 + 128 * i + 128],
                            selR[:, h * 192:h * 192 + 191],
                            start=False, stop=True), "spm")

    @p.block.scalar
    def _(s):
        for i4 in range(4):
            s.wait_ge(p.sem("spm")[0], 4 * k + i4 + 1)
            if k >= 2:
                s.wait_ge(p.sem("sdw")[0], 16 * (k - 1))
            p.cinc(s.activation(Dstg[r][:, 192 * i4:192 * i4 + 191],
                                p.aux1[:, POFF[i4]:POFF[i4] + 191],
                                AF.Sqrt, bias=p.eps[:, 0:1]), "sde")

    @p.block.sync
    def _(s):
        p.wait(s, "sde", 4 * (k + 1))
        base = r * 4 * 128 * 192
        p.dinc(s.dma_start(
            out=bass.AP(dr["dband"].tensor, base,
                        [[192, 128], [128 * 192, 4], [1, 192]]),
            in_=Dstg[r][:].rearrange("q (g c) -> q g c", c=192)), "sdw")
        p.wait(s, "sdw", 16 * (k + 1))
        if k >= 2:
            p.wait(s, "sxp", k - 1)
        p.dinc(s.dma_start(
            out=Dwin[r][:].rearrange("q (g j) -> q g j", j=64),
            in_=bass.AP(dr["dband"].tensor, base,
                        [[193, 128], [128 * 192, 4], [1, 64]])), "sdg")
        if grp == 0:
            if k >= 1:
                p.wait(s, "sxp", k)
            p.dinc(s.dma_start(
                out=d0c[:], in_=bass.AP(dr["dband"].tensor, base + 63,
                                        [[192, 128], [1, 1]]),
                allow_slow_non_contiguous=True), "sd0")

    @p.block.vector
    def _(v):
        v.wait_ge(p.sem("sdg")[0], 16 * (k + 1))
        va = bass.AP(Dwin[r][:].tensor, 63, [[4 * 64, 128], [64, 4], [1, 1]])
        p.cinc(v.memset(va, 1e-4), "sslf")
        if grp == 0:
            v.wait_ge(p.sem("sd0")[0], 16 * (h + 1))
            p.ssync(v, v.memset(d0c[0:1, 0:1], 1e-4), "vslf")
            p.ssync(v, v.tensor_copy(e0c[:], d0c[:]), "vslf")
            p.ssync(v, v.tensor_mul(tfx[:], Dwin[r][:, 0:64], clipn[:]),
                    "vslf")
            p.ssync(v, v.tensor_scalar_mul(tfx2[:], clipc[:], e0c[:]), "vslf")
            p.cinc(v.tensor_add(tfx3[:], tfx[:], tfx2[:]), "sfix")

    @p.block.scalar
    def _(s):
        p.wait(s, "sslf", k + 1)
        if k >= 2:
            s.wait_ge(p.sem("ssl")[0], k - 1)
        if grp == 0:
            s.wait_ge(p.sem("sfix")[0], h + 1)
            s.activation(Ewin[r][:, 0:64], tfx3[:], AF.Exp, scale=-1.0)
            s.activation(Ewin[r][:, 64:256], Dwin[r][:, 64:256], AF.Exp,
                         scale=-1.0)
            p.cinc(s.activation(e0c[:], d0c[:], AF.Exp, scale=-1.0), "sxp")
        else:
            p.cinc(s.activation(Ewin[r][:], Dwin[r][:], AF.Exp, scale=-1.0),
                   "sxp")

    @p.block.vector
    def _(v):
        v.wait_ge(p.sem("sxp")[0], k + 1)
        if k >= 2:
            v.wait_ge(p.sem("sws")[0], 16 * (k - 1))
        srcs = [Ewin[r], zapA, zapB, zapA]
        dsts = [zapA, zapB, zapA, zapg[r]]
        for rd in range(4):
            ins_t, out_t = srcs[rd], dsts[rd]
            last = None
            for i4 in range(4):
                sl = ts(i4, 64)
                s8s = s8[:, i4 * 8:i4 * 8 + 8]
                v.max(out=s8s, in_=ins_t[:, sl])
                last = v.match_replace(out=out_t[:, sl], in_to_replace=s8s,
                                       in_values=ins_t[:, sl], imm_value=2.0)
            p.ssync(v, last, "vslf")
        p.ssync(v, v.scalar_tensor_tensor(Wraw[:], zapg[r][:], 1.5, Ewin[r][:],
                                          op0=ALU.is_gt, op1=ALU.mult),
                "vslf")
        g3 = lambda tt: tt.rearrange("q (g j) -> q g j", j=64)
        p.ssync(v, v.reduce_sum(zc4[:].unsqueeze(2), g3(Wraw[:]), AX), "vslf")
        if grp == 0:
            v.tensor_scalar(msk[:], zapg[r][:], 1.5, None, op0=ALU.is_gt)
            p.ssync(v, v.reduce_sum(cnt4[:].unsqueeze(2), g3(msk[:]), AX),
                    "vslf")
            p.ssync(v, v.tensor_scalar_add(ov4[:], cnt4[:], -32.0), "vslf")
            p.ssync(v, v.tensor_scalar_max(cnt4[:], ov4[:], 0.0), "vslf")
            p.ssync(v, v.tensor_mul(oe0[:], cnt4[:, 0:1], e0c[:]), "vslf")
            p.ssync(v, v.tensor_sub(ov4[:, 0:1], zc4[:, 0:1], oe0[:]), "vslf")
            p.ssync(v, v.tensor_copy(zc4[:, 0:1], ov4[:, 0:1]), "vslf")
        p.ssync(v, v.tensor_scalar_add(rc4[:], zc4[:], 1e-8), "vslf")
        p.ssync(v, v.reciprocal(zc4[:], rc4[:]), "vslf")
        p.ssync(v, v.tensor_tensor(g3(Wn[r][:]), g3(Wraw[:]),
                                   zc4[:].unsqueeze(2).to_broadcast(
                                       [128, 4, 64]), ALU.mult), "vslf")
        p.cinc(v.tensor_copy(s8[0:1, 0:1], zc4[0:1, 0:1]), "swn")
        if grp == 0:
            v.tensor_mul(tfx[:], Wn[r][:, 0:64], clip2[:])
            p.ssync(v, v.reduce_sum(s0c[:], tfx[:], AX), "vslf")
            p.ssync(v, v.tensor_mul(tfx2[:, 0:1], oe0[:], zc4[:, 0:1]),
                    "vslf")
            p.ssync(v, v.tensor_sub(tfx[:, 0:1], s0c[:], tfx2[:, 0:1]),
                    "vslf")
            v.tensor_copy(s0b[:], tfx[:, 0:1])
            p.ssync(v, v.tensor_mul(wfix[:], Wn[r][:, 0:64], clip2n[:]),
                    "vslf")
            p.cinc(v.tensor_copy(Wn[r][:, 0:64], wfix[:]), "swf")
        p.cinc(v.tensor_copy(s8[0:1, 0:1], zc4[0:1, 0:1]), "ssl")

    @p.block.sync
    def _(s):
        p.wait(s, "ssl", k + 1)
        base = r * 4 * 128 * 256
        if k >= 2:
            p.wait(s, "srb", 16 * (k - 1))
        if grp == 0:
            p.wait(s, "swf", h + 1)
        p.dinc(s.dma_start(
            out=bass.AP(dr["wband"].tensor, base + 65,
                        [[257, 128], [128 * 256, 4], [1, 64]]),
            in_=Wn[r][:].rearrange("q (g j) -> q g j", j=64)), "sws")
        p.wait(s, "sws", 16 * (k + 1))
        if grp == 0:
            p.dinc(s.dma_start(
                out=bass.AP(dr["wband"].tensor, base + 128,
                            [[256, 128], [1, 1]]),
                in_=s0b[:], allow_slow_non_contiguous=True), "ss0")
            p.wait(s, "ss0", 16 * (h + 1))
        if k >= 2:
            p.wait(s, "stp", 8 * (k - 1))
        p.dinc(s.dma_start(
            out=Bsb[r][:].rearrange("q (g c) -> q g c", c=256),
            in_=bass.AP(dr["wband"].tensor, base,
                        [[256, 128], [128 * 256, 4], [1, 256]])), "srb")

    @p.block.tensor
    def _(t):
        t.wait_ge(p.sem("srb")[0], 16 * (k + 1))
        if k >= 1:
            t.wait_ge(p.sem("ste")[0], 8 * k)
        for j in range(8):
            p.cinc(t.matmul(p.tpsum8[:, ts(j)], Bsb[r][:, ts(j)], p.ident[:],
                            start=True, stop=True), "stp")
        if k >= 1:
            t.wait_ge(p.sem("sqs")[0], k)
        p.cinc(t.matmul(p.ps_b[:], selC[:, ts(h)],
                        coefb[:, own0:own0 + 512], start=True, stop=True),
               "scfb")

    @p.block.scalar
    def _(s):
        for j in range(8):
            s.wait_ge(p.sem("stp")[0], 8 * k + j + 1)
            p.cinc(s.copy(WtAB[r][:, ts(j)], p.tpsum8[:, ts(j)]), "ste")

    @p.block.vector
    def _(v):
        v.wait_ge(p.sem("scfb")[0], k + 1)
        if k >= 2:
            v.wait_ge(p.sem("sct")[0], 4 * (k - 1))
        p.cinc(v.tensor_mul(qscb[r][:], qT[:, h * QW + own0:
                                           h * QW + own0 + 512],
                            p.ps_b[:]), "sqs")

    @p.block.tensor
    def _(t):
        t.wait_ge(p.sem("ste")[0], 8 * (k + 1))
        t.wait_ge(p.sem("sqs")[0], k + 1)
        if grp == 0 and h == 0:
            t.wait_ge(p.sem("shvr")[0], 16)
        if k >= 1:
            t.wait_ge(p.sem("sot")[0], k)
        for i4 in range(4):
            i = grp * 4 + i4
            vb = 8 * 17 * 128
            t.matmul(p.ps_c[0][:, ts(i4)],
                     bass.AP(vsb[:].tensor, h * 17 * 128 + i * 128,
                             [[vb, 128], [1, 128]]),
                     WtAB[r][:, ts(2 * i4)], start=True, stop=False)
            t.matmul(p.ps_c[0][:, ts(i4)],
                     bass.AP(vsb[:].tensor, h * 17 * 128 + (i + 1) * 128,
                             [[vb, 128], [1, 128]]),
                     WtAB[r][:, ts(2 * i4 + 1)], start=False, stop=False)
            p.cinc(t.matmul(p.ps_c[0][:, ts(i4)], Hm[:, ts(h)],
                            qscb[r][:, ts(i4)], start=False, stop=True),
                   "sct")

    @p.block.vector
    def _(v):
        v.wait_ge(p.sem("sct")[0], 4 * (k + 1))
        if grp >= 1 and h == 0:
            v.wait_ge(p.sem("sdm")[0], 8 * grp)
        p.cinc(v.tensor_mul(outg[:, ts(h, 512)],
                            qT[:, h * QW + own0:h * QW + own0 + 512],
                            p.ps_c[0][:]), "sot")


def _l0a_decode(p, E, L, grp, dr):
    nc = p.nc
    dec_sb, outg = L["dec_sb"], L["outg"]

    @p.block.tensor
    def _(t):
        t.wait_ge(p.sem("sot")[0], 8 * (grp + 1))
        for half in range(2):
            gh = grp * 2 + half
            t.wait_ge(p.sem("sdcl")[0], 16 * DC * (gh + 1))
            for i4 in range(4):
                oidx = gh * 4 + i4
                if oidx >= 1:
                    t.wait_ge(p.sem("sdx")[0], oidx)
                for h in range(8):
                    ie = t.matmul(p.ps_c[1][:], outg[:, h * 512 + i4 * 128:
                                                     h * 512 + i4 * 128 + 128],
                                  dec_sb[:, ts(h, 512)],
                                  start=(h == 0), stop=(h == 7))
                p.cinc(ie, "sdm")

    @p.block.vector
    def _(v):
        for half in range(2):
            gh = grp * 2 + half
            for i4 in range(4):
                ti = grp * 4 + i4
                oidx = gh * 4 + i4
                v.wait_ge(p.sem("sdm")[0], oidx + 1)
                dst = p.xsb[:, ti * D + half * 512:ti * D + half * 512 + 512]
                p.cinc(v.scalar_tensor_tensor(dst, p.ps_c[1][:], 1.0, dst,
                                              op0=ALU.mult, op1=ALU.add),
                       "sdx")




def emit_l1_attn(p, xown_ap, dst_ap, xfull_ap, wq_ap, wk_ap, wv_ap, wo_ap,
                 bq_ap, bk_ap, bv_ap, bo_ap, sin_ap, cos_ap, partial_dram,
                 rs_out_dram, partial_only=False):
    nc = p.nc
    NCH = 8
    S4 = 4096
    QT = [p.sbuf(f"aQT{hp}", [128, S4], BF16) for hp in range(4)]
    KT = [p.sbuf(f"aKT{hp}", [128, S4], BF16) for hp in range(4)]
    vsb = p.sbuf("aV", [128, 8 * 65 * 32], BF16)
    r2m = p.sbuf("ar2", [128, 128], BF16)
    bq_sb = p.sbuf("abq", [128, 4], F32)
    bk_sb = p.sbuf("abk", [128, 4], F32)
    bv_row = p.sbuf("abvr", [1, 512], F32)
    bv_bc = p.sbuf("abvbc", [128, 512], F32)
    bo_row = p.sbuf("abor", [1, D], F32)

    p.push_scope()
    lnTc = [p.sbuf(f"alnT{dc}", [128, 2 * 512], BF16) for dc in range(DC)]
    xstage = p.sbuf("axst", [128, 2 * D], F32)
    wq_sb = p.sbuf("awq", [128, DC * 512], BF16)
    wk_sb = p.sbuf("awk", [128, DC * 512], BF16)
    wv_sb = p.sbuf("awv", [128, DC * 512], BF16)
    sc_sb = p.sbuf("asc", [128, 4 * 512], F32)
    io_a = p.sbuf("aio", [128, 512], I32)
    cs1 = p.sbuf("acs1", [128, 128], F32)
    cs2 = p.sbuf("acs2", [128, 128], F32)
    cs3 = p.sbuf("acs3", [128, 128], F32)
    t1f = p.sbuf("at1", [128, 512], F32)
    t2f = p.sbuf("at2", [128, 512], F32)
    qrawb = p.sbuf("aqraw", [128, 2 * 512], BF16)

    @p.block.gpsimd
    def _(g):
        p.cinc(g.iota(io_a[:, 0:128], pattern=[[1, 128]], base=0,
                      channel_multiplier=-1,
                      allow_small_or_imprecise_dtypes=True), "acg")
        p.wait(g, "acv", 1)
        p.cinc(g.iota(io_a[:, 0:128].rearrange("q (a b c) -> q a b c",
                                               b=2, c=32),
                      pattern=[[0, 2], [1, 2], [0, 32]], base=0,
                      channel_multiplier=0,
                      allow_small_or_imprecise_dtypes=True), "acg")
        for dc in range(DC):
            p.dinc(g.dma_start(out=wq_sb[:, ts(dc, 512)],
                               in_=wq_ap[ts(dc), :]), "awl")
            p.dinc(g.dma_start(out=wk_sb[:, ts(dc, 512)],
                               in_=wk_ap[ts(dc), :]), "awl")
            p.dinc(g.dma_start(out=wv_sb[:, ts(dc, 512)],
                               in_=wv_ap[ts(dc), :]), "awl")
        p.dinc(g.dma_start(out=bq_sb[:], in_=bq_ap), "awl")
        p.dinc(g.dma_start(out=bk_sb[:], in_=bk_ap), "awl")
        p.dinc(g.dma_start(out=bv_row[:], in_=bv_ap.unsqueeze(0)), "awl")
        p.dinc(g.dma_start(out=bo_row[:], in_=bo_ap.unsqueeze(0)), "awl")
        va = bass.AP(vsb[:].tensor, 64,
                     [[8 * 65 * 32, 128], [65, 8 * 32], [1, 1]])
        p.cinc(g.memset(va, 1.0), "avone")

    @p.block.vector
    def _(v):
        p.wait(v, "acg", 1)
        v.tensor_single_scalar(cs1[:], io_a[:, 0:128], -32, ALU.is_equal)
        p.cinc(v.tensor_single_scalar(cs2[:], io_a[:, 0:128], 32,
                                      ALU.is_equal), "acv")
        p.wait(v, "acg", 2)
        p.ssync(v, v.tensor_single_scalar(t1f[:, 0:128], io_a[:, 0:128], 0,
                                          ALU.is_equal), "vslf")
        cm = t1f[:, 0:128]
        v.tensor_mul(t2f[:, 0:128], cs1[:], cm)
        p.ssync(v, v.tensor_mul(t2f[:, 128:256], cs2[:], cm), "vslf")
        v.tensor_sub(cs3[:], cs2[:], t2f[:, 128:256])
        p.cinc(v.tensor_sub(r2m[:], cs3[:], t2f[:, 0:128]), "ar2c")

    @p.block.tensor
    def _(t):
        p.wait(t, "awl")
        p.wait(t, "cst")
        p.cinc(t.matmul(p.aux1[:, 0:512], p.ones_f[:, 0:128], bv_row[:],
                        start=True, stop=True), "abm")

    @p.block.scalar
    def _(s):
        p.wait(s, "abm", 1)
        p.cinc(s.copy(bv_bc[:], p.aux1[:, 0:512]), "abe")

    @p.block.sync
    def _(s):
        for i in range(32):
            if i >= 2:
                p.wait(s, "aln", i - 1)
            p.dinc(s.dma_start(out=xstage[:, ts(i % 2, D)],
                               in_=xfull_ap[ts(i), :]), "axld")

    @p.block.gpsimd
    def _(g):
        for c in range(NCH):
            if c >= 2:
                p.wait(g, "arp", 8 * (c - 1))
            p.dinc(g.dma_start(out=sc_sb[:, (c % 2) * 1024:(c % 2) * 1024 + 512],
                               in_=sin_ap[:, ts(c, 512)]), "ascl")
            p.dinc(g.dma_start(
                out=sc_sb[:, (c % 2) * 1024 + 512:(c % 2) * 1024 + 1024],
                in_=cos_ap[:, ts(c, 512)]), "ascl")

    def pre(eng, i):
        eng.wait_ge(p.sem("axld")[0], 16 * (i + 1))
        if i == 0:
            p.wait(eng, "cst")
            p.wait(eng, "abe", 1)
            for nm, tgt in p.x_ready:
                p.wait(eng, nm, tgt)

    def cb_tensor(t, c):
        lncols = slice((c % 2) * 512, (c % 2) * 512 + 512)
        if c == 0:
            p.wait(t, "ar2c", 1)
        p.wait(t, "ale", 32 * (c + 1))
        for qk in range(2):
            wsb = wq_sb if qk == 0 else wk_sb
            for hp in range(4):
                idx = (c * 2 + qk) * 4 + hp
                if idx >= 1:
                    t.wait_ge(p.sem("aqe")[0], idx)
                for dc in range(DC):
                    ins = t.matmul(p.ps_a[:],
                                   wsb[:, dc * 512 + hp * 128:
                                       dc * 512 + hp * 128 + 128],
                                   lnTc[dc][:, lncols],
                                   start=(dc == 0), stop=(dc == DC - 1))
                p.cinc(ins, "aqm")
                t.wait_ge(p.sem("aqe")[0], idx + 1)
                if idx >= 1:
                    t.wait_ge(p.sem("arp")[0], idx)
                p.cinc(t.matmul(p.ps_b[:], r2m[:],
                                qrawb[:, ts(idx % 2, 512)],
                                start=True, stop=True), "afm")
        for sub in range(4):
            vidx = c * 4 + sub
            if vidx >= 1:
                t.wait_ge(p.sem("ave")[0], vidx)
            for dc in range(DC):
                ins = t.matmul(p.ps_c[1][:],
                               lnTc[dc][:, (c % 2) * 512 + sub * 128:
                                    (c % 2) * 512 + sub * 128 + 128],
                               wv_sb[:, ts(dc, 512)],
                               start=(dc == 0), stop=(dc == DC - 1))
            p.cinc(ins, "avm")

    def cb_vector(v, c):
        v.wait_ge(p.sem("ascl")[0], 32 * (c + 1))
        sins = sc_sb[:, (c % 2) * 1024:(c % 2) * 1024 + 512]
        coss = sc_sb[:, (c % 2) * 1024 + 512:(c % 2) * 1024 + 1024]
        for qk in range(2):
            dstT = QT if qk == 0 else KT
            bsb = bq_sb if qk == 0 else bk_sb
            for hp in range(4):
                idx = (c * 2 + qk) * 4 + hp
                v.wait_ge(p.sem("aqm")[0], idx + 1)
                if idx >= 2:
                    v.wait_ge(p.sem("afm")[0], idx - 1)
                p.cinc(v.tensor_scalar(qrawb[:, ts(idx % 2, 512)],
                                       p.ps_a[:], bsb[:, hp:hp + 1], None,
                                       op0=ALU.add), "aqe")
                v.wait_ge(p.sem("afm")[0], idx + 1)
                v.tensor_mul(t1f[:], qrawb[:, ts(idx % 2, 512)], coss)
                v.scalar_tensor_tensor(t2f[:], p.ps_b[:], 1.0, sins,
                                       op0=ALU.mult, op1=ALU.mult)
                p.cinc(v.tensor_add(dstT[hp][:, ts(c, 512)], t1f[:],
                                    t2f[:]), "arp")
        for sub in range(4):
            vidx = c * 4 + sub
            v.wait_ge(p.sem("avm")[0], vidx + 1)
            v.wait_ge(p.sem("avone")[0], 1)
            tile = c * 4 + sub
            dst = bass.AP(vsb[:].tensor, tile * 65,
                          [[8 * 65 * 32, 128], [65 * 32, 8], [1, 64]])
            p.cinc(v.scalar_tensor_tensor(
                dst, p.ps_c[1][:].rearrange("q (h e) -> q h e", e=64),
                1.0, bv_bc[:].rearrange("q (h e) -> q h e", e=64),
                op0=ALU.mult, op1=ALU.add), "ave")

    emit_ln_transpose(p, lnTc, 32, lambda i: xstage[:, ts(i % 2, D)],
                      "ln", "al", pre_wait=pre,
                      dst_col_fn=lambda i: ((i // 4) % 2) * 512 + (i % 4) * 128,
                      cb_every=4, cb_tensor=cb_tensor, cb_vector=cb_vector)

    p.pop_scope()
    p.push_scope()
    masks = p.sbuf("amask", [128, 4 * 512], BF16)
    io_b = p.sbuf("aiob", [128, 512], I32)
    wo_sb = p.sbuf("awo", [64, 8 * 1024], BF16)
    bo_bc = p.sbuf("abobc", [128, D], F32)
    PtA = [p.sbuf(f"aPtA{j}", [128, 512], BF16) for j in range(2)]
    PtB = p.sbuf("aPtB", [128, 512], BF16)
    reciprow = p.sbuf("arr", [1, 512], F32)
    recipb = p.sbuf("arrb", [1, 512], BF16)
    zraw = p.sbuf("azraw", [64, 512], BF16)
    zdiv = p.sbuf("azdiv", [64, 8 * 512], BF16)
    stage = p.sbuf("astg", [128, D], F32)
    rstage = p.sbuf("arstg", [128, 2 * D], F32)
    xrs = p.sbuf("axrs", [128, 2 * D], F32)
    ostage = p.sbuf("aost", [128, 2 * D], F32)

    QKV_DONE = [("arp", 64), ("ave", 32), ("aqe", 64)]

    @p.block.gpsimd
    def _(g):
        for nm, tgt in QKV_DONE:
            p.wait(g, nm, tgt)
        for hh in range(8):
            p.dinc(g.dma_start(out=wo_sb[:, ts(hh, 1024)],
                               in_=wo_ap[hh * 64:hh * 64 + 64, :]), "awo")
        for var in range(4):
            if var > 0:
                p.wait(g, "amk", var)
            p.cinc(g.iota(io_b[:], pattern=[[1, 512]], base=-128 * var,
                          channel_multiplier=-1,
                          allow_small_or_imprecise_dtypes=True), "amg")

    @p.block.vector
    def _(v):
        for var in range(4):
            p.wait(v, "amg", var + 1)
            p.cinc(v.tensor_single_scalar(masks[:, ts(var, 512)], io_b[:],
                                          0, ALU.is_ge), "amk")

    @p.block.tensor
    def _(t):
        for half in range(2):
            ins = t.matmul(p.aux1[:, ts(half, 512)], p.ones_f[:, 0:128],
                           bo_row[:, ts(half, 512)], start=True, stop=True)
        p.cinc(ins, "abm")

    @p.block.scalar
    def _(s):
        for nm, tgt in QKV_DONE:
            p.wait(s, nm, tgt)
        p.wait(s, "abm", 2)
        for half in range(2):
            p.cinc(s.copy(bo_bc[:, ts(half, 512)], p.aux1[:, ts(half, 512)]),
                   "abe")

    p._akidx_t = 0
    p._adiag_t = 0

    @p.block.tensor
    def _(t):
        t.wait_ge(p.sem("abe")[0], 3)
        for c in range(NCH):
            qc = slice(c * 512, c * 512 + 512)
            ntk = 4 * c + 4
            for hh in range(8):
                hp, sb = hh // 2, hh % 2
                rows = slice(64 * sb, 64 * sb + 64)
                t.wait_ge(p.sem("arp")[0], 8 * (c + 1))
                for tk in range(ntk):
                    kidx = p._akidx_t
                    p._akidx_t += 1
                    if tk >= 4 * c:
                        p._adiag_t += 1
                    if kidx >= 2:
                        t.wait_ge(p.sem("aex")[0], kidx - 1)
                    ins = t.matmul(p.ps_a[:] if kidx % 2 == 0 else p.ps_b[:],
                                   KT[hp][rows, tk * 128:tk * 128 + 128],
                                   QT[hp][rows, qc], start=True, stop=True)
                    p.cinc(ins, "asm")
                    if tk >= 4 * c:
                        t.wait_ge(p.sem("apm")[0], p._adiag_t)
                    else:
                        t.wait_ge(p.sem("aex")[0], kidx + 1)
                    if tk == 0:
                        t.wait_ge(p.sem("azd")[0], c * 8 + hh)
                    vtile = bass.AP(vsb[:].tensor, hh * 65 * 32 + tk * 65,
                                    [[8 * 65 * 32, 128], [1, 65]])
                    pt = PtB if tk >= 4 * c else PtA[kidx % 2]
                    p.cinc(t.matmul(p.aux1[0:65, 0:512], vtile, pt[:],
                                    start=(tk == 0), stop=(tk == ntk - 1)),
                           "apv")
                t.wait_ge(p.sem("arcb")[0], c * 8 + hh + 1)
                if c * 8 + hh >= 1:
                    t.wait_ge(p.sem("azd")[0], c * 8 + hh)
                p.cinc(t.matmul(p.ps_c[0][0:64, 0:512], p.ones_r[:, 0:64],
                                recipb[:], start=True, stop=True), "albc")
            t.wait_ge(p.sem("azd")[0], (c + 1) * 8)
            if c == 0:
                t.wait_ge(p.sem("awo")[0], 128)
            for sub in range(4):
                for half in range(2):
                    oidx = (c * 4 + sub) * 2 + half
                    if oidx >= 2:
                        t.wait_ge(p.sem("aoe")[0], oidx - 1)
                    for hh in range(8):
                        hp, sb = hh // 2, hh % 2
                        ins = t.matmul(
                            p.ps_c[half][:],
                            zdiv[0:64, hh * 512 + sub * 128:
                                 hh * 512 + sub * 128 + 128],
                            wo_sb[0:64, hh * 1024 + half * 512:
                                  hh * 1024 + half * 512 + 512],
                            start=(hh == 0), stop=(hh == 7))
                    p.cinc(ins, "aom")

    @p.block.scalar
    def _(s):
        for nm, tgt in QKV_DONE:
            p.wait(s, nm, tgt)
        kidx = 0
        for c in range(NCH):
            ntk = 4 * c + 4
            for hh in range(8):
                for tk in range(ntk):
                    s.wait_ge(p.sem("asm")[0], kidx + 1)
                    if kidx >= 2:
                        s.wait_ge(p.sem("apv")[0], kidx - 1)
                    p.cinc(s.activation(PtA[kidx % 2][:],
                                        p.ps_a[:] if kidx % 2 == 0
                                        else p.ps_b[:],
                                        AF.Exp, scale=0.125), "aex")
                    kidx += 1
                s.wait_ge(p.sem("apv")[0], kidx)
                s.wait_ge(p.sem("arc")[0], c * 8 + hh + 1)
                p.cinc(s.copy(recipb[:], reciprow[:]), "arcb")
                p.cinc(s.copy(zraw[:], p.aux1[0:64, 0:512]), "azr")
            for sub in range(4):
                for half in range(2):
                    oidx = (c * 4 + sub) * 2 + half
                    s.wait_ge(p.sem("aom")[0], oidx + 1)
                    if oidx >= 2:
                        s.wait_ge(p.sem("aps")[0], 16 * (oidx - 1))
                    p.cinc(s.copy(stage[:, ts(half, 512)],
                                  p.ps_c[half][:]), "aoe")

    @p.block.vector
    def _(v):
        for nm, tgt in QKV_DONE:
            p.wait(v, nm, tgt)
        kidx = 0
        for c in range(NCH):
            ntk = 4 * c + 4
            for hh in range(8):
                for tk in range(ntk):
                    if tk >= 4 * c:
                        v.wait_ge(p.sem("aex")[0], kidx + 1)
                        if kidx >= 1:
                            v.wait_ge(p.sem("apv")[0], kidx)
                        p.cinc(v.tensor_mul(PtB[:], PtA[kidx % 2][:],
                                            masks[:, ts(tk - 4 * c, 512)]),
                               "apm")
                    kidx += 1
                v.wait_ge(p.sem("apv")[0], kidx)
                p.ssync(v, v.reciprocal(reciprow[:], p.aux1[64:65, 0:512]),
                        "vslf")
                p.cinc(v.tensor_copy(recipb[:], recipb[:]) if False else
                       v.tensor_scalar_mul(p.rrcol[:], p.rrcol[:], 1.0)
                       if False else
                       v.tensor_copy(p.scol[0:1, 0:1], reciprow[0:1, 0:1]), "arc")
                v.wait_ge(p.sem("azr")[0], c * 8 + hh + 1)
                v.wait_ge(p.sem("albc")[0], c * 8 + hh + 1)
                p.cinc(v.tensor_mul(zdiv[0:64, ts(hh, 512)], zraw[:],
                                    p.ps_c[0][0:64, 0:512]), "azd")

    @p.block.sync
    def _(s):
        for c in range(NCH):
            for sub in range(4):
                for half in range(2):
                    oidx = (c * 4 + sub) * 2 + half
                    p.wait(s, "aoe", oidx + 1)
                    p.dinc(s.dma_start(
                        out=partial_dram[c * 512 + sub * 128:
                                         c * 512 + sub * 128 + 128,
                                         half * 512:half * 512 + 512],
                        in_=stage[:, ts(half, 512)]), "aps")
        if partial_only:
            p.wait(s, "aps")

    if partial_only:
        p.pop_scope()
        p.x_ready = []
        return

    bar_out = nc.dram_tensor("abarout", [2, D], F32).ap()

    @p.block.gpsimd
    def _(g):
        p.wait(g, "aps")
        p.cinc(g.collective_compute(
            "ReduceScatter", ALU.add,
            replica_groups=[[0, 1], [2, 3], [4, 5], [6, 7]],
            ins=[partial_dram.opt()], outs=[rs_out_dram.opt()]), "acc")
        p.cinc(g.collective_compute(
            "AllReduce", ALU.add,
            replica_groups=[[0, 1], [2, 3], [4, 5], [6, 7]],
            ins=[partial_dram[0:2, :].opt()], outs=[bar_out.opt()]), "acc")

    dly = nc.dram_tensor("adly", [T_OWN, D], F32).ap()

    @p.block.sync
    def _(s):
        for i in range(NT):
            p.dinc(s.dma_start(out=xrs[:, ts(i % 2, D)],
                               in_=xown_ap[ts(i), :])
                   if False else
                   s.dma_start(out=dly[ts(i), :], in_=xown_ap[ts(i), :]),
                   "axw")
        p.wait(s, "acc", 2)
        for r in range(256):
            src = dly[ts(r % 4, 32), 0:D]
            dst_d = dly[ts(12 + (r % 4), 32), 0:D]
            p.dinc(s.dma_start(out=dst_d, in_=src), "adw")
            p.wait(s, "adw", 16 * (r + 1))
        for i in range(NT):
            if i >= 2:
                p.wait(s, "aad", i - 1)
            p.dinc(s.dma_start(out=rstage[:, ts(i % 2, D)],
                               in_=rs_out_dram[ts(i), :]), "ars")
            p.dinc(s.dma_start(out=xrs[:, ts(i % 2, D)],
                               in_=xown_ap[ts(i), :]), "ars")

    @p.block.vector
    def _(v):
        p.wait(v, "abe", 3)
        for i in range(NT):
            v.wait_ge(p.sem("ars")[0], 32 * (i + 1))
            if i >= 2:
                v.wait_ge(p.sem("aout")[0], 16 * (i - 1))
            osl = ostage[:, ts(i % 2, D)]
            v.scalar_tensor_tensor(osl, rstage[:, ts(i % 2, D)], 1.0,
                                   xrs[:, ts(i % 2, D)], op0=ALU.mult,
                                   op1=ALU.add)
            p.cinc(v.tensor_add(osl, osl, bo_bc[:]) if False else
                   v.scalar_tensor_tensor(osl, bo_bc[:], 1.0, osl,
                                          op0=ALU.mult, op1=ALU.add), "aad")

    @p.block.gpsimd
    def _(g):
        for i in range(NT):
            p.wait(g, "aad", i + 1)
            p.dinc(g.dma_start(out=dst_ap[ts(i), :],
                               in_=ostage[:, ts(i % 2, D)]), "aout")
        p.wait(g, "aout")
    p.pop_scope()
    p.x_ready = [("aout", 16 * NT)]


def emit_l1_sum(p, pa_ap, pb_ap, x_ap, bo_ap, out_ap, to_xsb=False):
    nc = p.nc
    p.push_scope()
    pa = p.sbuf("spa", [128, 2 * D], F32)
    pb = p.sbuf("spb", [128, 2 * D], F32)
    xr = p.sbuf("spx", [128, 2 * D], F32)
    ost = None if to_xsb else p.sbuf("spo", [128, 2 * D], F32)
    bo_row = p.sbuf("spbo", [1, D], F32)
    bo_bc = p.sbuf("spbobc", [128, D], F32)

    @p.block.gpsimd
    def _(g):
        p.dinc(g.dma_start(out=bo_row[:], in_=bo_ap.unsqueeze(0)), "pbo")

    @p.block.tensor
    def _(t):
        p.wait(t, "cst")
        p.wait(t, "pbo", 16)
        for half in range(2):
            ins = t.matmul(p.aux1[:, ts(half, 512)], p.ones_f[:, 0:128],
                           bo_row[:, ts(half, 512)], start=True, stop=True)
        p.cinc(ins, "pbm")

    @p.block.scalar
    def _(s):
        p.wait(s, "pbm", 1)
        for half in range(2):
            p.cinc(s.copy(bo_bc[:, ts(half, 512)], p.aux1[:, ts(half, 512)]),
                   "pbe")

    @p.block.sync
    def _(s):
        for i in range(NT):
            if i >= 2:
                p.wait(s, "pad", i - 1)
            p.dinc(s.dma_start(out=pa[:, ts(i % 2, D)], in_=pa_ap[ts(i), :]),
                   "pld")
            p.dinc(s.dma_start(out=pb[:, ts(i % 2, D)], in_=pb_ap[ts(i), :]),
                   "pld")
            p.dinc(s.dma_start(out=xr[:, ts(i % 2, D)], in_=x_ap[ts(i), :]),
                   "pld")

    @p.block.vector
    def _(v):
        p.wait(v, "pbe", 2)
        for i in range(NT):
            v.wait_ge(p.sem("pld")[0], 48 * (i + 1))
            if not to_xsb and i >= 2:
                v.wait_ge(p.sem("pout")[0], 16 * (i - 1))
            o = p.xsb[:, ts(i, D)] if to_xsb else ost[:, ts(i % 2, D)]
            v.tensor_add(o, pa[:, ts(i % 2, D)], pb[:, ts(i % 2, D)])
            v.tensor_add(o, xr[:, ts(i % 2, D)], o)
            p.cinc(v.scalar_tensor_tensor(o, bo_bc[:], 1.0, o, op0=ALU.mult,
                                          op1=ALU.add), "pad")

    if to_xsb:
        p.pop_scope()
        p.x_ready = [("pad", NT)]
        return

    @p.block.gpsimd
    def _(g):
        for i in range(NT):
            p.wait(g, "pad", i + 1)
            p.dinc(g.dma_start(out=out_ap[ts(i), :], in_=ost[:, ts(i % 2, D)]),
                   "pout")
        p.wait(g, "pout")
    p.pop_scope()
    p.x_ready = []




def emit_l1_ffn(p, wg_ap, wu_ap, wd_ap):
    nc = p.nc
    HC = 24
    TCN = 4
    SC = 1.0 / 127.0
    xnT = [p.sbuf(f"f1xnT{dc}", [128, T_OWN], BF16) for dc in range(DC)]

    def pre(v, i):
        if i == 0:
            p.wait(v, "cst")
            for nm, tgt in p.x_ready:
                p.wait(v, nm, tgt)

    emit_ln_transpose(p, xnT, NT, lambda i: p.xsb[:, ts(i, D)], "ln", "f1",
                      pre_wait=pre)

    wd_sb = p.sbuf("f1wd", [128, HC * D], BF16)
    wgu = [p.sbuf(f"f1wgu{j}", [128, 2 * D], BF16) for j in range(2)]
    h_all = p.sbuf("f1h", [128, HC * 512], BF16)
    s_sb = p.sbuf("f1s", [128, 512], BF16)
    ps_s, ps_u, ps_d = p.ps_a, p.ps_b, p.ps_c
    p.dbg_hall = h_all[:]

    def mm_mark(it):
        return it + 1

    def dn_mark(tc, sub, half):
        return (tc * 4 + sub) * 2 + half + 1

    @p.block.gpsimd
    def _(g):
        for nm, tgt in p.x_ready:
            p.wait(g, nm, tgt)
        for hc in range(HC):
            p.dinc(g.dma_start(out=wd_sb[:, ts(hc, D)], in_=wd_ap[ts(hc), :]),
                   "f1wd")
        for tc in range(TCN):
            for hc in range(HC):
                it = tc * HC + hc
                buf = wgu[it % 2]
                if it >= 2:
                    g.wait_ge(p.sem("f1mm")[0], mm_mark(it - 2))
                src_g = bass.AP(wg_ap.tensor, hc * 128,
                                [[3072, 128], [128 * 3072, 8], [1, 128]])
                src_u = bass.AP(wu_ap.tensor, hc * 128,
                                [[3072, 128], [128 * 3072, 8], [1, 128]])
                p.dinc(g.dma_start(out=buf[:, 0:D].rearrange(
                    "q (c m) -> q c m", m=128), in_=src_g), "f1w")
                p.dinc(g.dma_start(out=buf[:, D:2 * D].rearrange(
                    "q (c m) -> q c m", m=128), in_=src_u), "f1w")

    @p.block.tensor
    def _(t):
        for tc in range(TCN):
            tcs = slice(tc * 512, tc * 512 + 512)
            for hc in range(HC):
                it = tc * HC + hc
                buf = wgu[it % 2]
                t.wait_ge(p.sem("f1w")[0], (it + 1) * 32)
                if it == 0:
                    p.wait(t, "f1e", NT * DC)
                if it >= 1:
                    t.wait_ge(p.sem("f1hh")[0], 2 * it)
                for dc in range(DC):
                    t.matmul(ps_s[:], buf[:, ts(dc)], xnT[dc][:, tcs],
                             start=(dc == 0), stop=(dc == DC - 1))
                for dc in range(DC):
                    ins = t.matmul(ps_u[:], buf[:, D + dc * 128:D + dc * 128 + 128],
                                   xnT[dc][:, tcs], start=(dc == 0),
                                   stop=(dc == DC - 1))
                p.cinc(ins, "f1mm")
            t.wait_ge(p.sem("f1hh")[0], 2 * (tc + 1) * HC)
            if tc == 0:
                t.wait_ge(p.sem("f1wd")[0], 16 * HC)
            for sub in range(4):
                for half in range(2):
                    if tc * 4 + sub >= 1:
                        t.wait_ge(p.sem("f1d")[0], dn_mark(tc, sub, half) - 2)
                    pd = ps_d[half]
                    for hc in range(HC):
                        ins = t.matmul(pd[:],
                                       h_all[:, hc * 512 + sub * 128:
                                             hc * 512 + sub * 128 + 128],
                                       wd_sb[:, hc * D + half * 512:
                                             hc * D + half * 512 + 512],
                                       start=(hc == 0), stop=(hc == HC - 1))
                    p.cinc(ins, "f1dm")

    @p.block.scalar
    def _(s):
        for tc in range(TCN):
            for hc in range(HC):
                it = tc * HC + hc
                s.wait_ge(p.sem("f1mm")[0], mm_mark(it))
                if it >= 1:
                    s.wait_ge(p.sem("f1hh")[0], 2 * it)
                p.cinc(s.activation(s_sb[:], ps_s[:], AF.Silu, scale=SC), "f1hh")

    @p.block.vector
    def _(v):
        for tc in range(TCN):
            for hc in range(HC):
                it = tc * HC + hc
                v.wait_ge(p.sem("f1hh")[0], 2 * it + 1)
                if tc >= 1 and hc == 0:
                    v.wait_ge(p.sem("f1dm")[0], dn_mark(tc - 1, 3, 1))
                p.cinc(v.scalar_tensor_tensor(h_all[:, ts(hc, 512)], ps_u[:],
                                              SC, s_sb[:], op0=ALU.mult,
                                              op1=ALU.mult), "f1hh")
            for sub in range(4):
                ti = (tc * 512 + sub * 128) // 128
                for half in range(2):
                    v.wait_ge(p.sem("f1dm")[0], dn_mark(tc, sub, half))
                    dst = p.xsb[:, ti * D + half * 512: ti * D + half * 512 + 512]
                    p.cinc(v.scalar_tensor_tensor(dst, ps_d[half][:], SC, dst,
                                                  op0=ALU.mult, op1=ALU.add),
                           "f1d")
    p.x_ready = [("f1d", 8 * TCN)]




def build_program(blocks, inputs_spec, with_halo=False, debug_dump=None):
    p = Prog()
    nc = p.nc
    dram = {}
    for name, shape, dt in inputs_spec:
        dram[name] = nc.dram_tensor(name, list(shape), dt,
                                    kind="ExternalInput").ap()
    out = nc.dram_tensor("out", [T_OWN, D], F32, kind="ExternalOutput").ap()

    use_xsb = bool(set(blocks) & {"l0a", "l0f", "l0fo", "l1f"})
    with nc.Block() as block:
        p.block = block
        emit_consts(p)
        if "l1sx" in blocks:
            p.xsb = p.sbuf("xsb", [128, NT * D], F32)
            p.x_ready = []
        elif use_xsb:
            emit_load_x(p, dram["x_own"], None)
        alloc_ln_scratch(p)
        p.ps_a = p.psum("ps_a", [128, 512], F32)
        p.ps_b = p.psum("ps_b", [128, 512], F32)
        p.ps_c = [p.psum(f"ps_c{j}", [128, 512], F32) for j in range(2)]
        p.aux1 = p.psum("aux1", [128, 1024], F32)
        if "l1sx" not in blocks:
            p.x_ready = [("xld", None)] if use_xsb else []
        if "l1sx" in blocks:
            emit_l1_sum(p, dram["pa"], dram["pb"], dram["x_own"], dram["bo"],
                        None, to_xsb=True)
        if "l0a" in blocks:
            dr = {
                "dband": nc.dram_tensor("sdband", [2 * 4 * 128 * 192], BF16).ap(),
                "wband": nc.dram_tensor("swband", [2 * 4 * 128 * 256],
                                        BF16).ap(),
                "rq": nc.dram_tensor("srqd", [2, QW], BF16).ap(),
                "hv": nc.dram_tensor("shvd", [64, D], BF16).ap(),
                "hid": nc.dram_tensor("shidd", [512], BF16).ap(),
                "mod": nc.dram_tensor("smodd", [1024], BF16).ap(),
            }
            emit_l0_attn(p, dram, dr)
        if "l0f" in blocks or "l0fo" in blocks:
            emit_l0_ffn(p, dram["nw"], dram["rw"], dram["w1"], dram["w3"],
                        dram["w2"],
                        x_ap=dram["x_own"] if "l0fo" in blocks else None)
        if "l1a" in blocks:
            partial = nc.dram_tensor("apartial", [4096, D], F32).ap()
            rs_out = nc.dram_tensor("arsout", [T_OWN, D], F32).ap()
            emit_l1_attn(p, dram["x_own"], out, dram["xfull"], dram["wq"],
                         dram["wk"], dram["wv"], dram["wo"], dram["bq"],
                         dram["bk"], dram["bv"], dram["bo"], dram["sin"],
                         dram["cos"], partial, rs_out)
        if "l1ap" in blocks:
            pout = nc.dram_tensor("pout", [4096, D], F32,
                                  kind="ExternalOutput").ap()
            emit_l1_attn(p, dram["x_own"], out, dram["xfull"], dram["wq"],
                         dram["wk"], dram["wv"], dram["wo"], dram["bq"],
                         dram["bk"], dram["bv"], dram["bo"], dram["sin"],
                         dram["cos"], pout, None, partial_only=True)
        if "l1s" in blocks:
            emit_l1_sum(p, dram["pa"], dram["pb"], dram["x_own"], dram["bo"],
                        out)
        if "l1f" in blocks:
            emit_l1_ffn(p, dram["wgT"], dram["wuT"], dram["wdT"])
        done = list(p.x_ready) if use_xsb else []
        if debug_dump:
            dbg_outs = {}
            for name in debug_dump:
                ap = getattr(p, "dbg_" + name)
                sh = [ap.shape[0], int(np.prod(ap.shape[1:]))]
                dbg_outs[name] = nc.dram_tensor(
                    "dbg_" + name, sh, ap.dtype, kind="ExternalOutput").ap()

            @p.block.sync
            def _(s):
                for nm, tg in done:
                    p.wait(s, nm, tg)
                for name in debug_dump:
                    p.dinc(s.dma_start(out=dbg_outs[name],
                                       in_=getattr(p, "dbg_" + name)), "dbg")
                p.wait(s, "dbg")
        if use_xsb:
            emit_store_x(p, out, done)
    p.es.close()
    return nc


def run_l1f(x, wg, wu, wd):
    spec = [("x_own", (T_OWN, D), F32),
            ("wgT", (D, 3072), I32),
            ("wuT", (D, 3072), I32),
            ("wdT", (3072, D), I32)]
    import os
    dbg = os.environ.get("DBG", "")
    nc = build_program(["l1f"], spec,
                       debug_dump=dbg.split(",") if dbg else None)
    wgT = np.ascontiguousarray(wg.T)
    wuT = np.ascontiguousarray(wu.T)
    wdT = np.ascontiguousarray(wd.T)
    in_maps = [{"x_own": np.ascontiguousarray(x[c * T_OWN:(c + 1) * T_OWN]),
                "wgT": wgT, "wuT": wuT, "wdT": wdT} for c in range(8)]
    res = run_bass_kernel_spmd(nc, in_maps, core_ids=list(range(8)))
    out = np.concatenate([res.results[c]["out"] for c in range(8)], 0)
    if dbg:
        return out, res.results[0]
    return out


def run_l0f(x, nw, rw, w1, w3, w2):
    spec = [("x_own", (T_OWN, D), F32),
            ("nw", (D,), F32),
            ("rw", (D, 8), F32),
            ("w1", (8, D, 2048), F32),
            ("w3", (8, D, 2048), F32),
            ("w2", (8, 2048, D), F32)]
    import os
    dbg = os.environ.get("DBG", "")
    nc = build_program(["l0f"], spec,
                       debug_dump=dbg.split(",") if dbg else None)
    in_maps = [{"x_own": np.ascontiguousarray(x[c * T_OWN:(c + 1) * T_OWN]),
                "nw": nw, "rw": rw, "w1": w1, "w3": w3, "w2": w2}
               for c in range(8)]
    res = run_bass_kernel_spmd(nc, in_maps, core_ids=list(range(8)))
    out = np.concatenate([res.results[c]["out"] for c in range(8)], 0)
    if dbg:
        return out, res.results[0]
    return out


def run_blocks(blocks, x, extra_spec, extra_maps):
    spec = [("x_own", (T_OWN, D), F32)] + extra_spec
    import os
    dbg = os.environ.get("DBG", "")
    nc = build_program(blocks, spec, debug_dump=dbg.split(",") if dbg else None)
    in_maps = [dict({"x_own": np.ascontiguousarray(x[c * T_OWN:(c + 1) * T_OWN])},
                    **extra_maps) for c in range(8)]
    res = run_bass_kernel_spmd(nc, in_maps, core_ids=list(range(8)))
    out = np.concatenate([res.results[c]["out"] for c in range(8)], 0)
    if dbg:
        return out, res.results[0]
    return out


L0A_SPEC = [("x_halo", (63, D), F32), ("x_pair", (T_OWN, D), F32),
            ("encq", (D, D), F32),
            ("encv", (D, D), F32), ("mnw1", (D, 512), F32),
            ("b1c", (128, 4), F32), ("mnw2", (512, D), F32),
            ("b2c", (128, 8), F32), ("basem", (128, 8), F32),
            ("dec", (D, D), F32), ("hebUT", (32, D), F32),
            ("hebV", (32, D), F32), ("clipc", (128, 64), F32),
            ("clipn", (128, 64), F32), ("clip2", (128, 64), F32),
            ("clip2n", (128, 64), F32)]


def l0a_core_maps(x, encq, encv, mnw1, mnb1, mnw2, mnb2, basem, dec, hebU,
                  hebV):
    encqT = np.ascontiguousarray(
        np.transpose(encq, (1, 0, 2)).reshape(D, D))
    encvT = np.ascontiguousarray(
        np.transpose(encv, (1, 0, 2)).reshape(D, D))
    hebUT = np.ascontiguousarray(hebU.transpose(2, 0, 1).reshape(32, D))
    hebVT = np.ascontiguousarray(hebV.transpose(1, 0, 2).reshape(32, D))
    pj = np.arange(128)[:, None] + np.arange(64)[None, :]
    maps = []
    for c in range(8):
        b, hf = c // 2, c % 2
        if hf == 0:
            halo = np.zeros((63, D), np.float32)
            cc = (pj < 63).astype(np.float32)
            c2 = (pj <= 63).astype(np.float32)
        else:
            halo = np.ascontiguousarray(
                x[b * 4096 + 2048 - 63:b * 4096 + 2048])
            cc = np.zeros((128, 64), np.float32)
            c2 = (pj == 63).astype(np.float32)
        cp = b * 2 + (1 - hf)
        maps.append({
            "x_own": np.ascontiguousarray(x[c * T_OWN:(c + 1) * T_OWN]),
            "x_pair": np.ascontiguousarray(x[cp * T_OWN:(cp + 1) * T_OWN]),
            "x_halo": halo, "encq": encqT, "encv": encvT,
            "mnw1": mnw1, "b1c": np.ascontiguousarray(
                mnb1.reshape(4, 128).T),
            "mnw2": mnw2, "b2c": np.ascontiguousarray(
                mnb2.reshape(8, 128).T),
            "basem": np.ascontiguousarray(basem.T), "dec": dec,
            "hebUT": hebUT, "hebV": hebVT,
            "clipc": cc, "clipn": 1.0 - cc,
            "clip2": c2, "clip2n": 1.0 - c2,
        })
    return maps


def run_l0a(x, encq, encv, mnw1, mnb1, mnw2, mnb2, basem, dec, hebU, hebV):
    spec = [("x_own", (T_OWN, D), F32)] + L0A_SPEC
    import os
    dbg = os.environ.get("DBG", "")
    nc = build_program(["l0a"], spec, debug_dump=dbg.split(",") if dbg else None)
    in_maps = l0a_core_maps(x, encq, encv, mnw1, mnb1, mnw2, mnb2, basem,
                            dec, hebU, hebV)
    res = run_bass_kernel_spmd(nc, in_maps, core_ids=list(range(8)))
    out = np.concatenate([res.results[c]["out"] for c in range(8)], 0)
    if dbg:
        return out, res.results
    return out


def rope_tables():
    pos = np.arange(4096, dtype=np.float64)
    dim = np.arange(32, dtype=np.float64)
    freq = 10000.0 ** (dim / 32.0)
    freq = np.concatenate([freq, freq])
    ang = pos[:, None] / freq[None, :]
    sin = np.sin(ang).T.astype(np.float32)
    cos = np.cos(ang).T.astype(np.float32)
    return (np.concatenate([sin, sin], 0).copy(),
            np.concatenate([cos, cos], 0).copy())


def l1a_core_maps(x, WQ, bQ, WK, bK, WV, bV, WO, bO):
    sin, cos = rope_tables()
    maps = []
    for c in range(8):
        b, hp = c // 2, c % 2
        hsl = slice(8 * hp, 8 * hp + 8)
        xb = np.ascontiguousarray(x[b * 4096:(b + 1) * 4096])
        maps.append({
            "x_own": np.ascontiguousarray(x[c * T_OWN:(c + 1) * T_OWN]),
            "xfull": xb,
            "wq": np.ascontiguousarray(
                np.transpose(WQ[hsl], (1, 0, 2)).reshape(D, 512)),
            "wk": np.ascontiguousarray(
                np.transpose(WK[hsl], (1, 0, 2)).reshape(D, 512)),
            "wv": np.ascontiguousarray(
                np.transpose(WV[hsl], (1, 0, 2)).reshape(D, 512)),
            "wo": np.ascontiguousarray(WO[hsl].reshape(512, D)),
            "bq": np.ascontiguousarray(bQ[hsl].reshape(4, 128).T),
            "bk": np.ascontiguousarray(bK[hsl].reshape(4, 128).T),
            "bv": np.ascontiguousarray(bV[hsl].reshape(512)),
            "bo": bO if hp == 99 else bO,
            "sin": sin, "cos": cos,
        })
    return maps


L1A_SPEC = [("xfull", (4096, D), F32), ("wq", (D, 512), F32),
            ("wk", (D, 512), F32), ("wv", (D, 512), F32),
            ("wo", (512, D), F32), ("bq", (128, 4), F32),
            ("bk", (128, 4), F32), ("bv", (512,), F32), ("bo", (D,), F32),
            ("sin", (128, 4096), F32), ("cos", (128, 4096), F32)]


def run_l1a(x, WQ, bQ, WK, bK, WV, bV, WO, bO):
    spec = [("x_own", (T_OWN, D), F32)] + L1A_SPEC
    nc = build_program(["l1ap"], spec)
    in_maps = l1a_core_maps(x, WQ, bQ, WK, bK, WV, bV, WO, bO)
    res = run_bass_kernel_spmd(nc, in_maps, core_ids=list(range(8)))
    pouts = [np.asarray(res.results[c]["pout"]) for c in range(8)]
    specB = [("x_own", (T_OWN, D), F32), ("pa", (T_OWN, D), F32),
             ("pb", (T_OWN, D), F32), ("bo", (D,), F32)]
    ncB = build_program(["l1s"], specB)
    mapsB = []
    for c in range(8):
        hf = c % 2
        rows = slice(hf * T_OWN, hf * T_OWN + T_OWN)
        mapsB.append({
            "x_own": np.ascontiguousarray(x[c * T_OWN:(c + 1) * T_OWN]),
            "pa": np.ascontiguousarray(pouts[c][rows]),
            "pb": np.ascontiguousarray(pouts[c ^ 1][rows]),
            "bo": bO,
        })
    resB = run_bass_kernel_spmd(ncB, mapsB, core_ids=list(range(8)))
    return np.concatenate([resB.results[c]["out"] for c in range(8)], 0)


def run_l0f_outer(x, nw, rw, w1, w3, w2):
    spec = [("x_own", (T_OWN, D), F32),
            ("nw", (D,), F32),
            ("rw", (D, 8), F32),
            ("w1", (8, D, 2048), F32),
            ("w3", (8, D, 2048), F32),
            ("w2", (8, 2048, D), F32)]
    nc = build_program(["l0fo"], spec)
    in_maps = [{"x_own": np.ascontiguousarray(x[c * T_OWN:(c + 1) * T_OWN]),
                "nw": nw, "rw": rw, "w1": w1, "w3": w3, "w2": w2}
               for c in range(8)]
    res = run_bass_kernel_spmd(nc, in_maps, core_ids=list(range(8)))
    return np.concatenate([res.results[c]["out"] for c in range(8)], 0)


def run_l1a_partial(x, WQ, bQ, WK, bK, WV, bV, WO, bO):
    spec = [("x_own", (T_OWN, D), F32)] + L1A_SPEC
    nc = build_program(["l1ap"], spec)
    in_maps = l1a_core_maps(x, WQ, bQ, WK, bK, WV, bV, WO, bO)
    res = run_bass_kernel_spmd(nc, in_maps, core_ids=list(range(8)))
    return [np.asarray(res.results[c]["pout"]) for c in range(8)]


def run_l1sf(x, pouts, bO, wg, wu, wd):
    spec = [("x_own", (T_OWN, D), F32), ("pa", (T_OWN, D), F32),
            ("pb", (T_OWN, D), F32), ("bo", (D,), F32),
            ("wgT", (D, 3072), I32), ("wuT", (D, 3072), I32),
            ("wdT", (3072, D), I32)]
    nc = build_program(["l1sx", "l1f"], spec)
    wgT = np.ascontiguousarray(wg.T)
    wuT = np.ascontiguousarray(wu.T)
    wdT = np.ascontiguousarray(wd.T)
    maps = []
    for c in range(8):
        hf = c % 2
        rows = slice(hf * T_OWN, hf * T_OWN + T_OWN)
        maps.append({
            "x_own": np.ascontiguousarray(x[c * T_OWN:(c + 1) * T_OWN]),
            "pa": np.ascontiguousarray(pouts[c][rows]),
            "pb": np.ascontiguousarray(pouts[c ^ 1][rows]),
            "bo": bO, "wgT": wgT, "wuT": wuT, "wdT": wdT,
        })
    res = run_bass_kernel_spmd(nc, maps, core_ids=list(range(8)))
    return np.concatenate([res.results[c]["out"] for c in range(8)], 0)


def kernel(**inputs):
    g = {k: np.asarray(v) for k, v in inputs.items()}
    x0 = np.ascontiguousarray(g["L0_attn_x"].reshape(16384, D))
    x1 = run_l0a(x0, g["L0_attn_encoder_query"], g["L0_attn_encoder_value"],
                 g["L0_attn_mn_w1"], g["L0_attn_mn_b1"], g["L0_attn_mn_w2"],
                 g["L0_attn_mn_b2"], g["L0_attn_base_metric"],
                 g["L0_attn_decoder"], g["L0_attn_hebbian_U"],
                 g["L0_attn_hebbian_V"])
    x2 = run_l0f_outer(x1, g["L0_ffn_norm_w"], g["L0_ffn_router_w"],
                       g["L0_ffn_w1"], g["L0_ffn_w3"], g["L0_ffn_w2"])
    pouts = run_l1a_partial(x2, g["L1_attn_W_Q"], g["L1_attn_b_Q"],
                            g["L1_attn_W_K"], g["L1_attn_b_K"],
                            g["L1_attn_W_V"], g["L1_attn_b_V"],
                            g["L1_attn_W_O"], g["L1_attn_b_O"])
    x4 = run_l1sf(x2, pouts, g["L1_attn_b_O"], g["L1_ffn_w_gate"],
                  g["L1_ffn_w_up"], g["L1_ffn_w_down"])
    return np.ascontiguousarray(x4.reshape(4, 4096, D).astype(np.float32))
